# version 02
# baseline (speedup 1.0000x reference)
import numpy as np
import ml_dtypes
import concourse.bass as bass
import concourse.mybir as mybir
from concourse.bass_utils import run_bass_kernel_spmd
from contextlib import ExitStack

F32 = mybir.dt.float32
BF16 = mybir.dt.bfloat16
AF = mybir.ActivationFunctionType
OP = mybir.AluOpType
AX = mybir.AxisListType
BF = ml_dtypes.bfloat16
f32 = np.float32

NCORE = 8
D = 1024
DI = 2048
DS = 16
NT = 16384
TCORE = 2048
CH = 512
NCH = TCORE // CH


class Prog:

    def __init__(self, nc):
        self.nc = nc
        self.ops = []

    def op(self, eng, fn, inc=1, wait=True):
        self.ops.append((eng, fn, inc, wait))

    def dma(self, out, in_, **kw):
        self.op('sync', lambda e, o=out, i=in_, kw=kw: e.dma_start(out=o, in_=i, **kw), inc=16)

    def mm(self, out, lhsT, rhs, start=True, stop=True, wait=True):
        self.op('tensor',
                lambda e, o=out, l=lhsT, r=rhs, s1=start, s2=stop: e.matmul(o, l, r, start=s1, stop=s2),
                inc=1, wait=wait)

    def tr(self, out, in_, ident):
        self.op('tensor', lambda e, o=out, i=in_, d=ident: e.transpose(o, i, d))

    def v(self, method, *a, **kw):
        self.op('vector', lambda e, m=method, a=a, kw=kw: getattr(e, m)(*a, **kw))

    def s(self, method, *a, **kw):
        self.op('scalar', lambda e, m=method, a=a, kw=kw: getattr(e, m)(*a, **kw))

    def finish(self):
        nc = self.nc
        ops = self.ops
        with nc.semaphore("tok") as tok, nc.Block() as block:
            def mk(name):
                def body(eng):
                    cnt = 0
                    for (en, fn, inc, wait) in ops:
                        if en == name:
                            if wait and cnt > 0:
                                eng.wait_ge(tok, cnt)
                            r = fn(eng)
                            if inc:
                                r.then_inc(tok, inc)
                        cnt += inc
                    if name == 'sync':
                        eng.wait_ge(tok, cnt)
                return body
            block.sync(mk('sync'))
            block.tensor(mk('tensor'))
            block.vector(mk('vector'))
            block.scalar(mk('scalar'))


def _run(nc, in_maps):
    res = run_bass_kernel_spmd(nc, in_maps, core_ids=list(range(NCORE)))
    return res.results


def build_KA():
    nc = bass.Bass("TRN2", target_bir_lowering=False)
    xin_h = nc.dram_tensor("xin", [D, TCORE + 3], F32, kind="ExternalInput").ap()
    WinT = nc.dram_tensor("WinT", [D, 2 * DI], F32, kind="ExternalInput").ap()
    WxT = nc.dram_tensor("WxT", [DI, 33], F32, kind="ExternalInput").ap()
    WdtT = nc.dram_tensor("WdtT", [1, DI], F32, kind="ExternalInput").ap()
    convw = nc.dram_tensor("convw", [DI, 4], F32, kind="ExternalInput").ap()
    convb = nc.dram_tensor("convb", [DI, 1], F32, kind="ExternalInput").ap()
    bdt = nc.dram_tensor("bdt", [DI, 1], F32, kind="ExternalInput").ap()
    ones128 = nc.dram_tensor("ones128", [128, 1], F32, kind="ExternalInput").ap()
    onesrow = nc.dram_tensor("onesrow", [1, 128], F32, kind="ExternalInput").ap()
    xnT_o = nc.dram_tensor("xnT", [D, TCORE], F32, kind="ExternalOutput").ap()
    xcT_o = nc.dram_tensor("xcT", [DI, TCORE], F32, kind="ExternalOutput").ap()
    dtT_o = nc.dram_tensor("dtT", [DI, TCORE], F32, kind="ExternalOutput").ap()
    resT_o = nc.dram_tensor("resT", [DI, TCORE], F32, kind="ExternalOutput").ap()
    BT_o = nc.dram_tensor("BT", [16, TCORE], F32, kind="ExternalOutput").ap()
    CT_o = nc.dram_tensor("CT", [16, TCORE], F32, kind="ExternalOutput").ap()

    st = ExitStack()
    with st:
        sb = lambda n, s, d: st.enter_context(nc.sbuf_tensor(n, s, d)).ap()
        ps = lambda n, s: st.enter_context(nc.psum_tensor(n, s, F32)).ap()
        xin = sb("xinb", [128, 8 * 512], F32)
        sq = sb("sq", [128, 512], F32)
        xnf = sb("xnf", [128, 512], F32)
        xn_bf = sb("xn_bf", [128, 8 * 512], BF16)
        Win_bf = sb("Win_bf", [128, 8 * 4096], BF16)
        Wx_bf = sb("Wx_bf", [128, 16 * 33], BF16)
        Wdt_bf = sb("Wdt_bf", [1, DI], BF16)
        cwc = sb("cwc", [128, 16 * 4], F32)
        cbc = sb("cbc", [128, 16], F32)
        bdtc = sb("bdtc", [128, 16], F32)
        onec = sb("onec", [128, 1], F32)
        oner = sb("oner", [1, 128], F32)
        xpb = sb("xpb", [128, 16 * 515], F32)
        xcf = sb("xcf", [128, 512], F32)
        accv = sb("accv", [128, 512], F32)
        xc_bf = sb("xc_bf", [128, 16 * 512], BF16)
        resb = sb("resb", [128, 512], F32)
        ssmf = sb("ssmf", [33, 512], F32)
        dtrf = sb("dtrf", [1, 512], F32)
        dtr_bf = sb("dtr_bf", [1, 512], BF16)
        zf = sb("zf", [128, 512], F32)
        azf = sb("azf", [128, 512], F32)
        ef = sb("ef", [128, 512], F32)
        lf = sb("lf", [128, 512], F32)
        rf = sb("rf", [128, 512], F32)
        dtf = sb("dtf", [128, 512], F32)
        meanr = sb("meanr", [1, 512], F32)
        ex2r = sb("ex2r", [1, 512], F32)
        msqr = sb("msqr", [1, 512], F32)
        varr = sb("varr", [1, 512], F32)
        lnvr = sb("lnvr", [1, 512], F32)
        rstdr = sb("rstdr", [1, 512], F32)
        cst = sb("cst", [128, 2], F32)

        psS = ps("psS", [1, 512])
        psQ = ps("psQ", [1, 512])
        psM = ps("psM", [128, 512])
        psR = ps("psR", [128, 512])
        psX = ps("psX", [128, 512])
        psW = ps("psW", [33, 512])
        psD = ps("psD", [128, 512])

        p = Prog(nc)
        p.v('memset', cst[:, 0:1], 1e-5)
        p.v('memset', cst[:, 1:2], 1.0)
        p.dma(onec, ones128)
        p.dma(oner, onesrow)
        for k in range(8):
            p.dma(xin[:, 0:4096], WinT[k * 128:(k + 1) * 128, :])
            p.s('copy', Win_bf[:, k * 4096:(k + 1) * 4096], xin[:, 0:4096])
        for k in range(16):
            p.dma(xin[:, 0:33], WxT[k * 128:(k + 1) * 128, :])
            p.s('copy', Wx_bf[:, k * 33:(k + 1) * 33], xin[:, 0:33])
            p.dma(cwc[:, k * 4:(k + 1) * 4], convw[k * 128:(k + 1) * 128, :])
            p.dma(cbc[:, k:k + 1], convb[k * 128:(k + 1) * 128, :])
            p.dma(bdtc[:, k:k + 1], bdt[k * 128:(k + 1) * 128, :])
        for q in range(4):
            p.dma(xin[0:1, 0:512], WdtT[0:1, q * 512:(q + 1) * 512])
            p.s('copy', Wdt_bf[0:1, q * 512:(q + 1) * 512], xin[0:1, 0:512])

        def chunk(co, W, tail_off, main_ci):
            for k in range(8):
                p.dma(xin[:, k * 512:k * 512 + W], xin_h[k * 128:(k + 1) * 128, co:co + W])
            for k in range(8):
                p.mm(psS[0:1, 0:W], onec, xin[:, k * 512:k * 512 + W],
                     start=(k == 0), stop=(k == 7), wait=(k == 0))
            for k in range(8):
                p.s('activation', sq[:, 0:W], xin[:, k * 512:k * 512 + W], AF.Square)
                p.mm(psQ[0:1, 0:W], onec, sq[:, 0:W], start=(k == 0), stop=(k == 7))
            p.s('activation', meanr[0:1, 0:W], psS[0:1, 0:W], AF.Copy, scale=1.0 / 1024.0)
            p.s('activation', ex2r[0:1, 0:W], psQ[0:1, 0:W], AF.Copy, scale=1.0 / 1024.0)
            p.v('tensor_mul', msqr[0:1, 0:W], meanr[0:1, 0:W], meanr[0:1, 0:W])
            p.v('tensor_sub', varr[0:1, 0:W], ex2r[0:1, 0:W], msqr[0:1, 0:W])
            p.s('activation', lnvr[0:1, 0:W], varr[0:1, 0:W], AF.Ln, bias=cst[0:1, 0:1])
            p.s('activation', rstdr[0:1, 0:W], lnvr[0:1, 0:W], AF.Exp, scale=-0.5)
            p.mm(psM[:, 0:W], oner, meanr[0:1, 0:W])
            p.mm(psR[:, 0:W], oner, rstdr[0:1, 0:W])
            for k in range(8):
                p.v('tensor_sub', xnf[:, 0:W], xin[:, k * 512:k * 512 + W], psM[:, 0:W])
                p.v('tensor_mul', xnf[:, 0:W], xnf[:, 0:W], psR[:, 0:W])
                p.s('copy', xn_bf[:, k * 512:k * 512 + W], xnf[:, 0:W])
                if main_ci is not None:
                    p.dma(xnT_o[k * 128:(k + 1) * 128, main_ci * 512:main_ci * 512 + W], xnf[:, 0:W])
            n_m = 32 if main_ci is not None else 16
            for m in range(n_m):
                for k in range(8):
                    p.mm(psX[:, 0:W], Win_bf[:, k * 4096 + m * 128:k * 4096 + (m + 1) * 128],
                         xn_bf[:, k * 512:k * 512 + W], start=(k == 0), stop=(k == 7), wait=(k == 0))
                if m < 16:
                    p.s('copy', xpb[:, m * 515 + tail_off:m * 515 + tail_off + W], psX[:, 0:W])
                else:
                    p.s('copy', resb[:, 0:W], psX[:, 0:W])
                    p.dma(resT_o[(m - 16) * 128:(m - 15) * 128, main_ci * 512:main_ci * 512 + W],
                          resb[:, 0:W])

        chunk(0, 3, 0, None)
        for ci in range(NCH):
            chunk(3 + ci * 512, 512, 3, ci)
            for j in range(16):
                src = xpb[:, j * 515:(j + 1) * 515]
                p.v('tensor_scalar', accv, src[:, 0:512], cwc[:, j * 4:j * 4 + 1], None, OP.mult)
                for jj in range(1, 4):
                    p.v('scalar_tensor_tensor', accv, src[:, jj:jj + 512],
                        cwc[:, j * 4 + jj:j * 4 + jj + 1], accv, OP.mult, OP.add)
                p.s('activation', xcf, accv, AF.Silu, bias=cbc[:, j:j + 1])
                p.dma(xcT_o[j * 128:(j + 1) * 128, ci * 512:(ci + 1) * 512], xcf)
                p.s('copy', xc_bf[:, j * 512:(j + 1) * 512], xcf)
                p.v('tensor_copy', src[:, 0:3], src[:, 512:515])
            for k in range(16):
                p.mm(psW[0:33, :], Wx_bf[:, k * 33:(k + 1) * 33], xc_bf[:, k * 512:(k + 1) * 512],
                     start=(k == 0), stop=(k == 15), wait=(k == 0))
            p.s('copy', ssmf, psW[0:33, :])
            p.dma(BT_o[0:16, ci * 512:(ci + 1) * 512], ssmf[0:16, :])
            p.dma(CT_o[0:16, ci * 512:(ci + 1) * 512], ssmf[16:32, :])
            p.dma(dtrf[0:1, :], ssmf[32:33, :])
            p.s('copy', dtr_bf, dtrf)
            for m in range(16):
                p.mm(psD, Wdt_bf[0:1, m * 128:(m + 1) * 128], dtr_bf)
                p.v('tensor_scalar', zf, psD, bdtc[:, m:m + 1], None, OP.add)
                p.s('activation', azf, zf, AF.Abs)
                p.s('activation', ef, azf, AF.Exp, scale=-1.0)
                p.s('activation', lf, ef, AF.Ln, bias=cst[:, 1:2])
                p.v('tensor_scalar', rf, zf, 0.0, None, OP.max)
                p.v('tensor_add', dtf, rf, lf)
                p.dma(dtT_o[m * 128:(m + 1) * 128, ci * 512:(ci + 1) * 512], dtf)
        p.finish()
    return nc


def build_KB():
    TC = 1024
    nc = bass.Bass("TRN2", target_bir_lowering=False)
    dtT = nc.dram_tensor("dtT", [256, NT], F32, kind="ExternalInput").ap()
    xcT = nc.dram_tensor("xcT", [256, NT], F32, kind="ExternalInput").ap()
    resT = nc.dram_tensor("resT", [256, NT], F32, kind="ExternalInput").ap()
    BT = nc.dram_tensor("BT", [16, NT], F32, kind="ExternalInput").ap()
    CT = nc.dram_tensor("CT", [16, NT], F32, kind="ExternalInput").ap()
    At = nc.dram_tensor("At", [128, 16], F32, kind="ExternalInput").ap()
    Dv = nc.dram_tensor("Dv", [256, 1], F32, kind="ExternalInput").ap()
    onesrow = nc.dram_tensor("onesrow", [1, 128], F32, kind="ExternalInput").ap()
    y2T = nc.dram_tensor("y2T", [256, NT], BF16, kind="ExternalOutput").ap()

    st = ExitStack()
    with st:
        sb = lambda n, s, d: st.enter_context(nc.sbuf_tensor(n, s, d)).ap()
        ps = lambda n, s: st.enter_context(nc.psum_tensor(n, s, F32)).ap()
        dt_t = [sb(f"dt{e}", [128, TC], F32) for e in range(2)]
        xc_t = [sb(f"xc{e}", [128, TC], F32) for e in range(2)]
        u_t = [sb(f"u{e}", [128, TC], F32) for e in range(2)]
        ya = [sb(f"ya{e}", [128, TC], F32) for e in range(2)]
        a1 = sb("a1", [128, TC], F32)
        a2 = sb("a2", [128, TC], F32)
        bv = sb("bv", [128, TC], F32)
        hh = sb("hh", [128, TC], F32)
        tmp = sb("tmp", [128, TC], F32)
        res_t = sb("res_t", [128, TC], F32)
        y2b = sb("y2b", [128, TC], BF16)
        brow = sb("brow", [1, TC], F32)
        crow = sb("crow", [1, TC], F32)
        Acol = sb("Acol", [128, 16], F32)
        Dcol = sb("Dcol", [128, 2], F32)
        carry = sb("carry", [128, 32], F32)
        oner = sb("oner", [1, 128], F32)
        psB = ps("psB", [128, TC])
        psC = ps("psC", [128, TC])

        p = Prog(nc)
        p.dma(oner, onesrow)
        p.dma(Acol, At)
        p.s('activation', Acol, Acol, AF.Exp)
        p.v('tensor_scalar', Acol, Acol, -1.0, None, OP.mult)
        for e in range(2):
            p.dma(Dcol[:, e:e + 1], Dv[e * 128:(e + 1) * 128, :])
        for b in range(2):
            p.v('memset', carry, 0.0)
            for ch in range(8192 // TC):
                c0 = b * 8192 + ch * TC
                for e in range(2):
                    p.dma(dt_t[e], dtT[e * 128:(e + 1) * 128, c0:c0 + TC])
                    p.dma(xc_t[e], xcT[e * 128:(e + 1) * 128, c0:c0 + TC])
                    p.v('tensor_mul', u_t[e], dt_t[e], xc_t[e])
                for n in range(16):
                    p.dma(brow, BT[n:n + 1, c0:c0 + TC])
                    p.dma(crow, CT[n:n + 1, c0:c0 + TC])
                    for hf in range(2):
                        sl = slice(hf * 512, (hf + 1) * 512)
                        p.mm(psB[:, sl], oner, brow[0:1, sl])
                        p.mm(psC[:, sl], oner, crow[0:1, sl])
                    for e in range(2):
                        p.v('tensor_scalar', a1, dt_t[e], Acol[:, n:n + 1], None, OP.mult)
                        p.s('activation', a2, a1, AF.Exp)
                        for hf in range(2):
                            sl = slice(hf * 512, (hf + 1) * 512)
                            p.v('tensor_tensor', bv[:, sl], u_t[e][:, sl], psB[:, sl], OP.mult)
                        slot = e * 16 + n
                        p.v('tensor_tensor_scan', hh, a2, bv, carry[:, slot:slot + 1],
                            OP.mult, OP.add)
                        p.v('tensor_copy', carry[:, slot:slot + 1], hh[:, TC - 1:TC])
                        for hf in range(2):
                            sl = slice(hf * 512, (hf + 1) * 512)
                            if n == 0:
                                p.v('tensor_tensor', ya[e][:, sl], hh[:, sl], psC[:, sl], OP.mult)
                            else:
                                p.v('tensor_tensor', tmp[:, sl], hh[:, sl], psC[:, sl], OP.mult)
                        if n > 0:
                            p.v('tensor_add', ya[e], ya[e], tmp)
                for e in range(2):
                    p.v('scalar_tensor_tensor', ya[e], xc_t[e], Dcol[:, e:e + 1], ya[e],
                        OP.mult, OP.add)
                    p.dma(res_t, resT[e * 128:(e + 1) * 128, c0:c0 + TC])
                    p.v('tensor_add', ya[e], ya[e], res_t)
                    p.s('activation', y2b, ya[e], AF.Silu)
                    p.dma(y2T[e * 128:(e + 1) * 128, c0:c0 + TC], y2b)
        p.finish()
    return nc


def build_KC():
    nc = bass.Bass("TRN2", target_bir_lowering=False)
    y2T = nc.dram_tensor("y2T", [DI, TCORE], BF16, kind="ExternalInput").ap()
    WoutT = nc.dram_tensor("WoutT", [DI, D], F32, kind="ExternalInput").ap()
    xnT = nc.dram_tensor("xnT", [D, TCORE], F32, kind="ExternalInput").ap()
    gam = nc.dram_tensor("gam", [D, 1], F32, kind="ExternalInput").ap()
    routT = nc.dram_tensor("routT", [D, 8], F32, kind="ExternalInput").ap()
    I128 = nc.dram_tensor("I128", [128, 128], F32, kind="ExternalInput").ap()
    ones128 = nc.dram_tensor("ones128", [128, 1], F32, kind="ExternalInput").ap()
    onesrow = nc.dram_tensor("onesrow", [1, 128], F32, kind="ExternalInput").ap()
    sT_o = nc.dram_tensor("sT", [D, TCORE], F32, kind="ExternalOutput").ap()
    normT_o = nc.dram_tensor("normT", [D, TCORE], BF16, kind="ExternalOutput").ap()
    combT_o = nc.dram_tensor("combT", [8, TCORE], F32, kind="ExternalOutput").ap()

    st = ExitStack()
    with st:
        sb = lambda n, s, d: st.enter_context(nc.sbuf_tensor(n, s, d)).ap()
        ps = lambda n, s: st.enter_context(nc.psum_tensor(n, s, F32)).ap()
        Wo_bf = sb("Wo_bf", [128, 16 * 1024], BF16)
        ro_f = sb("ro_f", [128, 8 * 8], F32)
        n32 = sb("n32", [128, 8 * 512], F32)
        wtmp = sb("wtmp", [128, 1024], F32)
        y2_bf = sb("y2_bf", [128, 16 * 512], BF16)
        xn = sb("xn", [128, 8 * 512], F32)
        sq = sb("sq", [128, 512], F32)
        sf = sb("sf", [128, 512], F32)
        nf = sb("nf", [128, 512], F32)
        n_bf = sb("n_bf", [128, 8 * 512], BF16)
        gcol = sb("gcol", [128, 8], F32)
        onec = sb("onec", [128, 1], F32)
        oner = sb("oner", [1, 128], F32)
        I_f = sb("I_f", [128, 128], F32)
        ex2r = sb("ex2r", [1, 512], F32)
        lnvr = sb("lnvr", [1, 512], F32)
        rstdr = sb("rstdr", [1, 512], F32)
        probs = sb("probs", [8, 512], F32)
        tpro = sb("tpro", [128, 8], F32)
        m1 = sb("m1", [128, 1], F32)
        eqm = sb("eqm", [128, 8], F32)
        cnt = sb("cnt", [128, 1], F32)
        big = sb("big", [128, 8], F32)
        wo_ = sb("wo_", [128, 8], F32)
        m2 = sb("m2", [128, 1], F32)
        c2 = sb("c2", [128, 1], F32)
        min2 = sb("min2", [128, 1], F32)
        keep = sb("keep", [128, 8], F32)
        den = sb("den", [128, 1], F32)
        rden = sb("rden", [128, 1], F32)
        combt = sb("combt", [128, 8], F32)
        combf = sb("combf", [8, 512], F32)
        cst = sb("cst", [128, 1], F32)
        psS = ps("psS", [128, 512])
        psQ = ps("psQ", [1, 512])
        psR = ps("psR", [128, 512])
        psP = ps("psP", [8, 512])
        psT = ps("psT", [128, 8])
        psCT = ps("psCT", [8, 128])

        p = Prog(nc)
        p.v('memset', cst, 1e-8)
        p.dma(onec, ones128)
        p.dma(oner, onesrow)
        p.dma(I_f, I128)
        for k in range(16):
            p.dma(wtmp, WoutT[k * 128:(k + 1) * 128, :])
            p.s('copy', Wo_bf[:, k * 1024:(k + 1) * 1024], wtmp)
        for k in range(8):
            p.dma(ro_f[:, k * 8:(k + 1) * 8], routT[k * 128:(k + 1) * 128, :])
            p.dma(gcol[:, k:k + 1], gam[k * 128:(k + 1) * 128, :])
        for ci in range(NCH):
            c0 = ci * 512
            for k in range(16):
                p.dma(y2_bf[:, k * 512:(k + 1) * 512], y2T[k * 128:(k + 1) * 128, c0:c0 + 512])
            for m in range(8):
                for k in range(16):
                    p.mm(psS, Wo_bf[:, k * 1024 + m * 128:k * 1024 + (m + 1) * 128],
                         y2_bf[:, k * 512:(k + 1) * 512], start=(k == 0), stop=(k == 15),
                         wait=(k == 0))
                p.s('copy', sf, psS)
                p.dma(sT_o[m * 128:(m + 1) * 128, c0:c0 + 512], sf)
            for k in range(8):
                p.dma(xn[:, k * 512:(k + 1) * 512], xnT[k * 128:(k + 1) * 128, c0:c0 + 512])
            for k in range(8):
                p.s('activation', sq, xn[:, k * 512:(k + 1) * 512], AF.Square)
                p.mm(psQ[0:1, :], onec, sq, start=(k == 0), stop=(k == 7))
            p.s('activation', ex2r[0:1, :], psQ[0:1, :], AF.Copy, scale=1.0 / 1024.0)
            p.s('activation', lnvr[0:1, :], ex2r[0:1, :], AF.Ln, bias=cst[0:1, 0:1])
            p.s('activation', rstdr[0:1, :], lnvr[0:1, :], AF.Exp, scale=-0.5)
            p.mm(psR, oner, rstdr[0:1, :])
            for k in range(8):
                p.v('tensor_tensor', nf, xn[:, k * 512:(k + 1) * 512], psR, OP.mult)
                p.v('tensor_scalar', n32[:, k * 512:(k + 1) * 512], nf, gcol[:, k:k + 1],
                    None, OP.mult)
                p.s('copy', n_bf[:, k * 512:(k + 1) * 512], n32[:, k * 512:(k + 1) * 512])
                p.dma(normT_o[k * 128:(k + 1) * 128, c0:c0 + 512], n_bf[:, k * 512:(k + 1) * 512])
            for k in range(8):
                p.mm(psP[0:8, :], ro_f[:, k * 8:(k + 1) * 8], n32[:, k * 512:(k + 1) * 512],
                     start=(k == 0), stop=(k == 7), wait=(k == 0))
            p.s('activation', probs, psP[0:8, :], AF.Sigmoid)
            for j in range(4):
                p.tr(psT, probs[0:8, j * 128:(j + 1) * 128], I_f[0:8, 0:8])
                p.s('copy', tpro, psT)
                p.v('reduce_max', m1, tpro, axis=AX.X)
                p.v('tensor_scalar', eqm, tpro, m1, None, OP.is_ge)
                p.v('reduce_sum', cnt, eqm, axis=AX.X)
                p.v('tensor_scalar', big, eqm, 1e30, None, OP.mult)
                p.v('tensor_sub', wo_, tpro, big)
                p.v('reduce_max', m2, wo_, axis=AX.X)
                p.v('tensor_scalar', c2, cnt, 1.5, None, OP.is_ge)
                p.v('tensor_sub', min2, m1, m2)
                p.v('tensor_mul', min2, min2, c2)
                p.v('tensor_add', min2, min2, m2)
                p.v('tensor_scalar', keep, tpro, min2, None, OP.is_ge)
                p.v('tensor_add', den, m1, min2)
                p.v('tensor_scalar', den, den, 1e-6, None, OP.add)
                p.v('reciprocal', rden, den)
                p.v('tensor_mul', combt, tpro, keep)
                p.v('tensor_scalar', combt, combt, rden, None, OP.mult)
                p.tr(psCT, combt, I_f)
                p.s('copy', combf[0:8, j * 128:(j + 1) * 128], psCT[0:8, :])
            p.dma(combT_o[0:8, c0:c0 + 512], combf)
        p.finish()
    return nc


def build_KD1():
    nc = bass.Bass("TRN2", target_bir_lowering=False)
    normT = nc.dram_tensor("normT", [D, TCORE], BF16, kind="ExternalInput").ap()
    sw1 = nc.dram_tensor("sw1", [D, 4096], F32, kind="ExternalInput").ap()
    sw2 = nc.dram_tensor("sw2", [4096, D], F32, kind="ExternalInput").ap()
    sb1 = nc.dram_tensor("sb1", [4096, 1], F32, kind="ExternalInput").ap()
    sb2 = nc.dram_tensor("sb2", [D, 1], F32, kind="ExternalInput").ap()
    sharedT_o = nc.dram_tensor("sharedT", [D, TCORE], F32, kind="ExternalOutput").ap()

    st = ExitStack()
    with st:
        sb = lambda n, s, d: st.enter_context(nc.sbuf_tensor(n, s, d)).ap()
        ps = lambda n, s: st.enter_context(nc.psum_tensor(n, s, F32)).ap()
        w1_bf = sb("w1_bf", [128, 8 * 4096], BF16)
        w2_bf = sb("w2_bf", [128, 32 * 1024], BF16)
        wtmp = sb("wtmp", [128, 1024], F32)
        nb = sb("nb", [128, 8 * 512], BF16)
        he = sb("he", [128, 32 * 512], BF16)
        b1c = sb("b1c", [128, 32], F32)
        b2c = sb("b2c", [128, 8], F32)
        outf = sb("outf", [128, 512], F32)
        psH = ps("psH", [128, 512])
        psO = ps("psO", [128, 512])

        p = Prog(nc)
        for k in range(8):
            for q in range(4):
                p.dma(wtmp, sw1[k * 128:(k + 1) * 128, q * 1024:(q + 1) * 1024])
                p.s('copy', w1_bf[:, k * 4096 + q * 1024:k * 4096 + (q + 1) * 1024], wtmp)
        for k in range(32):
            p.dma(wtmp, sw2[k * 128:(k + 1) * 128, :])
            p.s('copy', w2_bf[:, k * 1024:(k + 1) * 1024], wtmp)
        for m in range(32):
            p.dma(b1c[:, m:m + 1], sb1[m * 128:(m + 1) * 128, :])
        for m in range(8):
            p.dma(b2c[:, m:m + 1], sb2[m * 128:(m + 1) * 128, :])
        for ci in range(NCH):
            c0 = ci * 512
            for k in range(8):
                p.dma(nb[:, k * 512:(k + 1) * 512], normT[k * 128:(k + 1) * 128, c0:c0 + 512])
            for m in range(32):
                for k in range(8):
                    p.mm(psH, w1_bf[:, k * 4096 + m * 128:k * 4096 + (m + 1) * 128],
                         nb[:, k * 512:(k + 1) * 512], start=(k == 0), stop=(k == 7),
                         wait=(k == 0))
                p.s('activation', he[:, m * 512:(m + 1) * 512], psH, AF.Relu,
                    bias=b1c[:, m:m + 1])
            for m in range(8):
                for k in range(32):
                    p.mm(psO, w2_bf[:, k * 1024 + m * 128:k * 1024 + (m + 1) * 128],
                         he[:, k * 512:(k + 1) * 512], start=(k == 0), stop=(k == 31),
                         wait=(k == 0))
                p.v('tensor_scalar', outf, psO, b2c[:, m:m + 1], None, OP.add)
                p.dma(sharedT_o[m * 128:(m + 1) * 128, c0:c0 + 512], outf)
        p.finish()
    return nc


def build_KD2():
    TH = 8192
    nc = bass.Bass("TRN2", target_bir_lowering=False)
    normT = nc.dram_tensor("normT", [D, TH], BF16, kind="ExternalInput").ap()
    ew1 = nc.dram_tensor("ew1", [D, 4096], F32, kind="ExternalInput").ap()
    ew2 = nc.dram_tensor("ew2", [4096, D], F32, kind="ExternalInput").ap()
    eb1 = nc.dram_tensor("eb1", [4096, 1], F32, kind="ExternalInput").ap()
    eb2 = nc.dram_tensor("eb2", [D, 1], F32, kind="ExternalInput").ap()
    combr = nc.dram_tensor("combr", [1, TH], F32, kind="ExternalInput").ap()
    onesrow = nc.dram_tensor("onesrow", [1, 128], F32, kind="ExternalInput").ap()
    routT_o = nc.dram_tensor("routT", [D, TH], F32, kind="ExternalOutput").ap()

    st = ExitStack()
    with st:
        sb = lambda n, s, d: st.enter_context(nc.sbuf_tensor(n, s, d)).ap()
        ps = lambda n, s: st.enter_context(nc.psum_tensor(n, s, F32)).ap()
        w1_bf = sb("w1_bf", [128, 8 * 4096], BF16)
        w2_bf = sb("w2_bf", [128, 32 * 1024], BF16)
        wtmp = sb("wtmp", [128, 1024], F32)
        nb = sb("nb", [128, 8 * 512], BF16)
        he = sb("he", [128, 32 * 512], BF16)
        b1c = sb("b1c", [128, 32], F32)
        b2c = sb("b2c", [128, 8], F32)
        crow = sb("crow", [1, 512], F32)
        gbc = sb("gbc", [128, 512], F32)
        outf = sb("outf", [128, 512], F32)
        oner = sb("oner", [1, 128], F32)
        psH = ps("psH", [128, 512])
        psO = ps("psO", [128, 512])
        psG = ps("psG", [128, 512])

        p = Prog(nc)
        p.dma(oner, onesrow)
        for k in range(8):
            for q in range(4):
                p.dma(wtmp, ew1[k * 128:(k + 1) * 128, q * 1024:(q + 1) * 1024])
                p.s('copy', w1_bf[:, k * 4096 + q * 1024:k * 4096 + (q + 1) * 1024], wtmp)
        for k in range(32):
            p.dma(wtmp, ew2[k * 128:(k + 1) * 128, :])
            p.s('copy', w2_bf[:, k * 1024:(k + 1) * 1024], wtmp)
        for m in range(32):
            p.dma(b1c[:, m:m + 1], eb1[m * 128:(m + 1) * 128, :])
        for m in range(8):
            p.dma(b2c[:, m:m + 1], eb2[m * 128:(m + 1) * 128, :])
        for ci in range(TH // 512):
            c0 = ci * 512
            for k in range(8):
                p.dma(nb[:, k * 512:(k + 1) * 512], normT[k * 128:(k + 1) * 128, c0:c0 + 512])
            p.dma(crow, combr[0:1, c0:c0 + 512])
            p.mm(psG, oner, crow)
            p.s('copy', gbc, psG)
            for m in range(32):
                for k in range(8):
                    p.mm(psH, w1_bf[:, k * 4096 + m * 128:k * 4096 + (m + 1) * 128],
                         nb[:, k * 512:(k + 1) * 512], start=(k == 0), stop=(k == 7),
                         wait=(k == 0))
                p.s('activation', he[:, m * 512:(m + 1) * 512], psH, AF.Relu,
                    bias=b1c[:, m:m + 1])
            for m in range(8):
                for k in range(32):
                    p.mm(psO, w2_bf[:, k * 1024 + m * 128:k * 1024 + (m + 1) * 128],
                         he[:, k * 512:(k + 1) * 512], start=(k == 0), stop=(k == 31),
                         wait=(k == 0))
                p.v('tensor_scalar', outf, psO, b2c[:, m:m + 1], None, OP.add)
                p.v('tensor_mul', outf, outf, gbc)
                p.dma(routT_o[m * 128:(m + 1) * 128, c0:c0 + 512], outf)
        p.finish()
    return nc


def build_KDsum():
    nc = bass.Bass("TRN2", target_bir_lowering=False)
    rins = [nc.dram_tensor(f"r{e}", [D, TCORE], F32, kind="ExternalInput").ap() for e in range(8)]
    sharedT = nc.dram_tensor("sharedT", [D, TCORE], F32, kind="ExternalInput").ap()
    sT = nc.dram_tensor("sT", [D, TCORE], F32, kind="ExternalInput").ap()
    x0T = nc.dram_tensor("x0T", [D, TCORE], F32, kind="ExternalInput").ap()
    x1T_o = nc.dram_tensor("x1T", [D, TCORE], F32, kind="ExternalOutput").ap()

    st = ExitStack()
    with st:
        sb = lambda n, s, d: st.enter_context(nc.sbuf_tensor(n, s, d)).ap()
        acc = sb("acc", [128, 512], F32)
        t2 = sb("t2", [128, 512], F32)
        g = sb("g", [128, 512], F32)
        p = Prog(nc)
        for ci in range(NCH):
            c0 = ci * 512
            for m in range(8):
                rs = (m * 128, (m + 1) * 128)
                p.dma(acc, sharedT[rs[0]:rs[1], c0:c0 + 512])
                for e in range(8):
                    p.dma(t2, rins[e][rs[0]:rs[1], c0:c0 + 512])
                    p.v('tensor_add', acc, acc, t2)
                p.s('activation', g, acc, AF.Silu)
                p.dma(t2, sT[rs[0]:rs[1], c0:c0 + 512])
                p.v('tensor_mul', g, g, t2)
                p.dma(t2, x0T[rs[0]:rs[1], c0:c0 + 512])
                p.v('tensor_add', g, g, t2)
                p.dma(x1T_o[rs[0]:rs[1], c0:c0 + 512], g)
        p.finish()
    return nc


def build_KE():
    nc = bass.Bass("TRN2", target_bir_lowering=False)
    x1T = nc.dram_tensor("x1T", [D, TCORE], F32, kind="ExternalInput").ap()
    Wqkv = nc.dram_tensor("Wqkv", [D, 3072], F32, kind="ExternalInput").ap()
    cosB = nc.dram_tensor("cosB", [128, TCORE], F32, kind="ExternalInput").ap()
    sinB = nc.dram_tensor("sinB", [128, TCORE], F32, kind="ExternalInput").ap()
    I128b = nc.dram_tensor("I128b", [128, 128], BF16, kind="ExternalInput").ap()
    ones128 = nc.dram_tensor("ones128", [128, 1], F32, kind="ExternalInput").ap()
    onesrow = nc.dram_tensor("onesrow", [1, 128], F32, kind="ExternalInput").ap()
    qT_o = nc.dram_tensor("qT", [D, TCORE], BF16, kind="ExternalOutput").ap()
    kT_o = nc.dram_tensor("kT", [D, TCORE], BF16, kind="ExternalOutput").ap()
    vR_o = nc.dram_tensor("vR", [TCORE, D], BF16, kind="ExternalOutput").ap()
    xn1T_o = nc.dram_tensor("xn1T", [D, TCORE], BF16, kind="ExternalOutput").ap()
    pool_o = nc.dram_tensor("pool", [128, 8], F32, kind="ExternalOutput").ap()

    st = ExitStack()
    with st:
        sb = lambda n, s, d: st.enter_context(nc.sbuf_tensor(n, s, d)).ap()
        ps = lambda n, s, d=F32: st.enter_context(nc.psum_tensor(n, s, d)).ap()
        Wq_bf = sb("Wq_bf", [128, 8 * 3072], BF16)
        wtmp = sb("wtmp", [128, 1024], F32)
        xin = sb("xin", [128, 8 * 512], F32)
        sq = sb("sq", [128, 512], F32)
        xnf = sb("xnf", [128, 512], F32)
        xn_bf = sb("xn_bf", [128, 8 * 512], BF16)
        cosb = sb("cosb", [128, 2048], F32)
        sinb = sb("sinb", [128, 2048], F32)
        qf = sb("qf", [128, 512], F32)
        qsw = sb("qsw", [128, 512], F32)
        t1_ = sb("t1_", [128, 512], F32)
        t2_ = sb("t2_", [128, 512], F32)
        qbf = sb("qbf", [128, 512], BF16)
        vbf = sb("vbf", [128, 512], BF16)
        vrow = sb("vrow", [128, 128], BF16)
        poolacc = sb("poolacc", [128, 8], F32)
        pcol = sb("pcol", [128, 1], F32)
        onec = sb("onec", [128, 1], F32)
        oner = sb("oner", [1, 128], F32)
        I_b = sb("I_b", [128, 128], BF16)
        meanr = sb("meanr", [1, 512], F32)
        ex2r = sb("ex2r", [1, 512], F32)
        msqr = sb("msqr", [1, 512], F32)
        varr = sb("varr", [1, 512], F32)
        lnvr = sb("lnvr", [1, 512], F32)
        rstdr = sb("rstdr", [1, 512], F32)
        cst = sb("cst", [128, 1], F32)
        psS = ps("psS", [1, 512])
        psQ = ps("psQ", [1, 512])
        psM = ps("psM", [128, 512])
        psR = ps("psR", [128, 512])
        psX = ps("psX", [128, 512])
        psTb = ps("psTb", [128, 128], BF16)

        p = Prog(nc)
        p.v('memset', cst, 1e-5)
        p.dma(onec, ones128)
        p.dma(oner, onesrow)
        p.dma(I_b, I128b)
        p.dma(cosb, cosB)
        p.dma(sinb, sinB)
        p.v('memset', poolacc, 0.0)
        for k in range(8):
            for q in range(3):
                p.dma(wtmp, Wqkv[k * 128:(k + 1) * 128, q * 1024:(q + 1) * 1024])
                p.s('copy', Wq_bf[:, k * 3072 + q * 1024:k * 3072 + (q + 1) * 1024], wtmp)
        for ci in range(NCH):
            c0 = ci * 512
            for k in range(8):
                p.dma(xin[:, k * 512:(k + 1) * 512], x1T[k * 128:(k + 1) * 128, c0:c0 + 512])
            for k in range(8):
                p.mm(psS[0:1, :], onec, xin[:, k * 512:(k + 1) * 512], start=(k == 0),
                     stop=(k == 7), wait=(k == 0))
            for k in range(8):
                p.s('activation', sq, xin[:, k * 512:(k + 1) * 512], AF.Square)
                p.mm(psQ[0:1, :], onec, sq, start=(k == 0), stop=(k == 7))
            p.s('activation', meanr[0:1, :], psS[0:1, :], AF.Copy, scale=1.0 / 1024.0)
            p.s('activation', ex2r[0:1, :], psQ[0:1, :], AF.Copy, scale=1.0 / 1024.0)
            p.v('tensor_mul', msqr[0:1, :], meanr[0:1, :], meanr[0:1, :])
            p.v('tensor_sub', varr[0:1, :], ex2r[0:1, :], msqr[0:1, :])
            p.s('activation', lnvr[0:1, :], varr[0:1, :], AF.Ln, bias=cst[0:1, 0:1])
            p.s('activation', rstdr[0:1, :], lnvr[0:1, :], AF.Exp, scale=-0.5)
            p.mm(psM, oner, meanr[0:1, :])
            p.mm(psR, oner, rstdr[0:1, :])
            for k in range(8):
                p.v('tensor_sub', xnf, xin[:, k * 512:(k + 1) * 512], psM)
                p.v('tensor_mul', xnf, xnf, psR)
                p.v('reduce_sum', pcol, xnf, axis=AX.X)
                p.v('tensor_add', poolacc[:, k:k + 1], poolacc[:, k:k + 1], pcol)
                p.s('copy', xn_bf[:, k * 512:(k + 1) * 512], xnf)
                p.dma(xn1T_o[k * 128:(k + 1) * 128, c0:c0 + 512], xn_bf[:, k * 512:(k + 1) * 512])
            for m in range(24):
                for k in range(8):
                    p.mm(psX, Wq_bf[:, k * 3072 + m * 128:k * 3072 + (m + 1) * 128],
                         xn_bf[:, k * 512:(k + 1) * 512], start=(k == 0), stop=(k == 7),
                         wait=(k == 0))
                if m < 16:
                    p.s('copy', qf, psX)
                    for g4 in range(4):
                        p.dma(qsw[((g4 ^ 1) * 32):((g4 ^ 1) * 32 + 32), :],
                              qf[g4 * 32:g4 * 32 + 32, :])
                    p.v('tensor_mul', t1_, qf, cosb[:, c0:c0 + 512])
                    p.v('tensor_mul', t2_, qsw, sinb[:, c0:c0 + 512])
                    p.v('tensor_add', t1_, t1_, t2_)
                    p.s('copy', qbf, t1_)
                    dst = qT_o if m < 8 else kT_o
                    mm = m if m < 8 else m - 8
                    p.dma(dst[mm * 128:(mm + 1) * 128, c0:c0 + 512], qbf)
                else:
                    p.s('copy', vbf, psX)
                    for j in range(4):
                        p.tr(psTb, vbf[:, j * 128:(j + 1) * 128], I_b)
                        p.s('copy', vrow, psTb)
                        p.dma(vR_o[c0 + j * 128:c0 + (j + 1) * 128,
                                   (m - 16) * 128:(m - 15) * 128], vrow)
        p.dma(pool_o, poolacc)
        p.finish()
    return nc


def build_KF():
    T = 8192
    nc = bass.Bass("TRN2", target_bir_lowering=False)
    qT = nc.dram_tensor("qT", [256, T], BF16, kind="ExternalInput").ap()
    kT = nc.dram_tensor("kT", [256, T], BF16, kind="ExternalInput").ap()
    vR = nc.dram_tensor("vR", [T, 256], BF16, kind="ExternalInput").ap()
    TRI = nc.dram_tensor("TRI", [128, 4 * 512], F32, kind="ExternalInput").ap()
    ones128 = nc.dram_tensor("ones128", [128, 1], F32, kind="ExternalInput").ap()
    onesrow = nc.dram_tensor("onesrow", [1, 128], F32, kind="ExternalInput").ap()
    aoT_o = nc.dram_tensor("aoT", [256, T], BF16, kind="ExternalOutput").ap()

    st = ExitStack()
    with st:
        sb = lambda n, s, d: st.enter_context(nc.sbuf_tensor(n, s, d)).ap()
        ps = lambda n, s: st.enter_context(nc.psum_tensor(n, s, F32)).ap()
        qfull = sb("qfull", [64, T], BF16)
        kfull = sb("kfull", [64, T], BF16)
        Vt = sb("Vt", [128, 64 * 64], BF16)
        tri = sb("tri", [128, 4 * 512], F32)
        ptef = sb("ptef", [128, 512], F32)
        ptbf = sb("ptbf", [128, 512], BF16)
        rs_ = sb("rs_", [1, 512], F32)
        rrs = sb("rrs", [1, 512], F32)
        of_ = sb("of_", [64, 512], F32)
        aob = sb("aob", [64, 512], BF16)
        onec = sb("onec", [128, 1], F32)
        oner = sb("oner", [1, 128], F32)
        psST = ps("psST", [128, 512])
        psRS = ps("psRS", [1, 512])
        psRR = ps("psRR", [128, 512])
        psO = ps("psO", [64, 512])

        p = Prog(nc)
        p.dma(onec, ones128)
        p.dma(oner, onesrow)
        p.dma(tri, TRI)
        for u in range(4):
            p.dma(qfull, qT[u * 64:(u + 1) * 64, :])
            p.dma(kfull, kT[u * 64:(u + 1) * 64, :])
            for t in range(64):
                p.dma(Vt[:, t * 64:(t + 1) * 64],
                      vR[t * 128:(t + 1) * 128, u * 64:(u + 1) * 64])
            for Q in range(16):
                nk = 4 * (Q + 1)
                for kt in range(nk):
                    p.mm(psST, kfull[:, kt * 128:(kt + 1) * 128],
                         qfull[:, Q * 512:(Q + 1) * 512])
                    if kt >= 4 * Q:
                        j = kt - 4 * Q
                        p.v('tensor_tensor', psST, psST, tri[:, j * 512:(j + 1) * 512], OP.add)
                    p.s('activation', ptef, psST, AF.Exp, scale=0.125)
                    p.mm(psRS[0:1, :], onec, ptef, start=(kt == 0), stop=(kt == nk - 1))
                    p.s('copy', ptbf, ptef)
                    p.mm(psO, Vt[:, kt * 64:(kt + 1) * 64], ptbf, start=(kt == 0),
                         stop=(kt == nk - 1))
                p.s('copy', rs_[0:1, :], psRS[0:1, :])
                p.v('reciprocal', rrs[0:1, :], rs_[0:1, :])
                p.mm(psRR, oner, rrs[0:1, :])
                p.s('copy', of_, psO[0:64, :])
                p.v('tensor_tensor', of_, of_, psRR[0:64, :], OP.mult)
                p.s('copy', aob, of_)
                p.dma(aoT_o[u * 64:(u + 1) * 64, Q * 512:(Q + 1) * 512], aob)
        p.finish()
    return nc


def build_KG(b_core):
    nc = bass.Bass("TRN2", target_bir_lowering=False)
    aoT = nc.dram_tensor("aoT", [D, TCORE], BF16, kind="ExternalInput").ap()
    WoA = nc.dram_tensor("WoA", [D, D], F32, kind="ExternalInput").ap()
    poolP = nc.dram_tensor("poolP", [8, D], F32, kind="ExternalInput").ap()
    gwT = nc.dram_tensor("gwT", [D, 8], F32, kind="ExternalInput").ap()
    gb = nc.dram_tensor("gb", [8, 1], F32, kind="ExternalInput").ap()
    expT = nc.dram_tensor("expT", [8 * D, D], F32, kind="ExternalInput").ap()
    ebT = nc.dram_tensor("ebT", [D, 8], F32, kind="ExternalInput").ap()
    xn1T = nc.dram_tensor("xn1T", [D, TCORE], BF16, kind="ExternalInput").ap()
    x1T = nc.dram_tensor("x1T", [D, TCORE], F32, kind="ExternalInput").ap()
    I128 = nc.dram_tensor("I128", [128, 128], F32, kind="ExternalInput").ap()
    onesrow = nc.dram_tensor("onesrow", [1, 128], F32, kind="ExternalInput").ap()
    outT_o = nc.dram_tensor("outT", [D, TCORE], F32, kind="ExternalOutput").ap()

    st = ExitStack()
    with st:
        sb = lambda n, s, d: st.enter_context(nc.sbuf_tensor(n, s, d)).ap()
        ps = lambda n, s: st.enter_context(nc.psum_tensor(n, s, F32)).ap()
        Wo_bf = sb("Wo_bf", [128, 8 * 1024], BF16)
        ew_bf = sb("ew_bf", [128, 8 * 1024], BF16)
        wtmp = sb("wtmp", [128, 1024], F32)
        gwc = sb("gwc", [128, 8 * 8], F32)
        ebc = sb("ebc", [128, 8 * 8], F32)
        gbc = sb("gbc", [8, 1], F32)
        pr = sb("pr", [8, 1024], F32)
        ptj = sb("ptj", [128, 8], F32)
        pool2 = sb("pool2", [128, 16], F32)
        lgf = sb("lgf", [8, 2], F32)
        lg2 = sb("lg2", [2, 8], F32)
        m1 = sb("m1", [2, 1], F32)
        mneg = sb("mneg", [2, 1], F32)
        eqm = sb("eqm", [2, 8], F32)
        cnt = sb("cnt", [2, 1], F32)
        big = sb("big", [2, 8], F32)
        wo_ = sb("wo_", [2, 8], F32)
        m2 = sb("m2", [2, 1], F32)
        c2 = sb("c2", [2, 1], F32)
        min2 = sb("min2", [2, 1], F32)
        keep = sb("keep", [2, 8], F32)
        ex = sb("ex", [2, 8], F32)
        den = sb("den", [2, 1], F32)
        rden = sb("rden", [2, 1], F32)
        gat = sb("gat", [2, 8], F32)
        gT8 = sb("gT8", [8, 2], F32)
        grow = sb("grow", [1, 16], F32)
        gcols = sb("gcols", [128, 8], F32)
        xnb = sb("xnb", [128, 8 * 512], BF16)
        aob = sb("aob", [128, 8 * 512], BF16)
        s1f = sb("s1f", [128, 8 * 512], F32)
        accf = sb("accf", [128, 8 * 512], F32)
        eo = sb("eo", [128, 512], F32)
        g1 = sb("g1", [128, 512], F32)
        x1f = sb("x1f", [128, 512], F32)
        I_f = sb("I_f", [128, 128], F32)
        oner = sb("oner", [1, 128], F32)
        psX = ps("psX", [128, 512])
        psPT = ps("psPT", [128, 8])
        psL = ps("psL", [8, 2])
        psLT = ps("psLT", [2, 8])
        psGT = ps("psGT", [8, 2])
        psGC = ps("psGC", [128, 1])

        p = Prog(nc)
        p.dma(oner, onesrow)
        p.dma(I_f, I128)
        p.dma(gbc, gb)
        for k in range(8):
            p.dma(wtmp, WoA[k * 128:(k + 1) * 128, :])
            p.s('copy', Wo_bf[:, k * 1024:(k + 1) * 1024], wtmp)
            p.dma(gwc[:, k * 8:(k + 1) * 8], gwT[k * 128:(k + 1) * 128, :])
            p.dma(ebc[:, k * 8:(k + 1) * 8], ebT[k * 128:(k + 1) * 128, :])
        p.dma(pr, poolP)
        for j in range(8):
            p.tr(psPT, pr[0:8, j * 128:(j + 1) * 128], I_f[0:8, 0:8])
            p.s('copy', ptj, psPT)
            p.v('reduce_sum', pool2[:, j * 2:j * 2 + 1], ptj[:, 0:4], axis=AX.X)
            p.v('reduce_sum', pool2[:, j * 2 + 1:j * 2 + 2], ptj[:, 4:8], axis=AX.X)
        p.v('tensor_scalar', pool2, pool2, 1.0 / 8192.0, None, OP.mult)
        for j in range(8):
            p.mm(psL[0:8, 0:2], gwc[:, j * 8:(j + 1) * 8], pool2[:, j * 2:(j + 1) * 2],
                 start=(j == 0), stop=(j == 7), wait=(j == 0))
        p.v('tensor_scalar', lgf, psL[0:8, 0:2], gbc[0:8, 0:1], None, OP.add)
        p.tr(psLT, lgf[0:8, 0:2], I_f[0:8, 0:8])
        p.s('copy', lg2, psLT[0:2, 0:8])
        p.v('reduce_max', m1, lg2, axis=AX.X)
        p.v('tensor_scalar', mneg, m1, -1.0, None, OP.mult)
        p.v('tensor_scalar', eqm, lg2, m1, None, OP.is_ge)
        p.v('reduce_sum', cnt, eqm, axis=AX.X)
        p.v('tensor_scalar', big, eqm, 1e30, None, OP.mult)
        p.v('tensor_sub', wo_, lg2, big)
        p.v('reduce_max', m2, wo_, axis=AX.X)
        p.v('tensor_scalar', c2, cnt, 1.5, None, OP.is_ge)
        p.v('tensor_sub', min2, m1, m2)
        p.v('tensor_mul', min2, min2, c2)
        p.v('tensor_add', min2, min2, m2)
        p.v('tensor_scalar', keep, lg2, min2, None, OP.is_ge)
        p.s('activation', ex, lg2, AF.Exp, bias=mneg)
        p.v('tensor_mul', ex, ex, keep)
        p.v('reduce_sum', den, ex, axis=AX.X)
        p.v('reciprocal', rden, den)
        p.v('tensor_scalar', gat, ex, rden, None, OP.mult)
        p.tr(psGT, gat[0:2, 0:8], I_f[0:2, 0:2])
        p.s('copy', gT8, psGT[0:8, 0:2])
        for e in range(8):
            p.dma(grow[0:1, e * 2:(e + 1) * 2], gT8[e:e + 1, 0:2])
        for e in range(8):
            p.mm(psGC, oner, grow[0:1, e * 2 + b_core:e * 2 + b_core + 1])
            p.s('copy', gcols[:, e:e + 1], psGC)
        for ci in range(NCH):
            c0 = ci * 512
            for k in range(8):
                p.dma(aob[:, k * 512:(k + 1) * 512], aoT[k * 128:(k + 1) * 128, c0:c0 + 512])
                p.dma(xnb[:, k * 512:(k + 1) * 512], xn1T[k * 128:(k + 1) * 128, c0:c0 + 512])
            for m in range(8):
                for k in range(8):
                    p.mm(psX, Wo_bf[:, k * 1024 + m * 128:k * 1024 + (m + 1) * 128],
                         aob[:, k * 512:(k + 1) * 512], start=(k == 0), stop=(k == 7),
                         wait=(k == 0))
                p.s('copy', s1f[:, m * 512:(m + 1) * 512], psX)
            for e in range(8):
                for k in range(8):
                    p.dma(wtmp, expT[e * 1024 + k * 128:e * 1024 + (k + 1) * 128, :])
                    p.s('copy', ew_bf[:, k * 1024:(k + 1) * 1024], wtmp)
                for m in range(8):
                    for k in range(8):
                        p.mm(psX, ew_bf[:, k * 1024 + m * 128:k * 1024 + (m + 1) * 128],
                             xnb[:, k * 512:(k + 1) * 512], start=(k == 0), stop=(k == 7),
                             wait=(k == 0))
                    p.s('activation', eo, psX, AF.Gelu, bias=ebc[:, m * 8 + e:m * 8 + e + 1])
                    if e == 0:
                        p.v('tensor_scalar', accf[:, m * 512:(m + 1) * 512], eo,
                            gcols[:, 0:1], None, OP.mult)
                    else:
                        p.v('scalar_tensor_tensor', accf[:, m * 512:(m + 1) * 512], eo,
                            gcols[:, e:e + 1], accf[:, m * 512:(m + 1) * 512],
                            OP.mult, OP.add)
            for m in range(8):
                p.s('activation', g1, accf[:, m * 512:(m + 1) * 512], AF.Silu)
                p.v('tensor_mul', g1, g1, s1f[:, m * 512:(m + 1) * 512])
                p.dma(x1f, x1T[m * 128:(m + 1) * 128, c0:c0 + 512])
                p.v('tensor_add', g1, g1, x1f)
                p.dma(outT_o[m * 128:(m + 1) * 128, c0:c0 + 512], g1)
        p.finish()
    return nc


def _f(a):
    return np.ascontiguousarray(a, dtype=np.float32)


def kernel(**inp):
    x = _f(inp['L0_attn_x'])
    xT = np.ascontiguousarray(x.reshape(NT, D).T)
    ones128 = np.ones((128, 1), f32)
    onesrow = np.ones((1, 128), f32)
    I128 = np.eye(128, dtype=f32)
    I128b = np.eye(128, dtype=BF)

    W_inT = np.ascontiguousarray(_f(inp['L0_attn_W_in']).T)
    W_xT = np.ascontiguousarray(_f(inp['L0_attn_W_x']).T)
    W_dtT = np.ascontiguousarray(_f(inp['L0_attn_W_dt']).T)
    convw = np.ascontiguousarray(_f(inp['L0_attn_conv_w'])[:, 0, :])
    convb = _f(inp['L0_attn_conv_b']).reshape(DI, 1)
    bdt = _f(inp['L0_attn_b_dt']).reshape(DI, 1)
    nc = build_KA()
    maps = []
    for c in range(NCORE):
        xin = np.zeros((D, TCORE + 3), f32)
        xin[:, 3:] = xT[:, c * TCORE:(c + 1) * TCORE]
        if c not in (0, 4):
            xin[:, :3] = xT[:, c * TCORE - 3:c * TCORE]
        maps.append(dict(xin=xin, WinT=W_inT, WxT=W_xT, WdtT=W_dtT, convw=convw,
                         convb=convb, bdt=bdt, ones128=ones128, onesrow=onesrow))
    rA = _run(nc, maps)
    xnT = np.concatenate([r['xnT'] for r in rA], axis=1)
    xcT = np.concatenate([r['xcT'] for r in rA], axis=1)
    dtT = np.concatenate([r['dtT'] for r in rA], axis=1)
    resT = np.concatenate([r['resT'] for r in rA], axis=1)
    BTf = np.concatenate([r['BT'] for r in rA], axis=1)
    CTf = np.concatenate([r['CT'] for r in rA], axis=1)

    At = np.ascontiguousarray(np.tile(_f(inp['L0_attn_A_log']).reshape(1, 16), (128, 1)))
    Dvf = _f(inp['L0_attn_D']).reshape(DI, 1)
    nc = build_KB()
    maps = []
    for c in range(NCORE):
        r0 = c * 256
        maps.append(dict(dtT=np.ascontiguousarray(dtT[r0:r0 + 256]),
                         xcT=np.ascontiguousarray(xcT[r0:r0 + 256]),
                         resT=np.ascontiguousarray(resT[r0:r0 + 256]),
                         BT=BTf, CT=CTf, At=At,
                         Dv=np.ascontiguousarray(Dvf[r0:r0 + 256]),
                         onesrow=onesrow))
    rB = _run(nc, maps)
    y2T = np.concatenate([r['y2T'] for r in rB], axis=0)

    W_outT = np.ascontiguousarray(_f(inp['L0_attn_W_out']).T)
    gam = _f(inp['L0_ffn_norm_weight']).reshape(D, 1)
    routT = np.ascontiguousarray(_f(inp['L0_ffn_router_w']).T)
    nc = build_KC()
    maps = []
    for c in range(NCORE):
        cs = slice(c * TCORE, (c + 1) * TCORE)
        maps.append(dict(y2T=np.ascontiguousarray(y2T[:, cs]), WoutT=W_outT,
                         xnT=np.ascontiguousarray(xnT[:, cs]), gam=gam, routT=routT,
                         I128=I128, ones128=ones128, onesrow=onesrow))
    rC = _run(nc, maps)
    sT = [r['sT'] for r in rC]
    normT = [r['normT'] for r in rC]
    combT = np.concatenate([r['combT'] for r in rC], axis=1)

    nc = build_KD1()
    maps = [dict(normT=normT[c], sw1=_f(inp['L0_ffn_sw1']), sw2=_f(inp['L0_ffn_sw2']),
                 sb1=_f(inp['L0_ffn_sb1']).reshape(4096, 1),
                 sb2=_f(inp['L0_ffn_sb2']).reshape(D, 1)) for c in range(NCORE)]
    rD1 = _run(nc, maps)
    sharedT = [r['sharedT'] for r in rD1]

    ew1 = _f(inp['L0_ffn_ew1'])
    ew2 = _f(inp['L0_ffn_ew2'])
    eb1 = _f(inp['L0_ffn_eb1'])
    eb2 = _f(inp['L0_ffn_eb2'])
    normhalf = [np.concatenate([normT[c] for c in range(4)], axis=1),
                np.concatenate([normT[c] for c in range(4, 8)], axis=1)]
    routhalf = []
    nc = build_KD2()
    for half in range(2):
        cs = slice(half * 8192, (half + 1) * 8192)
        maps = []
        for e in range(NCORE):
            maps.append(dict(normT=normhalf[half], ew1=np.ascontiguousarray(ew1[e]),
                             ew2=np.ascontiguousarray(ew2[e]),
                             eb1=np.ascontiguousarray(eb1[e]).reshape(4096, 1),
                             eb2=np.ascontiguousarray(eb2[e]).reshape(D, 1),
                             combr=np.ascontiguousarray(combT[e:e + 1, cs]),
                             onesrow=onesrow))
        rD2 = _run(nc, maps)
        routhalf.append([r['routT'] for r in rD2])

    nc = build_KDsum()
    maps = []
    for c in range(NCORE):
        half = c // 4
        lo = (c % 4) * TCORE
        m = {f'r{e}': np.ascontiguousarray(routhalf[half][e][:, lo:lo + TCORE])
             for e in range(8)}
        m['sharedT'] = sharedT[c]
        m['sT'] = sT[c]
        m['x0T'] = np.ascontiguousarray(xT[:, c * TCORE:(c + 1) * TCORE])
        maps.append(m)
    rDs = _run(nc, maps)
    x1T = [r['x1T'] for r in rDs]

    DH = 64
    inv = 1.0 / (10000.0 ** (np.arange(0, DH, 2, dtype=np.float64) / DH))
    fr = np.arange(8192, dtype=np.float64)[:, None] * inv[None, :]
    cosT = np.cos(fr).T.astype(f32)
    sinT = np.sin(fr).T.astype(f32)
    cosB = np.ascontiguousarray(np.tile(cosT, (4, 1)))
    sinB = np.ascontiguousarray(np.concatenate([-sinT, sinT, -sinT, sinT], axis=0))
    nc = build_KE()
    maps = []
    for c in range(NCORE):
        t0 = (c * TCORE) % 8192
        maps.append(dict(x1T=x1T[c], Wqkv=_f(inp['L1_attn_Wqkv']),
                         cosB=np.ascontiguousarray(cosB[:, t0:t0 + TCORE]),
                         sinB=np.ascontiguousarray(sinB[:, t0:t0 + TCORE]),
                         I128b=I128b, ones128=ones128, onesrow=onesrow))
    rE = _run(nc, maps)
    qTf = np.concatenate([r['qT'] for r in rE], axis=1)
    kTf = np.concatenate([r['kT'] for r in rE], axis=1)
    vRf = np.concatenate([r['vR'] for r in rE], axis=0)
    xn1T = [r['xn1T'] for r in rE]
    poolP = np.stack([np.ascontiguousarray(r['pool'].T).reshape(D) for r in rE], axis=0)

    TRI = np.zeros((128, 4 * 512), f32)
    for j in range(4):
        blk = np.where(np.arange(128)[:, None] > (j * 128 + np.arange(512))[None, :] - j * 0,
                       0.0, 0.0)
        r = np.arange(128)[:, None]
        qq = np.arange(512)[None, :]
        mask = (r + j * 128) > qq
        TRI[:, j * 512:(j + 1) * 512] = np.where(mask, -1e30, 0.0)
    nc = build_KF()
    maps = []
    for c in range(NCORE):
        qs, ks, vs = [], [], []
        for u in range(4):
            pair = c * 4 + u
            b, h = pair // 16, pair % 16
            qs.append(qTf[h * 64:(h + 1) * 64, b * 8192:(b + 1) * 8192])
            ks.append(kTf[h * 64:(h + 1) * 64, b * 8192:(b + 1) * 8192])
            vs.append(vRf[b * 8192:(b + 1) * 8192, h * 64:(h + 1) * 64])
        maps.append(dict(qT=np.ascontiguousarray(np.concatenate(qs, axis=0)),
                         kT=np.ascontiguousarray(np.concatenate(ks, axis=0)),
                         vR=np.ascontiguousarray(np.concatenate(vs, axis=1)),
                         TRI=TRI, ones128=ones128, onesrow=onesrow))
    rF = _run(nc, maps)
    aoTf = np.zeros((D, NT), BF)
    for c in range(NCORE):
        ao = rF[c]['aoT']
        for u in range(4):
            pair = c * 4 + u
            b, h = pair // 16, pair % 16
            aoTf[h * 64:(h + 1) * 64, b * 8192:(b + 1) * 8192] = ao[u * 64:(u + 1) * 64]

    gwT = np.ascontiguousarray(_f(inp['L1_ffn_gate_w']).T)
    gb = _f(inp['L1_ffn_gate_b']).reshape(8, 1)
    expT = np.ascontiguousarray(_f(inp['L1_ffn_expert_w']).transpose(0, 2, 1)).reshape(8 * D, D)
    ebT = np.ascontiguousarray(_f(inp['L1_ffn_expert_b']).T)
    nc = build_KG(0)
    maps = []
    for c in range(NCORE):
        cs = slice(c * TCORE, (c + 1) * TCORE)
        b = c // 4
        pp = poolP if b == 0 else np.ascontiguousarray(poolP[[4, 5, 6, 7, 0, 1, 2, 3]])
        maps.append(dict(aoT=np.ascontiguousarray(aoTf[:, cs]), WoA=_f(inp['L1_attn_Wout']),
                         poolP=pp, gwT=gwT, gb=gb, expT=expT, ebT=ebT,
                         xn1T=xn1T[c], x1T=x1T[c], I128=I128, onesrow=onesrow))
    rG = _run(nc, maps)
    outT = np.concatenate([r['outT'] for r in rG], axis=1)
    out = np.ascontiguousarray(outT.T).reshape(2, 8192, D).astype(np.float32)
    return out


# version 04
# speedup vs baseline: 1.0948x; 1.0948x over previous
import numpy as np
import ml_dtypes
import concourse.bass as bass
import concourse.mybir as mybir
from concourse.bass_utils import run_bass_kernel_spmd
from contextlib import ExitStack

F32 = mybir.dt.float32
BF16 = mybir.dt.bfloat16
AF = mybir.ActivationFunctionType
OP = mybir.AluOpType
AX = mybir.AxisListType
BF = ml_dtypes.bfloat16
f32 = np.float32

NCORE = 8
D = 1024
DI = 2048
DS = 16
NT = 16384
TCORE = 2048
CH = 512
NCH = TCORE // CH


class Prog:

    def __init__(self, nc):
        self.nc = nc
        self.ops = []

    def op(self, eng, fn, inc=1, wait=True):
        self.ops.append((eng, fn, inc, wait))

    def dma(self, out, in_, **kw):
        self.op('sync', lambda e, o=out, i=in_, kw=kw: e.dma_start(out=o, in_=i, **kw), inc=16)

    def mm(self, out, lhsT, rhs, start=True, stop=True, wait=True):
        self.op('tensor',
                lambda e, o=out, l=lhsT, r=rhs, s1=start, s2=stop: e.matmul(o, l, r, start=s1, stop=s2),
                inc=1, wait=wait)

    def tr(self, out, in_, ident):
        self.op('tensor', lambda e, o=out, i=in_, d=ident: e.transpose(o, i, d))

    def v(self, method, *a, **kw):
        self.op('vector', lambda e, m=method, a=a, kw=kw: getattr(e, m)(*a, **kw))

    def s(self, method, *a, **kw):
        self.op('scalar', lambda e, m=method, a=a, kw=kw: getattr(e, m)(*a, **kw))

    def finish(self):
        nc = self.nc
        ops = self.ops
        with nc.semaphore("tok") as tok, nc.Block() as block:
            def mk(name):
                def body(eng):
                    cnt = 0
                    prev_en = None
                    for (en, fn, inc, wait) in ops:
                        if en == name:
                            if wait and cnt > 0 and not (
                                    prev_en == name and name in ('sync', 'tensor')):
                                eng.wait_ge(tok, cnt)
                            r = fn(eng)
                            if inc:
                                r.then_inc(tok, inc)
                        cnt += inc
                        prev_en = en
                    if name == 'sync':
                        eng.wait_ge(tok, cnt)
                return body
            block.sync(mk('sync'))
            block.tensor(mk('tensor'))
            block.vector(mk('vector'))
            block.scalar(mk('scalar'))
            if any(en == 'gpsimd' for (en, _, _, _) in ops):
                block.gpsimd(mk('gpsimd'))


def _run(nc, in_maps):
    res = run_bass_kernel_spmd(nc, in_maps, core_ids=list(range(NCORE)))
    return res.results


def build_KA():
    nc = bass.Bass("TRN2", target_bir_lowering=False)
    xin_h = nc.dram_tensor("xin", [D, TCORE + 3], F32, kind="ExternalInput").ap()
    WinT = nc.dram_tensor("WinT", [D, 2 * DI], F32, kind="ExternalInput").ap()
    WxT = nc.dram_tensor("WxT", [DI, 33], F32, kind="ExternalInput").ap()
    WdtT = nc.dram_tensor("WdtT", [1, DI], F32, kind="ExternalInput").ap()
    convw = nc.dram_tensor("convw", [DI, 4], F32, kind="ExternalInput").ap()
    convb = nc.dram_tensor("convb", [DI, 1], F32, kind="ExternalInput").ap()
    bdt = nc.dram_tensor("bdt", [DI, 1], F32, kind="ExternalInput").ap()
    ones128 = nc.dram_tensor("ones128", [128, 1], F32, kind="ExternalInput").ap()
    onesrow = nc.dram_tensor("onesrow", [1, 128], F32, kind="ExternalInput").ap()
    xnT_o = nc.dram_tensor("xnT", [D, TCORE], F32, kind="ExternalOutput").ap()
    xcT_o = nc.dram_tensor("xcT", [DI, TCORE], F32, kind="ExternalOutput").ap()
    dtT_o = nc.dram_tensor("dtT", [DI, TCORE], F32, kind="ExternalOutput").ap()
    resT_o = nc.dram_tensor("resT", [DI, TCORE], F32, kind="ExternalOutput").ap()
    BT_o = nc.dram_tensor("BT", [16, TCORE], F32, kind="ExternalOutput").ap()
    CT_o = nc.dram_tensor("CT", [16, TCORE], F32, kind="ExternalOutput").ap()

    st = ExitStack()
    with st:
        sb = lambda n, s, d: st.enter_context(nc.sbuf_tensor(n, s, d)).ap()
        ps = lambda n, s: st.enter_context(nc.psum_tensor(n, s, F32)).ap()
        xin = sb("xinb", [128, 8 * 512], F32)
        sq = sb("sq", [128, 512], F32)
        xnf = sb("xnf", [128, 512], F32)
        xn_bf = sb("xn_bf", [128, 8 * 512], BF16)
        Win_bf = sb("Win_bf", [128, 8 * 4096], BF16)
        Wx_bf = sb("Wx_bf", [128, 16 * 33], BF16)
        Wdt_bf = sb("Wdt_bf", [1, DI], BF16)
        cwc = sb("cwc", [128, 16 * 4], F32)
        cbc = sb("cbc", [128, 16], F32)
        bdtc = sb("bdtc", [128, 16], F32)
        onec = sb("onec", [128, 1], F32)
        oner = sb("oner", [1, 128], F32)
        xpb = sb("xpb", [128, 16 * 515], F32)
        xcf = sb("xcf", [128, 512], F32)
        accv = sb("accv", [128, 512], F32)
        xc_bf = sb("xc_bf", [128, 16 * 512], BF16)
        resb = sb("resb", [128, 512], F32)
        ssmf = sb("ssmf", [33, 512], F32)
        dtrf = sb("dtrf", [1, 512], F32)
        dtr_bf = sb("dtr_bf", [1, 512], BF16)
        zf = sb("zf", [128, 512], F32)
        azf = sb("azf", [128, 512], F32)
        ef = sb("ef", [128, 512], F32)
        lf = sb("lf", [128, 512], F32)
        rf = sb("rf", [128, 512], F32)
        dtf = sb("dtf", [128, 512], F32)
        meanr = sb("meanr", [1, 512], F32)
        ex2r = sb("ex2r", [1, 512], F32)
        msqr = sb("msqr", [1, 512], F32)
        varr = sb("varr", [1, 512], F32)
        lnvr = sb("lnvr", [1, 512], F32)
        rstdr = sb("rstdr", [1, 512], F32)
        cst = sb("cst", [128, 2], F32)

        psS = ps("psS", [1, 512])
        psQ = ps("psQ", [1, 512])
        psM = ps("psM", [128, 512])
        psR = ps("psR", [128, 512])
        psX = ps("psX", [128, 512])
        psW = ps("psW", [33, 512])
        psD = ps("psD", [128, 512])

        p = Prog(nc)
        p.v('memset', cst[:, 0:1], 1e-5)
        p.v('memset', cst[:, 1:2], 1.0)
        p.dma(onec, ones128)
        p.dma(oner, onesrow)
        for k in range(8):
            p.dma(xin[:, 0:4096], WinT[k * 128:(k + 1) * 128, :])
            p.s('copy', Win_bf[:, k * 4096:(k + 1) * 4096], xin[:, 0:4096])
        for k in range(16):
            p.dma(xin[:, 0:33], WxT[k * 128:(k + 1) * 128, :])
            p.s('copy', Wx_bf[:, k * 33:(k + 1) * 33], xin[:, 0:33])
            p.dma(cwc[:, k * 4:(k + 1) * 4], convw[k * 128:(k + 1) * 128, :])
            p.dma(cbc[:, k:k + 1], convb[k * 128:(k + 1) * 128, :])
            p.dma(bdtc[:, k:k + 1], bdt[k * 128:(k + 1) * 128, :])
        for q in range(4):
            p.dma(xin[0:1, 0:512], WdtT[0:1, q * 512:(q + 1) * 512])
            p.s('copy', Wdt_bf[0:1, q * 512:(q + 1) * 512], xin[0:1, 0:512])

        def chunk(co, W, tail_off, main_ci):
            for k in range(8):
                p.dma(xin[:, k * 512:k * 512 + W], xin_h[k * 128:(k + 1) * 128, co:co + W])
            for k in range(8):
                p.mm(psS[0:1, 0:W], onec, xin[:, k * 512:k * 512 + W],
                     start=(k == 0), stop=(k == 7), wait=(k == 0))
            for k in range(8):
                p.s('activation', sq[:, 0:W], xin[:, k * 512:k * 512 + W], AF.Square)
                p.mm(psQ[0:1, 0:W], onec, sq[:, 0:W], start=(k == 0), stop=(k == 7))
            p.s('activation', meanr[0:1, 0:W], psS[0:1, 0:W], AF.Copy, scale=1.0 / 1024.0)
            p.s('activation', ex2r[0:1, 0:W], psQ[0:1, 0:W], AF.Copy, scale=1.0 / 1024.0)
            p.v('tensor_mul', msqr[0:1, 0:W], meanr[0:1, 0:W], meanr[0:1, 0:W])
            p.v('tensor_sub', varr[0:1, 0:W], ex2r[0:1, 0:W], msqr[0:1, 0:W])
            p.s('activation', lnvr[0:1, 0:W], varr[0:1, 0:W], AF.Ln, bias=cst[0:1, 0:1])
            p.s('activation', rstdr[0:1, 0:W], lnvr[0:1, 0:W], AF.Exp, scale=-0.5)
            p.mm(psM[:, 0:W], oner, meanr[0:1, 0:W])
            p.mm(psR[:, 0:W], oner, rstdr[0:1, 0:W])
            for k in range(8):
                p.v('tensor_sub', xnf[:, 0:W], xin[:, k * 512:k * 512 + W], psM[:, 0:W])
                p.v('tensor_mul', xnf[:, 0:W], xnf[:, 0:W], psR[:, 0:W])
                p.s('copy', xn_bf[:, k * 512:k * 512 + W], xnf[:, 0:W])
                if main_ci is not None:
                    p.dma(xnT_o[k * 128:(k + 1) * 128, main_ci * 512:main_ci * 512 + W], xnf[:, 0:W])
            n_m = 32 if main_ci is not None else 16
            for m in range(n_m):
                for k in range(8):
                    p.mm(psX[:, 0:W], Win_bf[:, k * 4096 + m * 128:k * 4096 + (m + 1) * 128],
                         xn_bf[:, k * 512:k * 512 + W], start=(k == 0), stop=(k == 7), wait=(k == 0))
                if m < 16:
                    p.s('copy', xpb[:, m * 515 + tail_off:m * 515 + tail_off + W], psX[:, 0:W])
                else:
                    p.s('copy', resb[:, 0:W], psX[:, 0:W])
                    p.dma(resT_o[(m - 16) * 128:(m - 15) * 128, main_ci * 512:main_ci * 512 + W],
                          resb[:, 0:W])

        chunk(0, 3, 0, None)
        for ci in range(NCH):
            chunk(3 + ci * 512, 512, 3, ci)
            for j in range(16):
                src = xpb[:, j * 515:(j + 1) * 515]
                p.v('tensor_scalar', accv, src[:, 0:512], cwc[:, j * 4:j * 4 + 1], None, OP.mult)
                for jj in range(1, 4):
                    p.v('scalar_tensor_tensor', accv, src[:, jj:jj + 512],
                        cwc[:, j * 4 + jj:j * 4 + jj + 1], accv, OP.mult, OP.add)
                p.s('activation', xcf, accv, AF.Silu, bias=cbc[:, j:j + 1])
                p.dma(xcT_o[j * 128:(j + 1) * 128, ci * 512:(ci + 1) * 512], xcf)
                p.s('copy', xc_bf[:, j * 512:(j + 1) * 512], xcf)
                p.v('tensor_copy', src[:, 0:3], src[:, 512:515])
            for k in range(16):
                p.mm(psW[0:33, :], Wx_bf[:, k * 33:(k + 1) * 33], xc_bf[:, k * 512:(k + 1) * 512],
                     start=(k == 0), stop=(k == 15), wait=(k == 0))
            p.s('copy', ssmf, psW[0:33, :])
            p.dma(BT_o[0:16, ci * 512:(ci + 1) * 512], ssmf[0:16, :])
            p.dma(CT_o[0:16, ci * 512:(ci + 1) * 512], ssmf[16:32, :])
            p.dma(dtrf[0:1, :], ssmf[32:33, :])
            p.s('copy', dtr_bf, dtrf)
            for m in range(16):
                p.mm(psD, Wdt_bf[0:1, m * 128:(m + 1) * 128], dtr_bf)
                p.v('tensor_scalar', zf, psD, bdtc[:, m:m + 1], None, OP.add)
                p.s('activation', azf, zf, AF.Abs)
                p.s('activation', ef, azf, AF.Exp, scale=-1.0)
                p.s('activation', lf, ef, AF.Ln, bias=cst[:, 1:2])
                p.v('tensor_scalar', rf, zf, 0.0, None, OP.max)
                p.v('tensor_add', dtf, rf, lf)
                p.dma(dtT_o[m * 128:(m + 1) * 128, ci * 512:(ci + 1) * 512], dtf)
        p.finish()
    return nc


def build_KB():
    TC = 1024
    nc = bass.Bass("TRN2", target_bir_lowering=False)
    dtT = nc.dram_tensor("dtT", [256, NT], F32, kind="ExternalInput").ap()
    xcT = nc.dram_tensor("xcT", [256, NT], F32, kind="ExternalInput").ap()
    resT = nc.dram_tensor("resT", [256, NT], F32, kind="ExternalInput").ap()
    BT = nc.dram_tensor("BT", [16, NT], F32, kind="ExternalInput").ap()
    CT = nc.dram_tensor("CT", [16, NT], F32, kind="ExternalInput").ap()
    At = nc.dram_tensor("At", [128, 16], F32, kind="ExternalInput").ap()
    Dv = nc.dram_tensor("Dv", [256, 1], F32, kind="ExternalInput").ap()
    onesrow = nc.dram_tensor("onesrow", [1, 128], F32, kind="ExternalInput").ap()
    y2T = nc.dram_tensor("y2T", [256, NT], BF16, kind="ExternalOutput").ap()

    st = ExitStack()
    with st:
        sb = lambda n, s, d: st.enter_context(nc.sbuf_tensor(n, s, d)).ap()
        ps = lambda n, s: st.enter_context(nc.psum_tensor(n, s, F32)).ap()
        dt_t = [sb(f"dt{e}", [128, TC], F32) for e in range(2)]
        xc_t = [sb(f"xc{e}", [128, TC], F32) for e in range(2)]
        u_t = [sb(f"u{e}", [128, TC], F32) for e in range(2)]
        ya = [sb(f"ya{e}", [128, TC], F32) for e in range(2)]
        a1 = sb("a1", [128, TC], F32)
        a2 = sb("a2", [128, TC], F32)
        bv = sb("bv", [128, TC], F32)
        hh = sb("hh", [128, TC], F32)
        tmp = sb("tmp", [128, TC], F32)
        res_t = sb("res_t", [128, TC], F32)
        y2b = sb("y2b", [128, TC], BF16)
        brow = sb("brow", [1, TC], F32)
        crow = sb("crow", [1, TC], F32)
        Acol = sb("Acol", [128, 16], F32)
        Dcol = sb("Dcol", [128, 2], F32)
        carry = sb("carry", [128, 32], F32)
        oner = sb("oner", [1, 128], F32)
        psB = ps("psB", [128, TC])
        psC = ps("psC", [128, TC])

        p = Prog(nc)
        p.dma(oner, onesrow)
        p.dma(Acol, At)
        p.s('activation', Acol, Acol, AF.Exp)
        p.v('tensor_scalar', Acol, Acol, -1.0, None, OP.mult)
        for e in range(2):
            p.dma(Dcol[:, e:e + 1], Dv[e * 128:(e + 1) * 128, :])
        for b in range(2):
            p.v('memset', carry, 0.0)
            for ch in range(8192 // TC):
                c0 = b * 8192 + ch * TC
                for e in range(2):
                    p.dma(dt_t[e], dtT[e * 128:(e + 1) * 128, c0:c0 + TC])
                    p.dma(xc_t[e], xcT[e * 128:(e + 1) * 128, c0:c0 + TC])
                    p.v('tensor_mul', u_t[e], dt_t[e], xc_t[e])
                for n in range(16):
                    p.dma(brow, BT[n:n + 1, c0:c0 + TC])
                    p.dma(crow, CT[n:n + 1, c0:c0 + TC])
                    for hf in range(2):
                        sl = slice(hf * 512, (hf + 1) * 512)
                        p.mm(psB[:, sl], oner, brow[0:1, sl])
                        p.mm(psC[:, sl], oner, crow[0:1, sl])
                    for e in range(2):
                        p.v('tensor_scalar', a1, dt_t[e], Acol[:, n:n + 1], None, OP.mult)
                        p.s('activation', a2, a1, AF.Exp)
                        for hf in range(2):
                            sl = slice(hf * 512, (hf + 1) * 512)
                            p.v('tensor_tensor', bv[:, sl], u_t[e][:, sl], psB[:, sl], OP.mult)
                        slot = e * 16 + n
                        p.v('tensor_tensor_scan', hh, a2, bv, carry[:, slot:slot + 1],
                            OP.mult, OP.add)
                        p.v('tensor_copy', carry[:, slot:slot + 1], hh[:, TC - 1:TC])
                        for hf in range(2):
                            sl = slice(hf * 512, (hf + 1) * 512)
                            if n == 0:
                                p.v('tensor_tensor', ya[e][:, sl], hh[:, sl], psC[:, sl], OP.mult)
                            else:
                                p.v('tensor_tensor', tmp[:, sl], hh[:, sl], psC[:, sl], OP.mult)
                        if n > 0:
                            p.v('tensor_add', ya[e], ya[e], tmp)
                for e in range(2):
                    p.v('scalar_tensor_tensor', ya[e], xc_t[e], Dcol[:, e:e + 1], ya[e],
                        OP.mult, OP.add)
                    p.dma(res_t, resT[e * 128:(e + 1) * 128, c0:c0 + TC])
                    p.v('tensor_add', ya[e], ya[e], res_t)
                    p.s('activation', y2b, ya[e], AF.Silu)
                    p.dma(y2T[e * 128:(e + 1) * 128, c0:c0 + TC], y2b)
        p.finish()
    return nc


def build_KC():
    nc = bass.Bass("TRN2", target_bir_lowering=False)
    y2T = nc.dram_tensor("y2T", [DI, TCORE], BF16, kind="ExternalInput").ap()
    WoutT = nc.dram_tensor("WoutT", [DI, D], F32, kind="ExternalInput").ap()
    xnT = nc.dram_tensor("xnT", [D, TCORE], F32, kind="ExternalInput").ap()
    gam = nc.dram_tensor("gam", [D, 1], F32, kind="ExternalInput").ap()
    routT = nc.dram_tensor("routT", [D, 8], F32, kind="ExternalInput").ap()
    I128 = nc.dram_tensor("I128", [128, 128], F32, kind="ExternalInput").ap()
    ones128 = nc.dram_tensor("ones128", [128, 1], F32, kind="ExternalInput").ap()
    onesrow = nc.dram_tensor("onesrow", [1, 128], F32, kind="ExternalInput").ap()
    sT_o = nc.dram_tensor("sT", [D, TCORE], F32, kind="ExternalOutput").ap()
    normT_o = nc.dram_tensor("normT", [D, TCORE], BF16, kind="ExternalOutput").ap()
    combT_o = nc.dram_tensor("combT", [8, TCORE], F32, kind="ExternalOutput").ap()

    st = ExitStack()
    with st:
        sb = lambda n, s, d: st.enter_context(nc.sbuf_tensor(n, s, d)).ap()
        ps = lambda n, s: st.enter_context(nc.psum_tensor(n, s, F32)).ap()
        Wo_bf = sb("Wo_bf", [128, 16 * 1024], BF16)
        ro_f = sb("ro_f", [128, 8 * 8], F32)
        n32 = sb("n32", [128, 8 * 512], F32)
        wtmp = sb("wtmp", [128, 1024], F32)
        y2_bf = sb("y2_bf", [128, 16 * 512], BF16)
        xn = sb("xn", [128, 8 * 512], F32)
        sq = sb("sq", [128, 512], F32)
        sf = sb("sf", [128, 512], F32)
        nf = sb("nf", [128, 512], F32)
        n_bf = sb("n_bf", [128, 8 * 512], BF16)
        gcol = sb("gcol", [128, 8], F32)
        onec = sb("onec", [128, 1], F32)
        oner = sb("oner", [1, 128], F32)
        I_f = sb("I_f", [128, 128], F32)
        ex2r = sb("ex2r", [1, 512], F32)
        lnvr = sb("lnvr", [1, 512], F32)
        rstdr = sb("rstdr", [1, 512], F32)
        probs = sb("probs", [8, 512], F32)
        tpro = sb("tpro", [128, 8], F32)
        m1 = sb("m1", [128, 1], F32)
        eqm = sb("eqm", [128, 8], F32)
        cnt = sb("cnt", [128, 1], F32)
        big = sb("big", [128, 8], F32)
        wo_ = sb("wo_", [128, 8], F32)
        m2 = sb("m2", [128, 1], F32)
        c2 = sb("c2", [128, 1], F32)
        min2 = sb("min2", [128, 1], F32)
        keep = sb("keep", [128, 8], F32)
        den = sb("den", [128, 1], F32)
        rden = sb("rden", [128, 1], F32)
        combt = sb("combt", [128, 8], F32)
        combf = sb("combf", [8, 512], F32)
        cst = sb("cst", [128, 1], F32)
        psS = ps("psS", [128, 512])
        psQ = ps("psQ", [1, 512])
        psR = ps("psR", [128, 512])
        psP = ps("psP", [8, 512])
        psT = ps("psT", [128, 8])
        psCT = ps("psCT", [8, 128])

        p = Prog(nc)
        p.v('memset', cst, 1e-8)
        p.dma(onec, ones128)
        p.dma(oner, onesrow)
        p.dma(I_f, I128)
        for k in range(16):
            p.dma(wtmp, WoutT[k * 128:(k + 1) * 128, :])
            p.s('copy', Wo_bf[:, k * 1024:(k + 1) * 1024], wtmp)
        for k in range(8):
            p.dma(ro_f[:, k * 8:(k + 1) * 8], routT[k * 128:(k + 1) * 128, :])
            p.dma(gcol[:, k:k + 1], gam[k * 128:(k + 1) * 128, :])
        for ci in range(NCH):
            c0 = ci * 512
            for k in range(16):
                p.dma(y2_bf[:, k * 512:(k + 1) * 512], y2T[k * 128:(k + 1) * 128, c0:c0 + 512])
            for m in range(8):
                for k in range(16):
                    p.mm(psS, Wo_bf[:, k * 1024 + m * 128:k * 1024 + (m + 1) * 128],
                         y2_bf[:, k * 512:(k + 1) * 512], start=(k == 0), stop=(k == 15),
                         wait=(k == 0))
                p.s('copy', sf, psS)
                p.dma(sT_o[m * 128:(m + 1) * 128, c0:c0 + 512], sf)
            for k in range(8):
                p.dma(xn[:, k * 512:(k + 1) * 512], xnT[k * 128:(k + 1) * 128, c0:c0 + 512])
            for k in range(8):
                p.s('activation', sq, xn[:, k * 512:(k + 1) * 512], AF.Square)
                p.mm(psQ[0:1, :], onec, sq, start=(k == 0), stop=(k == 7))
            p.s('activation', ex2r[0:1, :], psQ[0:1, :], AF.Copy, scale=1.0 / 1024.0)
            p.s('activation', lnvr[0:1, :], ex2r[0:1, :], AF.Ln, bias=cst[0:1, 0:1])
            p.s('activation', rstdr[0:1, :], lnvr[0:1, :], AF.Exp, scale=-0.5)
            p.mm(psR, oner, rstdr[0:1, :])
            for k in range(8):
                p.v('tensor_tensor', nf, xn[:, k * 512:(k + 1) * 512], psR, OP.mult)
                p.v('tensor_scalar', n32[:, k * 512:(k + 1) * 512], nf, gcol[:, k:k + 1],
                    None, OP.mult)
                p.s('copy', n_bf[:, k * 512:(k + 1) * 512], n32[:, k * 512:(k + 1) * 512])
                p.dma(normT_o[k * 128:(k + 1) * 128, c0:c0 + 512], n_bf[:, k * 512:(k + 1) * 512])
            for k in range(8):
                p.mm(psP[0:8, :], ro_f[:, k * 8:(k + 1) * 8], n32[:, k * 512:(k + 1) * 512],
                     start=(k == 0), stop=(k == 7), wait=(k == 0))
            p.s('activation', probs, psP[0:8, :], AF.Sigmoid)
            for j in range(4):
                p.tr(psT, probs[0:8, j * 128:(j + 1) * 128], I_f[0:8, 0:8])
                p.s('copy', tpro, psT)
                p.v('reduce_max', m1, tpro, axis=AX.X)
                p.v('tensor_scalar', eqm, tpro, m1, None, OP.is_ge)
                p.v('reduce_sum', cnt, eqm, axis=AX.X)
                p.v('tensor_scalar', big, eqm, 1e30, None, OP.mult)
                p.v('tensor_sub', wo_, tpro, big)
                p.v('reduce_max', m2, wo_, axis=AX.X)
                p.v('tensor_scalar', c2, cnt, 1.5, None, OP.is_ge)
                p.v('tensor_sub', min2, m1, m2)
                p.v('tensor_mul', min2, min2, c2)
                p.v('tensor_add', min2, min2, m2)
                p.v('tensor_scalar', keep, tpro, min2, None, OP.is_ge)
                p.v('tensor_add', den, m1, min2)
                p.v('tensor_scalar', den, den, 1e-6, None, OP.add)
                p.v('reciprocal', rden, den)
                p.v('tensor_mul', combt, tpro, keep)
                p.v('tensor_scalar', combt, combt, rden, None, OP.mult)
                p.tr(psCT, combt, I_f)
                p.s('copy', combf[0:8, j * 128:(j + 1) * 128], psCT[0:8, :])
            p.dma(combT_o[0:8, c0:c0 + 512], combf)
        p.finish()
    return nc


def build_KD1():
    nc = bass.Bass("TRN2", target_bir_lowering=False)
    normT = nc.dram_tensor("normT", [D, TCORE], BF16, kind="ExternalInput").ap()
    sw1 = nc.dram_tensor("sw1", [D, 4096], F32, kind="ExternalInput").ap()
    sw2 = nc.dram_tensor("sw2", [4096, D], F32, kind="ExternalInput").ap()
    sb1 = nc.dram_tensor("sb1", [4096, 1], F32, kind="ExternalInput").ap()
    sb2 = nc.dram_tensor("sb2", [D, 1], F32, kind="ExternalInput").ap()
    sharedT_o = nc.dram_tensor("sharedT", [D, TCORE], F32, kind="ExternalOutput").ap()

    st = ExitStack()
    with st:
        sb = lambda n, s, d: st.enter_context(nc.sbuf_tensor(n, s, d)).ap()
        ps = lambda n, s: st.enter_context(nc.psum_tensor(n, s, F32)).ap()
        w1_bf = sb("w1_bf", [128, 8 * 4096], BF16)
        w2_bf = sb("w2_bf", [128, 32 * 1024], BF16)
        wtmp = sb("wtmp", [128, 1024], F32)
        nb = sb("nb", [128, 8 * 512], BF16)
        he = sb("he", [128, 32 * 512], BF16)
        b1c = sb("b1c", [128, 32], F32)
        b2c = sb("b2c", [128, 8], F32)
        outf = sb("outf", [128, 512], F32)
        psH = ps("psH", [128, 512])
        psO = ps("psO", [128, 512])

        p = Prog(nc)
        for k in range(8):
            for q in range(4):
                p.dma(wtmp, sw1[k * 128:(k + 1) * 128, q * 1024:(q + 1) * 1024])
                p.s('copy', w1_bf[:, k * 4096 + q * 1024:k * 4096 + (q + 1) * 1024], wtmp)
        for k in range(32):
            p.dma(wtmp, sw2[k * 128:(k + 1) * 128, :])
            p.s('copy', w2_bf[:, k * 1024:(k + 1) * 1024], wtmp)
        for m in range(32):
            p.dma(b1c[:, m:m + 1], sb1[m * 128:(m + 1) * 128, :])
        for m in range(8):
            p.dma(b2c[:, m:m + 1], sb2[m * 128:(m + 1) * 128, :])
        for ci in range(NCH):
            c0 = ci * 512
            for k in range(8):
                p.dma(nb[:, k * 512:(k + 1) * 512], normT[k * 128:(k + 1) * 128, c0:c0 + 512])
            for m in range(32):
                for k in range(8):
                    p.mm(psH, w1_bf[:, k * 4096 + m * 128:k * 4096 + (m + 1) * 128],
                         nb[:, k * 512:(k + 1) * 512], start=(k == 0), stop=(k == 7),
                         wait=(k == 0))
                p.s('activation', he[:, m * 512:(m + 1) * 512], psH, AF.Relu,
                    bias=b1c[:, m:m + 1])
            for m in range(8):
                for k in range(32):
                    p.mm(psO, w2_bf[:, k * 1024 + m * 128:k * 1024 + (m + 1) * 128],
                         he[:, k * 512:(k + 1) * 512], start=(k == 0), stop=(k == 31),
                         wait=(k == 0))
                p.v('tensor_scalar', outf, psO, b2c[:, m:m + 1], None, OP.add)
                p.dma(sharedT_o[m * 128:(m + 1) * 128, c0:c0 + 512], outf)
        p.finish()
    return nc


def build_KD2():
    TH = 8192
    nc = bass.Bass("TRN2", target_bir_lowering=False)
    normT = nc.dram_tensor("normT", [D, TH], BF16, kind="ExternalInput").ap()
    ew1 = nc.dram_tensor("ew1", [D, 4096], F32, kind="ExternalInput").ap()
    ew2 = nc.dram_tensor("ew2", [4096, D], F32, kind="ExternalInput").ap()
    eb1 = nc.dram_tensor("eb1", [4096, 1], F32, kind="ExternalInput").ap()
    eb2 = nc.dram_tensor("eb2", [D, 1], F32, kind="ExternalInput").ap()
    combr = nc.dram_tensor("combr", [1, TH], F32, kind="ExternalInput").ap()
    onesrow = nc.dram_tensor("onesrow", [1, 128], F32, kind="ExternalInput").ap()
    routT_o = nc.dram_tensor("routT", [D, TH], F32, kind="ExternalOutput").ap()

    st = ExitStack()
    with st:
        sb = lambda n, s, d: st.enter_context(nc.sbuf_tensor(n, s, d)).ap()
        ps = lambda n, s: st.enter_context(nc.psum_tensor(n, s, F32)).ap()
        w1_bf = sb("w1_bf", [128, 8 * 4096], BF16)
        w2_bf = sb("w2_bf", [128, 32 * 1024], BF16)
        wtmp = sb("wtmp", [128, 1024], F32)
        nb = sb("nb", [128, 8 * 512], BF16)
        he = sb("he", [128, 32 * 512], BF16)
        b1c = sb("b1c", [128, 32], F32)
        b2c = sb("b2c", [128, 8], F32)
        crow = sb("crow", [1, 512], F32)
        gbc = sb("gbc", [128, 512], F32)
        outf = sb("outf", [128, 512], F32)
        oner = sb("oner", [1, 128], F32)
        psH = ps("psH", [128, 512])
        psO = ps("psO", [128, 512])
        psG = ps("psG", [128, 512])

        p = Prog(nc)
        p.dma(oner, onesrow)
        for k in range(8):
            for q in range(4):
                p.dma(wtmp, ew1[k * 128:(k + 1) * 128, q * 1024:(q + 1) * 1024])
                p.s('copy', w1_bf[:, k * 4096 + q * 1024:k * 4096 + (q + 1) * 1024], wtmp)
        for k in range(32):
            p.dma(wtmp, ew2[k * 128:(k + 1) * 128, :])
            p.s('copy', w2_bf[:, k * 1024:(k + 1) * 1024], wtmp)
        for m in range(32):
            p.dma(b1c[:, m:m + 1], eb1[m * 128:(m + 1) * 128, :])
        for m in range(8):
            p.dma(b2c[:, m:m + 1], eb2[m * 128:(m + 1) * 128, :])
        for ci in range(TH // 512):
            c0 = ci * 512
            for k in range(8):
                p.dma(nb[:, k * 512:(k + 1) * 512], normT[k * 128:(k + 1) * 128, c0:c0 + 512])
            p.dma(crow, combr[0:1, c0:c0 + 512])
            p.mm(psG, oner, crow)
            p.s('copy', gbc, psG)
            for m in range(32):
                for k in range(8):
                    p.mm(psH, w1_bf[:, k * 4096 + m * 128:k * 4096 + (m + 1) * 128],
                         nb[:, k * 512:(k + 1) * 512], start=(k == 0), stop=(k == 7),
                         wait=(k == 0))
                p.s('activation', he[:, m * 512:(m + 1) * 512], psH, AF.Relu,
                    bias=b1c[:, m:m + 1])
            for m in range(8):
                for k in range(32):
                    p.mm(psO, w2_bf[:, k * 1024 + m * 128:k * 1024 + (m + 1) * 128],
                         he[:, k * 512:(k + 1) * 512], start=(k == 0), stop=(k == 31),
                         wait=(k == 0))
                p.v('tensor_scalar', outf, psO, b2c[:, m:m + 1], None, OP.add)
                p.v('tensor_mul', outf, outf, gbc)
                p.dma(routT_o[m * 128:(m + 1) * 128, c0:c0 + 512], outf)
        p.finish()
    return nc


def build_KDsum():
    nc = bass.Bass("TRN2", target_bir_lowering=False)
    rins = [nc.dram_tensor(f"r{e}", [D, TCORE], F32, kind="ExternalInput").ap() for e in range(8)]
    sharedT = nc.dram_tensor("sharedT", [D, TCORE], F32, kind="ExternalInput").ap()
    sT = nc.dram_tensor("sT", [D, TCORE], F32, kind="ExternalInput").ap()
    x0T = nc.dram_tensor("x0T", [D, TCORE], F32, kind="ExternalInput").ap()
    x1T_o = nc.dram_tensor("x1T", [D, TCORE], F32, kind="ExternalOutput").ap()

    st = ExitStack()
    with st:
        sb = lambda n, s, d: st.enter_context(nc.sbuf_tensor(n, s, d)).ap()
        acc = sb("acc", [128, 512], F32)
        t2 = sb("t2", [128, 512], F32)
        g = sb("g", [128, 512], F32)
        p = Prog(nc)
        for ci in range(NCH):
            c0 = ci * 512
            for m in range(8):
                rs = (m * 128, (m + 1) * 128)
                p.dma(acc, sharedT[rs[0]:rs[1], c0:c0 + 512])
                for e in range(8):
                    p.dma(t2, rins[e][rs[0]:rs[1], c0:c0 + 512])
                    p.v('tensor_add', acc, acc, t2)
                p.s('activation', g, acc, AF.Silu)
                p.dma(t2, sT[rs[0]:rs[1], c0:c0 + 512])
                p.v('tensor_mul', g, g, t2)
                p.dma(t2, x0T[rs[0]:rs[1], c0:c0 + 512])
                p.v('tensor_add', g, g, t2)
                p.dma(x1T_o[rs[0]:rs[1], c0:c0 + 512], g)
        p.finish()
    return nc


def build_KE():
    nc = bass.Bass("TRN2", target_bir_lowering=False)
    x1T = nc.dram_tensor("x1T", [D, TCORE], F32, kind="ExternalInput").ap()
    Wqkv = nc.dram_tensor("Wqkv", [D, 3072], F32, kind="ExternalInput").ap()
    cosB = nc.dram_tensor("cosB", [128, TCORE], F32, kind="ExternalInput").ap()
    sinB = nc.dram_tensor("sinB", [128, TCORE], F32, kind="ExternalInput").ap()
    I128b = nc.dram_tensor("I128b", [128, 128], BF16, kind="ExternalInput").ap()
    ones128 = nc.dram_tensor("ones128", [128, 1], F32, kind="ExternalInput").ap()
    onesrow = nc.dram_tensor("onesrow", [1, 128], F32, kind="ExternalInput").ap()
    qT_o = nc.dram_tensor("qT", [D, TCORE], BF16, kind="ExternalOutput").ap()
    kT_o = nc.dram_tensor("kT", [D, TCORE], BF16, kind="ExternalOutput").ap()
    vR_o = nc.dram_tensor("vR", [TCORE, D], BF16, kind="ExternalOutput").ap()
    xn1T_o = nc.dram_tensor("xn1T", [D, TCORE], BF16, kind="ExternalOutput").ap()
    pool_o = nc.dram_tensor("pool", [128, 8], F32, kind="ExternalOutput").ap()

    st = ExitStack()
    with st:
        sb = lambda n, s, d: st.enter_context(nc.sbuf_tensor(n, s, d)).ap()
        ps = lambda n, s, d=F32: st.enter_context(nc.psum_tensor(n, s, d)).ap()
        Wq_bf = sb("Wq_bf", [128, 8 * 3072], BF16)
        wtmp = sb("wtmp", [128, 1024], F32)
        xin = sb("xin", [128, 8 * 512], F32)
        sq = sb("sq", [128, 512], F32)
        xnf = sb("xnf", [128, 512], F32)
        xn_bf = sb("xn_bf", [128, 8 * 512], BF16)
        cosb = sb("cosb", [128, 2048], F32)
        sinb = sb("sinb", [128, 2048], F32)
        qf = sb("qf", [128, 512], F32)
        qsw = sb("qsw", [128, 512], F32)
        t1_ = sb("t1_", [128, 512], F32)
        t2_ = sb("t2_", [128, 512], F32)
        qbf = sb("qbf", [128, 512], BF16)
        vbf = sb("vbf", [128, 512], BF16)
        vrow = sb("vrow", [128, 128], BF16)
        poolacc = sb("poolacc", [128, 8], F32)
        pcol = sb("pcol", [128, 1], F32)
        onec = sb("onec", [128, 1], F32)
        oner = sb("oner", [1, 128], F32)
        I_b = sb("I_b", [128, 128], BF16)
        meanr = sb("meanr", [1, 512], F32)
        ex2r = sb("ex2r", [1, 512], F32)
        msqr = sb("msqr", [1, 512], F32)
        varr = sb("varr", [1, 512], F32)
        lnvr = sb("lnvr", [1, 512], F32)
        rstdr = sb("rstdr", [1, 512], F32)
        cst = sb("cst", [128, 1], F32)
        psS = ps("psS", [1, 512])
        psQ = ps("psQ", [1, 512])
        psM = ps("psM", [128, 512])
        psR = ps("psR", [128, 512])
        psX = ps("psX", [128, 512])
        psTb = ps("psTb", [128, 128], BF16)

        p = Prog(nc)
        p.v('memset', cst, 1e-5)
        p.dma(onec, ones128)
        p.dma(oner, onesrow)
        p.dma(I_b, I128b)
        p.dma(cosb, cosB)
        p.dma(sinb, sinB)
        p.v('memset', poolacc, 0.0)
        for k in range(8):
            for q in range(3):
                p.dma(wtmp, Wqkv[k * 128:(k + 1) * 128, q * 1024:(q + 1) * 1024])
                p.s('copy', Wq_bf[:, k * 3072 + q * 1024:k * 3072 + (q + 1) * 1024], wtmp)
        for ci in range(NCH):
            c0 = ci * 512
            for k in range(8):
                p.dma(xin[:, k * 512:(k + 1) * 512], x1T[k * 128:(k + 1) * 128, c0:c0 + 512])
            for k in range(8):
                p.mm(psS[0:1, :], onec, xin[:, k * 512:(k + 1) * 512], start=(k == 0),
                     stop=(k == 7), wait=(k == 0))
            for k in range(8):
                p.s('activation', sq, xin[:, k * 512:(k + 1) * 512], AF.Square)
                p.mm(psQ[0:1, :], onec, sq, start=(k == 0), stop=(k == 7))
            p.s('activation', meanr[0:1, :], psS[0:1, :], AF.Copy, scale=1.0 / 1024.0)
            p.s('activation', ex2r[0:1, :], psQ[0:1, :], AF.Copy, scale=1.0 / 1024.0)
            p.v('tensor_mul', msqr[0:1, :], meanr[0:1, :], meanr[0:1, :])
            p.v('tensor_sub', varr[0:1, :], ex2r[0:1, :], msqr[0:1, :])
            p.s('activation', lnvr[0:1, :], varr[0:1, :], AF.Ln, bias=cst[0:1, 0:1])
            p.s('activation', rstdr[0:1, :], lnvr[0:1, :], AF.Exp, scale=-0.5)
            p.mm(psM, oner, meanr[0:1, :])
            p.mm(psR, oner, rstdr[0:1, :])
            for k in range(8):
                p.v('tensor_sub', xnf, xin[:, k * 512:(k + 1) * 512], psM)
                p.v('tensor_mul', xnf, xnf, psR)
                p.v('reduce_sum', pcol, xnf, axis=AX.X)
                p.v('tensor_add', poolacc[:, k:k + 1], poolacc[:, k:k + 1], pcol)
                p.s('copy', xn_bf[:, k * 512:(k + 1) * 512], xnf)
                p.dma(xn1T_o[k * 128:(k + 1) * 128, c0:c0 + 512], xn_bf[:, k * 512:(k + 1) * 512])
            for m in range(24):
                for k in range(8):
                    p.mm(psX, Wq_bf[:, k * 3072 + m * 128:k * 3072 + (m + 1) * 128],
                         xn_bf[:, k * 512:(k + 1) * 512], start=(k == 0), stop=(k == 7),
                         wait=(k == 0))
                if m < 16:
                    p.s('copy', qf, psX)
                    for g4 in range(4):
                        p.dma(qsw[((g4 ^ 1) * 32):((g4 ^ 1) * 32 + 32), :],
                              qf[g4 * 32:g4 * 32 + 32, :])
                    p.v('tensor_mul', t1_, qf, cosb[:, c0:c0 + 512])
                    p.v('tensor_mul', t2_, qsw, sinb[:, c0:c0 + 512])
                    p.v('tensor_add', t1_, t1_, t2_)
                    p.s('copy', qbf, t1_)
                    dst = qT_o if m < 8 else kT_o
                    mm = m if m < 8 else m - 8
                    p.dma(dst[mm * 128:(mm + 1) * 128, c0:c0 + 512], qbf)
                else:
                    p.s('copy', vbf, psX)
                    for j in range(4):
                        p.tr(psTb, vbf[:, j * 128:(j + 1) * 128], I_b)
                        p.s('copy', vrow, psTb)
                        p.dma(vR_o[c0 + j * 128:c0 + (j + 1) * 128,
                                   (m - 16) * 128:(m - 15) * 128], vrow)
        p.dma(pool_o, poolacc)
        p.finish()
    return nc


def build_KF():
    T = 8192
    nc = bass.Bass("TRN2", target_bir_lowering=False)
    qT = nc.dram_tensor("qT", [256, T], BF16, kind="ExternalInput").ap()
    kT = nc.dram_tensor("kT", [256, T], BF16, kind="ExternalInput").ap()
    vR = nc.dram_tensor("vR", [T, 256], BF16, kind="ExternalInput").ap()
    TRI = nc.dram_tensor("TRI", [128, 4 * 512], F32, kind="ExternalInput").ap()
    ones128 = nc.dram_tensor("ones128", [128, 1], F32, kind="ExternalInput").ap()
    onesrow = nc.dram_tensor("onesrow", [1, 128], F32, kind="ExternalInput").ap()
    aoT_o = nc.dram_tensor("aoT", [256, T], BF16, kind="ExternalOutput").ap()

    st = ExitStack()
    with st:
        sb = lambda n, s, d: st.enter_context(nc.sbuf_tensor(n, s, d)).ap()
        ps = lambda n, s: st.enter_context(nc.psum_tensor(n, s, F32)).ap()
        qfull = sb("qfull", [64, T], BF16)
        kfull = sb("kfull", [64, T], BF16)
        Vt = sb("Vt", [128, 64 * 64], BF16)
        tri = sb("tri", [128, 4 * 512], F32)
        ptef = sb("ptef", [128, 512], F32)
        ptbf = sb("ptbf", [128, 512], BF16)
        rs_ = sb("rs_", [1, 512], F32)
        rrs = sb("rrs", [1, 512], F32)
        of_ = sb("of_", [64, 512], F32)
        aob = sb("aob", [64, 512], BF16)
        onec = sb("onec", [128, 1], F32)
        oner = sb("oner", [1, 128], F32)
        psST = ps("psST", [128, 512])
        psRS = ps("psRS", [1, 512])
        psRR = ps("psRR", [128, 512])
        psO = ps("psO", [64, 512])

        p = Prog(nc)
        p.dma(onec, ones128)
        p.dma(oner, onesrow)
        p.dma(tri, TRI)
        for u in range(4):
            p.dma(qfull, qT[u * 64:(u + 1) * 64, :])
            p.dma(kfull, kT[u * 64:(u + 1) * 64, :])
            for t in range(64):
                p.dma(Vt[:, t * 64:(t + 1) * 64],
                      vR[t * 128:(t + 1) * 128, u * 64:(u + 1) * 64])
            for Q in range(16):
                nk = 4 * (Q + 1)
                for kt in range(nk):
                    p.mm(psST, kfull[:, kt * 128:(kt + 1) * 128],
                         qfull[:, Q * 512:(Q + 1) * 512])
                    if kt >= 4 * Q:
                        j = kt - 4 * Q
                        p.v('tensor_tensor', psST, psST, tri[:, j * 512:(j + 1) * 512], OP.add)
                    p.s('activation', ptef, psST, AF.Exp, scale=0.125)
                    p.mm(psRS[0:1, :], onec, ptef, start=(kt == 0), stop=(kt == nk - 1))
                    p.s('copy', ptbf, ptef)
                    p.mm(psO, Vt[:, kt * 64:(kt + 1) * 64], ptbf, start=(kt == 0),
                         stop=(kt == nk - 1))
                p.s('copy', rs_[0:1, :], psRS[0:1, :])
                p.v('reciprocal', rrs[0:1, :], rs_[0:1, :])
                p.mm(psRR, oner, rrs[0:1, :])
                p.s('copy', of_, psO[0:64, :])
                p.v('tensor_tensor', of_, of_, psRR[0:64, :], OP.mult)
                p.s('copy', aob, of_)
                p.dma(aoT_o[u * 64:(u + 1) * 64, Q * 512:(Q + 1) * 512], aob)
        p.finish()
    return nc


def build_KG(b_core):
    nc = bass.Bass("TRN2", target_bir_lowering=False)
    aoT = nc.dram_tensor("aoT", [D, TCORE], BF16, kind="ExternalInput").ap()
    WoA = nc.dram_tensor("WoA", [D, D], F32, kind="ExternalInput").ap()
    poolP = nc.dram_tensor("poolP", [8, D], F32, kind="ExternalInput").ap()
    gwT = nc.dram_tensor("gwT", [D, 8], F32, kind="ExternalInput").ap()
    gb = nc.dram_tensor("gb", [8, 1], F32, kind="ExternalInput").ap()
    expT = nc.dram_tensor("expT", [8 * D, D], F32, kind="ExternalInput").ap()
    ebT = nc.dram_tensor("ebT", [D, 8], F32, kind="ExternalInput").ap()
    xn1T = nc.dram_tensor("xn1T", [D, TCORE], BF16, kind="ExternalInput").ap()
    x1T = nc.dram_tensor("x1T", [D, TCORE], F32, kind="ExternalInput").ap()
    I128 = nc.dram_tensor("I128", [128, 128], F32, kind="ExternalInput").ap()
    onesrow = nc.dram_tensor("onesrow", [1, 128], F32, kind="ExternalInput").ap()
    outT_o = nc.dram_tensor("outT", [D, TCORE], F32, kind="ExternalOutput").ap()

    st = ExitStack()
    with st:
        sb = lambda n, s, d: st.enter_context(nc.sbuf_tensor(n, s, d)).ap()
        ps = lambda n, s: st.enter_context(nc.psum_tensor(n, s, F32)).ap()
        Wo_bf = sb("Wo_bf", [128, 8 * 1024], BF16)
        ew_bf = sb("ew_bf", [128, 8 * 1024], BF16)
        wtmp = sb("wtmp", [128, 1024], F32)
        gwc = sb("gwc", [128, 8 * 8], F32)
        ebc = sb("ebc", [128, 8 * 8], F32)
        gbc = sb("gbc", [8, 1], F32)
        pr = sb("pr", [8, 1024], F32)
        ptj = sb("ptj", [128, 8], F32)
        pool2 = sb("pool2", [128, 16], F32)
        lgf = sb("lgf", [8, 2], F32)
        lg2 = sb("lg2", [2, 8], F32)
        m1 = sb("m1", [2, 1], F32)
        mneg = sb("mneg", [2, 1], F32)
        eqm = sb("eqm", [2, 8], F32)
        cnt = sb("cnt", [2, 1], F32)
        big = sb("big", [2, 8], F32)
        wo_ = sb("wo_", [2, 8], F32)
        m2 = sb("m2", [2, 1], F32)
        c2 = sb("c2", [2, 1], F32)
        min2 = sb("min2", [2, 1], F32)
        keep = sb("keep", [2, 8], F32)
        ex = sb("ex", [2, 8], F32)
        den = sb("den", [2, 1], F32)
        rden = sb("rden", [2, 1], F32)
        gat = sb("gat", [2, 8], F32)
        gT8 = sb("gT8", [8, 2], F32)
        grow = sb("grow", [1, 16], F32)
        gcols = sb("gcols", [128, 8], F32)
        xnb = sb("xnb", [128, 8 * 512], BF16)
        aob = sb("aob", [128, 8 * 512], BF16)
        s1f = sb("s1f", [128, 8 * 512], F32)
        accf = sb("accf", [128, 8 * 512], F32)
        eo = sb("eo", [128, 512], F32)
        g1 = sb("g1", [128, 512], F32)
        x1f = sb("x1f", [128, 512], F32)
        I_f = sb("I_f", [128, 128], F32)
        oner = sb("oner", [1, 128], F32)
        psX = ps("psX", [128, 512])
        psPT = ps("psPT", [128, 8])
        psL = ps("psL", [8, 2])
        psLT = ps("psLT", [2, 8])
        psGT = ps("psGT", [8, 2])
        psGC = ps("psGC", [128, 1])

        p = Prog(nc)
        p.dma(oner, onesrow)
        p.dma(I_f, I128)
        p.dma(gbc, gb)
        for k in range(8):
            p.dma(wtmp, WoA[k * 128:(k + 1) * 128, :])
            p.s('copy', Wo_bf[:, k * 1024:(k + 1) * 1024], wtmp)
            p.dma(gwc[:, k * 8:(k + 1) * 8], gwT[k * 128:(k + 1) * 128, :])
            p.dma(ebc[:, k * 8:(k + 1) * 8], ebT[k * 128:(k + 1) * 128, :])
        p.dma(pr, poolP)
        for j in range(8):
            p.tr(psPT, pr[0:8, j * 128:(j + 1) * 128], I_f[0:8, 0:8])
            p.s('copy', ptj, psPT)
            p.v('reduce_sum', pool2[:, j * 2:j * 2 + 1], ptj[:, 0:4], axis=AX.X)
            p.v('reduce_sum', pool2[:, j * 2 + 1:j * 2 + 2], ptj[:, 4:8], axis=AX.X)
        p.v('tensor_scalar', pool2, pool2, 1.0 / 8192.0, None, OP.mult)
        for j in range(8):
            p.mm(psL[0:8, 0:2], gwc[:, j * 8:(j + 1) * 8], pool2[:, j * 2:(j + 1) * 2],
                 start=(j == 0), stop=(j == 7), wait=(j == 0))
        p.v('tensor_scalar', lgf, psL[0:8, 0:2], gbc[0:8, 0:1], None, OP.add)
        p.tr(psLT, lgf[0:8, 0:2], I_f[0:8, 0:8])
        p.s('copy', lg2, psLT[0:2, 0:8])
        p.v('reduce_max', m1, lg2, axis=AX.X)
        p.v('tensor_scalar', mneg, m1, -1.0, None, OP.mult)
        p.v('tensor_scalar', eqm, lg2, m1, None, OP.is_ge)
        p.v('reduce_sum', cnt, eqm, axis=AX.X)
        p.v('tensor_scalar', big, eqm, 1e30, None, OP.mult)
        p.v('tensor_sub', wo_, lg2, big)
        p.v('reduce_max', m2, wo_, axis=AX.X)
        p.v('tensor_scalar', c2, cnt, 1.5, None, OP.is_ge)
        p.v('tensor_sub', min2, m1, m2)
        p.v('tensor_mul', min2, min2, c2)
        p.v('tensor_add', min2, min2, m2)
        p.v('tensor_scalar', keep, lg2, min2, None, OP.is_ge)
        p.s('activation', ex, lg2, AF.Exp, bias=mneg)
        p.v('tensor_mul', ex, ex, keep)
        p.v('reduce_sum', den, ex, axis=AX.X)
        p.v('reciprocal', rden, den)
        p.v('tensor_scalar', gat, ex, rden, None, OP.mult)
        p.tr(psGT, gat[0:2, 0:8], I_f[0:2, 0:2])
        p.s('copy', gT8, psGT[0:8, 0:2])
        for e in range(8):
            p.dma(grow[0:1, e * 2:(e + 1) * 2], gT8[e:e + 1, 0:2])
        for e in range(8):
            p.mm(psGC, oner, grow[0:1, e * 2 + b_core:e * 2 + b_core + 1])
            p.s('copy', gcols[:, e:e + 1], psGC)
        for ci in range(NCH):
            c0 = ci * 512
            for k in range(8):
                p.dma(aob[:, k * 512:(k + 1) * 512], aoT[k * 128:(k + 1) * 128, c0:c0 + 512])
                p.dma(xnb[:, k * 512:(k + 1) * 512], xn1T[k * 128:(k + 1) * 128, c0:c0 + 512])
            for m in range(8):
                for k in range(8):
                    p.mm(psX, Wo_bf[:, k * 1024 + m * 128:k * 1024 + (m + 1) * 128],
                         aob[:, k * 512:(k + 1) * 512], start=(k == 0), stop=(k == 7),
                         wait=(k == 0))
                p.s('copy', s1f[:, m * 512:(m + 1) * 512], psX)
            for e in range(8):
                for k in range(8):
                    p.dma(wtmp, expT[e * 1024 + k * 128:e * 1024 + (k + 1) * 128, :])
                    p.s('copy', ew_bf[:, k * 1024:(k + 1) * 1024], wtmp)
                for m in range(8):
                    for k in range(8):
                        p.mm(psX, ew_bf[:, k * 1024 + m * 128:k * 1024 + (m + 1) * 128],
                             xnb[:, k * 512:(k + 1) * 512], start=(k == 0), stop=(k == 7),
                             wait=(k == 0))
                    p.s('activation', eo, psX, AF.Gelu, bias=ebc[:, m * 8 + e:m * 8 + e + 1])
                    if e == 0:
                        p.v('tensor_scalar', accf[:, m * 512:(m + 1) * 512], eo,
                            gcols[:, 0:1], None, OP.mult)
                    else:
                        p.v('scalar_tensor_tensor', accf[:, m * 512:(m + 1) * 512], eo,
                            gcols[:, e:e + 1], accf[:, m * 512:(m + 1) * 512],
                            OP.mult, OP.add)
            for m in range(8):
                p.s('activation', g1, accf[:, m * 512:(m + 1) * 512], AF.Silu)
                p.v('tensor_mul', g1, g1, s1f[:, m * 512:(m + 1) * 512])
                p.dma(x1f, x1T[m * 128:(m + 1) * 128, c0:c0 + 512])
                p.v('tensor_add', g1, g1, x1f)
                p.dma(outT_o[m * 128:(m + 1) * 128, c0:c0 + 512], g1)
        p.finish()
    return nc


def _f(a):
    return np.ascontiguousarray(a, dtype=np.float32)


def kernel(**inp):
    x = _f(inp['L0_attn_x'])
    xT = np.ascontiguousarray(x.reshape(NT, D).T)
    ones128 = np.ones((128, 1), f32)
    onesrow = np.ones((1, 128), f32)
    I128 = np.eye(128, dtype=f32)
    I128b = np.eye(128, dtype=BF)

    W_inT = np.ascontiguousarray(_f(inp['L0_attn_W_in']).T)
    W_xT = np.ascontiguousarray(_f(inp['L0_attn_W_x']).T)
    W_dtT = np.ascontiguousarray(_f(inp['L0_attn_W_dt']).T)
    convw = np.ascontiguousarray(_f(inp['L0_attn_conv_w'])[:, 0, :])
    convb = _f(inp['L0_attn_conv_b']).reshape(DI, 1)
    bdt = _f(inp['L0_attn_b_dt']).reshape(DI, 1)
    nc = build_KA()
    maps = []
    for c in range(NCORE):
        xin = np.zeros((D, TCORE + 3), f32)
        xin[:, 3:] = xT[:, c * TCORE:(c + 1) * TCORE]
        if c not in (0, 4):
            xin[:, :3] = xT[:, c * TCORE - 3:c * TCORE]
        maps.append(dict(xin=xin, WinT=W_inT, WxT=W_xT, WdtT=W_dtT, convw=convw,
                         convb=convb, bdt=bdt, ones128=ones128, onesrow=onesrow))
    rA = _run(nc, maps)
    xnT = np.concatenate([r['xnT'] for r in rA], axis=1)
    xcT = np.concatenate([r['xcT'] for r in rA], axis=1)
    dtT = np.concatenate([r['dtT'] for r in rA], axis=1)
    resT = np.concatenate([r['resT'] for r in rA], axis=1)
    BTf = np.concatenate([r['BT'] for r in rA], axis=1)
    CTf = np.concatenate([r['CT'] for r in rA], axis=1)

    At = np.ascontiguousarray(np.tile(_f(inp['L0_attn_A_log']).reshape(1, 16), (128, 1)))
    Dvf = _f(inp['L0_attn_D']).reshape(DI, 1)
    nc = build_KB()
    maps = []
    for c in range(NCORE):
        r0 = c * 256
        maps.append(dict(dtT=np.ascontiguousarray(dtT[r0:r0 + 256]),
                         xcT=np.ascontiguousarray(xcT[r0:r0 + 256]),
                         resT=np.ascontiguousarray(resT[r0:r0 + 256]),
                         BT=BTf, CT=CTf, At=At,
                         Dv=np.ascontiguousarray(Dvf[r0:r0 + 256]),
                         onesrow=onesrow))
    rB = _run(nc, maps)
    y2T = np.concatenate([r['y2T'] for r in rB], axis=0)

    W_outT = np.ascontiguousarray(_f(inp['L0_attn_W_out']).T)
    gam = _f(inp['L0_ffn_norm_weight']).reshape(D, 1)
    routT = np.ascontiguousarray(_f(inp['L0_ffn_router_w']).T)
    nc = build_KC()
    maps = []
    for c in range(NCORE):
        cs = slice(c * TCORE, (c + 1) * TCORE)
        maps.append(dict(y2T=np.ascontiguousarray(y2T[:, cs]), WoutT=W_outT,
                         xnT=np.ascontiguousarray(xnT[:, cs]), gam=gam, routT=routT,
                         I128=I128, ones128=ones128, onesrow=onesrow))
    rC = _run(nc, maps)
    sT = [r['sT'] for r in rC]
    normT = [r['normT'] for r in rC]
    combT = np.concatenate([r['combT'] for r in rC], axis=1)

    nc = build_KD1()
    maps = [dict(normT=normT[c], sw1=_f(inp['L0_ffn_sw1']), sw2=_f(inp['L0_ffn_sw2']),
                 sb1=_f(inp['L0_ffn_sb1']).reshape(4096, 1),
                 sb2=_f(inp['L0_ffn_sb2']).reshape(D, 1)) for c in range(NCORE)]
    rD1 = _run(nc, maps)
    sharedT = [r['sharedT'] for r in rD1]

    ew1 = _f(inp['L0_ffn_ew1'])
    ew2 = _f(inp['L0_ffn_ew2'])
    eb1 = _f(inp['L0_ffn_eb1'])
    eb2 = _f(inp['L0_ffn_eb2'])
    normhalf = [np.concatenate([normT[c] for c in range(4)], axis=1),
                np.concatenate([normT[c] for c in range(4, 8)], axis=1)]
    routhalf = []
    nc = build_KD2()
    for half in range(2):
        cs = slice(half * 8192, (half + 1) * 8192)
        maps = []
        for e in range(NCORE):
            maps.append(dict(normT=normhalf[half], ew1=np.ascontiguousarray(ew1[e]),
                             ew2=np.ascontiguousarray(ew2[e]),
                             eb1=np.ascontiguousarray(eb1[e]).reshape(4096, 1),
                             eb2=np.ascontiguousarray(eb2[e]).reshape(D, 1),
                             combr=np.ascontiguousarray(combT[e:e + 1, cs]),
                             onesrow=onesrow))
        rD2 = _run(nc, maps)
        routhalf.append([r['routT'] for r in rD2])

    nc = build_KDsum()
    maps = []
    for c in range(NCORE):
        half = c // 4
        lo = (c % 4) * TCORE
        m = {f'r{e}': np.ascontiguousarray(routhalf[half][e][:, lo:lo + TCORE])
             for e in range(8)}
        m['sharedT'] = sharedT[c]
        m['sT'] = sT[c]
        m['x0T'] = np.ascontiguousarray(xT[:, c * TCORE:(c + 1) * TCORE])
        maps.append(m)
    rDs = _run(nc, maps)
    x1T = [r['x1T'] for r in rDs]

    DH = 64
    inv = 1.0 / (10000.0 ** (np.arange(0, DH, 2, dtype=np.float64) / DH))
    fr = np.arange(8192, dtype=np.float64)[:, None] * inv[None, :]
    cosT = np.cos(fr).T.astype(f32)
    sinT = np.sin(fr).T.astype(f32)
    cosB = np.ascontiguousarray(np.tile(cosT, (4, 1)))
    sinB = np.ascontiguousarray(np.concatenate([-sinT, sinT, -sinT, sinT], axis=0))
    nc = build_KE()
    maps = []
    for c in range(NCORE):
        t0 = (c * TCORE) % 8192
        maps.append(dict(x1T=x1T[c], Wqkv=_f(inp['L1_attn_Wqkv']),
                         cosB=np.ascontiguousarray(cosB[:, t0:t0 + TCORE]),
                         sinB=np.ascontiguousarray(sinB[:, t0:t0 + TCORE]),
                         I128b=I128b, ones128=ones128, onesrow=onesrow))
    rE = _run(nc, maps)
    qTf = np.concatenate([r['qT'] for r in rE], axis=1)
    kTf = np.concatenate([r['kT'] for r in rE], axis=1)
    vRf = np.concatenate([r['vR'] for r in rE], axis=0)
    xn1T = [r['xn1T'] for r in rE]
    poolP = np.stack([np.ascontiguousarray(r['pool'].T).reshape(D) for r in rE], axis=0)

    TRI = np.zeros((128, 4 * 512), f32)
    for j in range(4):
        blk = np.where(np.arange(128)[:, None] > (j * 128 + np.arange(512))[None, :] - j * 0,
                       0.0, 0.0)
        r = np.arange(128)[:, None]
        qq = np.arange(512)[None, :]
        mask = (r + j * 128) > qq
        TRI[:, j * 512:(j + 1) * 512] = np.where(mask, -1e30, 0.0)
    nc = build_KF()
    maps = []
    for c in range(NCORE):
        qs, ks, vs = [], [], []
        for u in range(4):
            pair = c * 4 + u
            b, h = pair // 16, pair % 16
            qs.append(qTf[h * 64:(h + 1) * 64, b * 8192:(b + 1) * 8192])
            ks.append(kTf[h * 64:(h + 1) * 64, b * 8192:(b + 1) * 8192])
            vs.append(vRf[b * 8192:(b + 1) * 8192, h * 64:(h + 1) * 64])
        maps.append(dict(qT=np.ascontiguousarray(np.concatenate(qs, axis=0)),
                         kT=np.ascontiguousarray(np.concatenate(ks, axis=0)),
                         vR=np.ascontiguousarray(np.concatenate(vs, axis=1)),
                         TRI=TRI, ones128=ones128, onesrow=onesrow))
    rF = _run(nc, maps)
    aoTf = np.zeros((D, NT), BF)
    for c in range(NCORE):
        ao = rF[c]['aoT']
        for u in range(4):
            pair = c * 4 + u
            b, h = pair // 16, pair % 16
            aoTf[h * 64:(h + 1) * 64, b * 8192:(b + 1) * 8192] = ao[u * 64:(u + 1) * 64]

    gwT = np.ascontiguousarray(_f(inp['L1_ffn_gate_w']).T)
    gb = _f(inp['L1_ffn_gate_b']).reshape(8, 1)
    expT = np.ascontiguousarray(_f(inp['L1_ffn_expert_w']).transpose(0, 2, 1)).reshape(8 * D, D)
    ebT = np.ascontiguousarray(_f(inp['L1_ffn_expert_b']).T)
    nc = build_KG(0)
    maps = []
    for c in range(NCORE):
        cs = slice(c * TCORE, (c + 1) * TCORE)
        b = c // 4
        pp = poolP if b == 0 else np.ascontiguousarray(poolP[[4, 5, 6, 7, 0, 1, 2, 3]])
        maps.append(dict(aoT=np.ascontiguousarray(aoTf[:, cs]), WoA=_f(inp['L1_attn_Wout']),
                         poolP=pp, gwT=gwT, gb=gb, expT=expT, ebT=ebT,
                         xn1T=xn1T[c], x1T=x1T[c], I128=I128, onesrow=onesrow))
    rG = _run(nc, maps)
    outT = np.concatenate([r['outT'] for r in rG], axis=1)
    out = np.ascontiguousarray(outT.T).reshape(2, 8192, D).astype(np.float32)
    return out


# version 05
# speedup vs baseline: 1.3068x; 1.1936x over previous
import numpy as np
import ml_dtypes
import concourse.bass as bass
import concourse.mybir as mybir
from concourse.bass_utils import run_bass_kernel_spmd
from contextlib import ExitStack

F32 = mybir.dt.float32
BF16 = mybir.dt.bfloat16
AF = mybir.ActivationFunctionType
OP = mybir.AluOpType
AX = mybir.AxisListType
BF = ml_dtypes.bfloat16
f32 = np.float32

NCORE = 8
D = 1024
DI = 2048
DS = 16
NT = 16384
TCORE = 2048
CH = 512
NCH = TCORE // CH


class Prog:

    def __init__(self, nc):
        self.nc = nc
        self.ops = []

    def op(self, eng, fn, inc=1, wait=True):
        self.ops.append((eng, fn, inc, wait))

    def dma(self, out, in_, **kw):
        self.op('sync', lambda e, o=out, i=in_, kw=kw: e.dma_start(out=o, in_=i, **kw), inc=16)

    def mm(self, out, lhsT, rhs, start=True, stop=True, wait=True):
        self.op('tensor',
                lambda e, o=out, l=lhsT, r=rhs, s1=start, s2=stop: e.matmul(o, l, r, start=s1, stop=s2),
                inc=1, wait=wait)

    def tr(self, out, in_, ident):
        self.op('tensor', lambda e, o=out, i=in_, d=ident: e.transpose(o, i, d))

    def v(self, method, *a, wait=True, **kw):
        self.op('vector', lambda e, m=method, a=a, kw=kw: getattr(e, m)(*a, **kw), wait=wait)

    def s(self, method, *a, wait=True, **kw):
        self.op('scalar', lambda e, m=method, a=a, kw=kw: getattr(e, m)(*a, **kw), wait=wait)

    def finish(self):
        nc = self.nc
        ops = self.ops
        with nc.semaphore("tok") as tok, nc.Block() as block:
            def mk(name):
                def body(eng):
                    cnt = 0
                    prev_en = None
                    for (en, fn, inc, wait) in ops:
                        if en == name:
                            if wait and cnt > 0 and not (
                                    prev_en == name and name in ('sync', 'tensor')):
                                eng.wait_ge(tok, cnt)
                            r = fn(eng)
                            if inc:
                                r.then_inc(tok, inc)
                        cnt += inc
                        prev_en = en
                    if name == 'sync':
                        eng.wait_ge(tok, cnt)
                return body
            block.sync(mk('sync'))
            block.tensor(mk('tensor'))
            block.vector(mk('vector'))
            block.scalar(mk('scalar'))
            if any(en == 'gpsimd' for (en, _, _, _) in ops):
                block.gpsimd(mk('gpsimd'))


def _run(nc, in_maps):
    res = run_bass_kernel_spmd(nc, in_maps, core_ids=list(range(NCORE)))
    return res.results


def build_KA():
    nc = bass.Bass("TRN2", target_bir_lowering=False)
    xin_h = nc.dram_tensor("xin", [D, TCORE + 3], F32, kind="ExternalInput").ap()
    WinT = nc.dram_tensor("WinT", [D, 2 * DI], F32, kind="ExternalInput").ap()
    WxT = nc.dram_tensor("WxT", [DI, 33], F32, kind="ExternalInput").ap()
    WdtT = nc.dram_tensor("WdtT", [1, DI], F32, kind="ExternalInput").ap()
    convw = nc.dram_tensor("convw", [DI, 4], F32, kind="ExternalInput").ap()
    convb = nc.dram_tensor("convb", [DI, 1], F32, kind="ExternalInput").ap()
    bdt = nc.dram_tensor("bdt", [DI, 1], F32, kind="ExternalInput").ap()
    ones128 = nc.dram_tensor("ones128", [128, 1], F32, kind="ExternalInput").ap()
    onesrow = nc.dram_tensor("onesrow", [1, 128], F32, kind="ExternalInput").ap()
    xnT_o = nc.dram_tensor("xnT", [D, TCORE], F32, kind="ExternalOutput").ap()
    xcT_o = nc.dram_tensor("xcT", [DI, TCORE], F32, kind="ExternalOutput").ap()
    dtT_o = nc.dram_tensor("dtT", [DI, TCORE], F32, kind="ExternalOutput").ap()
    resT_o = nc.dram_tensor("resT", [DI, TCORE], F32, kind="ExternalOutput").ap()
    BT_o = nc.dram_tensor("BT", [16, TCORE], F32, kind="ExternalOutput").ap()
    CT_o = nc.dram_tensor("CT", [16, TCORE], F32, kind="ExternalOutput").ap()

    st = ExitStack()
    with st:
        sb = lambda n, s, d: st.enter_context(nc.sbuf_tensor(n, s, d)).ap()
        ps = lambda n, s: st.enter_context(nc.psum_tensor(n, s, F32)).ap()
        xin = sb("xinb", [128, 8 * 512], F32)
        sq = sb("sq", [128, 512], F32)
        xnf = sb("xnf", [128, 512], F32)
        xn_bf = sb("xn_bf", [128, 8 * 512], BF16)
        Win_bf = sb("Win_bf", [128, 8 * 4096], BF16)
        Wx_bf = sb("Wx_bf", [128, 16 * 33], BF16)
        Wdt_bf = sb("Wdt_bf", [1, DI], BF16)
        cwc = sb("cwc", [128, 16 * 4], F32)
        cbc = sb("cbc", [128, 16], F32)
        bdtc = sb("bdtc", [128, 16], F32)
        onec = sb("onec", [128, 1], F32)
        oner = sb("oner", [1, 128], F32)
        xpb = sb("xpb", [128, 16 * 515], F32)
        xcf = sb("xcf", [128, 512], F32)
        accv = sb("accv", [128, 512], F32)
        xc_bf = sb("xc_bf", [128, 16 * 512], BF16)
        resb = sb("resb", [128, 512], F32)
        ssmf = sb("ssmf", [33, 512], F32)
        dtrf = sb("dtrf", [1, 512], F32)
        dtr_bf = sb("dtr_bf", [1, 512], BF16)
        zf = sb("zf", [128, 512], F32)
        azf = sb("azf", [128, 512], F32)
        ef = sb("ef", [128, 512], F32)
        lf = sb("lf", [128, 512], F32)
        rf = sb("rf", [128, 512], F32)
        dtf = sb("dtf", [128, 512], F32)
        meanr = sb("meanr", [1, 512], F32)
        ex2r = sb("ex2r", [1, 512], F32)
        msqr = sb("msqr", [1, 512], F32)
        varr = sb("varr", [1, 512], F32)
        lnvr = sb("lnvr", [1, 512], F32)
        rstdr = sb("rstdr", [1, 512], F32)
        cst = sb("cst", [128, 2], F32)

        psS = ps("psS", [1, 512])
        psQ = ps("psQ", [1, 512])
        psM = ps("psM", [128, 512])
        psR = ps("psR", [128, 512])
        psX = ps("psX", [128, 512])
        psW = ps("psW", [33, 512])
        psD = ps("psD", [128, 512])

        p = Prog(nc)
        p.v('memset', cst[:, 0:1], 1e-5)
        p.v('memset', cst[:, 1:2], 1.0)
        p.dma(onec, ones128)
        p.dma(oner, onesrow)
        for k in range(8):
            p.dma(xin[:, 0:4096], WinT[k * 128:(k + 1) * 128, :])
            p.s('copy', Win_bf[:, k * 4096:(k + 1) * 4096], xin[:, 0:4096])
        for k in range(16):
            p.dma(xin[:, 0:33], WxT[k * 128:(k + 1) * 128, :])
            p.s('copy', Wx_bf[:, k * 33:(k + 1) * 33], xin[:, 0:33])
            p.dma(cwc[:, k * 4:(k + 1) * 4], convw[k * 128:(k + 1) * 128, :])
            p.dma(cbc[:, k:k + 1], convb[k * 128:(k + 1) * 128, :])
            p.dma(bdtc[:, k:k + 1], bdt[k * 128:(k + 1) * 128, :])
        for q in range(4):
            p.dma(xin[0:1, 0:512], WdtT[0:1, q * 512:(q + 1) * 512])
            p.s('copy', Wdt_bf[0:1, q * 512:(q + 1) * 512], xin[0:1, 0:512])

        def chunk(co, W, tail_off, main_ci):
            for k in range(8):
                p.dma(xin[:, k * 512:k * 512 + W], xin_h[k * 128:(k + 1) * 128, co:co + W])
            for k in range(8):
                p.mm(psS[0:1, 0:W], onec, xin[:, k * 512:k * 512 + W],
                     start=(k == 0), stop=(k == 7), wait=(k == 0))
            for k in range(8):
                p.s('activation', sq[:, 0:W], xin[:, k * 512:k * 512 + W], AF.Square)
                p.mm(psQ[0:1, 0:W], onec, sq[:, 0:W], start=(k == 0), stop=(k == 7))
            p.s('activation', meanr[0:1, 0:W], psS[0:1, 0:W], AF.Copy, scale=1.0 / 1024.0)
            p.s('activation', ex2r[0:1, 0:W], psQ[0:1, 0:W], AF.Copy, scale=1.0 / 1024.0)
            p.v('tensor_mul', msqr[0:1, 0:W], meanr[0:1, 0:W], meanr[0:1, 0:W])
            p.v('tensor_sub', varr[0:1, 0:W], ex2r[0:1, 0:W], msqr[0:1, 0:W])
            p.s('activation', lnvr[0:1, 0:W], varr[0:1, 0:W], AF.Ln, bias=cst[0:1, 0:1])
            p.s('activation', rstdr[0:1, 0:W], lnvr[0:1, 0:W], AF.Exp, scale=-0.5)
            p.mm(psM[:, 0:W], oner, meanr[0:1, 0:W])
            p.mm(psR[:, 0:W], oner, rstdr[0:1, 0:W])
            for k in range(8):
                p.v('tensor_sub', xnf[:, 0:W], xin[:, k * 512:k * 512 + W], psM[:, 0:W])
                p.v('tensor_mul', xnf[:, 0:W], xnf[:, 0:W], psR[:, 0:W])
                p.s('copy', xn_bf[:, k * 512:k * 512 + W], xnf[:, 0:W])
                if main_ci is not None:
                    p.dma(xnT_o[k * 128:(k + 1) * 128, main_ci * 512:main_ci * 512 + W], xnf[:, 0:W])
            n_m = 32 if main_ci is not None else 16
            for m in range(n_m):
                for k in range(8):
                    p.mm(psX[:, 0:W], Win_bf[:, k * 4096 + m * 128:k * 4096 + (m + 1) * 128],
                         xn_bf[:, k * 512:k * 512 + W], start=(k == 0), stop=(k == 7), wait=(k == 0))
                if m < 16:
                    p.s('copy', xpb[:, m * 515 + tail_off:m * 515 + tail_off + W], psX[:, 0:W])
                else:
                    p.s('copy', resb[:, 0:W], psX[:, 0:W])
                    p.dma(resT_o[(m - 16) * 128:(m - 15) * 128, main_ci * 512:main_ci * 512 + W],
                          resb[:, 0:W])

        chunk(0, 3, 0, None)
        for ci in range(NCH):
            chunk(3 + ci * 512, 512, 3, ci)
            for j in range(16):
                src = xpb[:, j * 515:(j + 1) * 515]
                p.v('tensor_scalar', accv, src[:, 0:512], cwc[:, j * 4:j * 4 + 1], None, OP.mult)
                for jj in range(1, 4):
                    p.v('scalar_tensor_tensor', accv, src[:, jj:jj + 512],
                        cwc[:, j * 4 + jj:j * 4 + jj + 1], accv, OP.mult, OP.add)
                p.s('activation', xcf, accv, AF.Silu, bias=cbc[:, j:j + 1])
                p.dma(xcT_o[j * 128:(j + 1) * 128, ci * 512:(ci + 1) * 512], xcf)
                p.s('copy', xc_bf[:, j * 512:(j + 1) * 512], xcf)
                p.v('tensor_copy', src[:, 0:3], src[:, 512:515])
            for k in range(16):
                p.mm(psW[0:33, :], Wx_bf[:, k * 33:(k + 1) * 33], xc_bf[:, k * 512:(k + 1) * 512],
                     start=(k == 0), stop=(k == 15), wait=(k == 0))
            p.s('copy', ssmf, psW[0:33, :])
            p.dma(BT_o[0:16, ci * 512:(ci + 1) * 512], ssmf[0:16, :])
            p.dma(CT_o[0:16, ci * 512:(ci + 1) * 512], ssmf[16:32, :])
            p.dma(dtrf[0:1, :], ssmf[32:33, :])
            p.s('copy', dtr_bf, dtrf)
            for m in range(16):
                p.mm(psD, Wdt_bf[0:1, m * 128:(m + 1) * 128], dtr_bf)
                p.v('tensor_scalar', zf, psD, bdtc[:, m:m + 1], None, OP.add)
                p.s('activation', azf, zf, AF.Abs)
                p.s('activation', ef, azf, AF.Exp, scale=-1.0)
                p.s('activation', lf, ef, AF.Ln, bias=cst[:, 1:2])
                p.v('tensor_scalar', rf, zf, 0.0, None, OP.max)
                p.v('tensor_add', dtf, rf, lf)
                p.dma(dtT_o[m * 128:(m + 1) * 128, ci * 512:(ci + 1) * 512], dtf)
        p.finish()
    return nc


def build_KB():
    TC = 1024
    nc = bass.Bass("TRN2", target_bir_lowering=False)
    dtT = nc.dram_tensor("dtT", [256, NT], F32, kind="ExternalInput").ap()
    xcT = nc.dram_tensor("xcT", [256, NT], F32, kind="ExternalInput").ap()
    resT = nc.dram_tensor("resT", [256, NT], F32, kind="ExternalInput").ap()
    BT = nc.dram_tensor("BT", [16, NT], F32, kind="ExternalInput").ap()
    CT = nc.dram_tensor("CT", [16, NT], F32, kind="ExternalInput").ap()
    At = nc.dram_tensor("At", [128, 16], F32, kind="ExternalInput").ap()
    Dv = nc.dram_tensor("Dv", [256, 1], F32, kind="ExternalInput").ap()
    onesrow = nc.dram_tensor("onesrow", [1, 128], F32, kind="ExternalInput").ap()
    y2T = nc.dram_tensor("y2T", [256, NT], BF16, kind="ExternalOutput").ap()

    st = ExitStack()
    with st:
        sb = lambda n, s, d: st.enter_context(nc.sbuf_tensor(n, s, d)).ap()
        ps = lambda n, s: st.enter_context(nc.psum_tensor(n, s, F32)).ap()
        dt_t = [sb(f"dt{e}", [128, TC], F32) for e in range(2)]
        xc_t = [sb(f"xc{e}", [128, TC], F32) for e in range(2)]
        u_t = [sb(f"u{e}", [128, TC], F32) for e in range(2)]
        ya = [sb(f"ya{e}", [128, TC], F32) for e in range(2)]
        a1 = sb("a1", [128, TC], F32)
        a2 = sb("a2", [128, TC], F32)
        bv = sb("bv", [128, TC], F32)
        hh = sb("hh", [128, TC], F32)
        tmp = sb("tmp", [128, TC], F32)
        res_t = sb("res_t", [128, TC], F32)
        y2b = sb("y2b", [128, TC], BF16)
        brow = sb("brow", [1, TC], F32)
        crow = sb("crow", [1, TC], F32)
        Acol = sb("Acol", [128, 16], F32)
        Dcol = sb("Dcol", [128, 2], F32)
        carry = sb("carry", [128, 32], F32)
        oner = sb("oner", [1, 128], F32)
        psB = ps("psB", [128, TC])
        psC = ps("psC", [128, TC])

        p = Prog(nc)
        p.dma(oner, onesrow)
        p.dma(Acol, At)
        p.s('activation', Acol, Acol, AF.Exp)
        p.v('tensor_scalar', Acol, Acol, -1.0, None, OP.mult)
        for e in range(2):
            p.dma(Dcol[:, e:e + 1], Dv[e * 128:(e + 1) * 128, :])
        for b in range(2):
            p.v('memset', carry, 0.0)
            for ch in range(8192 // TC):
                c0 = b * 8192 + ch * TC
                for e in range(2):
                    p.dma(dt_t[e], dtT[e * 128:(e + 1) * 128, c0:c0 + TC])
                    p.dma(xc_t[e], xcT[e * 128:(e + 1) * 128, c0:c0 + TC])
                    p.v('tensor_mul', u_t[e], dt_t[e], xc_t[e])
                for n in range(16):
                    p.dma(brow, BT[n:n + 1, c0:c0 + TC])
                    p.dma(crow, CT[n:n + 1, c0:c0 + TC])
                    for hf in range(2):
                        sl = slice(hf * 512, (hf + 1) * 512)
                        p.mm(psB[:, sl], oner, brow[0:1, sl])
                        p.mm(psC[:, sl], oner, crow[0:1, sl])
                    for e in range(2):
                        p.v('tensor_scalar', a1, dt_t[e], Acol[:, n:n + 1], None, OP.mult)
                        p.s('activation', a2, a1, AF.Exp)
                        for hf in range(2):
                            sl = slice(hf * 512, (hf + 1) * 512)
                            p.v('tensor_tensor', bv[:, sl], u_t[e][:, sl], psB[:, sl], OP.mult)
                        slot = e * 16 + n
                        p.v('tensor_tensor_scan', hh, a2, bv, carry[:, slot:slot + 1],
                            OP.mult, OP.add)
                        p.v('tensor_copy', carry[:, slot:slot + 1], hh[:, TC - 1:TC])
                        for hf in range(2):
                            sl = slice(hf * 512, (hf + 1) * 512)
                            if n == 0:
                                p.v('tensor_tensor', ya[e][:, sl], hh[:, sl], psC[:, sl], OP.mult)
                            else:
                                p.v('tensor_tensor', tmp[:, sl], hh[:, sl], psC[:, sl], OP.mult)
                        if n > 0:
                            p.v('tensor_add', ya[e], ya[e], tmp)
                for e in range(2):
                    p.v('scalar_tensor_tensor', ya[e], xc_t[e], Dcol[:, e:e + 1], ya[e],
                        OP.mult, OP.add)
                    p.dma(res_t, resT[e * 128:(e + 1) * 128, c0:c0 + TC])
                    p.v('tensor_add', ya[e], ya[e], res_t)
                    p.s('activation', y2b, ya[e], AF.Silu)
                    p.dma(y2T[e * 128:(e + 1) * 128, c0:c0 + TC], y2b)
        p.finish()
    return nc


def build_KC():
    nc = bass.Bass("TRN2", target_bir_lowering=False)
    y2T = nc.dram_tensor("y2T", [DI, TCORE], BF16, kind="ExternalInput").ap()
    WoutT = nc.dram_tensor("WoutT", [DI, D], F32, kind="ExternalInput").ap()
    xnT = nc.dram_tensor("xnT", [D, TCORE], F32, kind="ExternalInput").ap()
    gam = nc.dram_tensor("gam", [D, 1], F32, kind="ExternalInput").ap()
    routT = nc.dram_tensor("routT", [D, 8], F32, kind="ExternalInput").ap()
    I128 = nc.dram_tensor("I128", [128, 128], F32, kind="ExternalInput").ap()
    ones128 = nc.dram_tensor("ones128", [128, 1], F32, kind="ExternalInput").ap()
    onesrow = nc.dram_tensor("onesrow", [1, 128], F32, kind="ExternalInput").ap()
    sT_o = nc.dram_tensor("sT", [D, TCORE], F32, kind="ExternalOutput").ap()
    normT_o = nc.dram_tensor("normT", [D, TCORE], BF16, kind="ExternalOutput").ap()
    combT_o = nc.dram_tensor("combT", [8, TCORE], F32, kind="ExternalOutput").ap()

    st = ExitStack()
    with st:
        sb = lambda n, s, d: st.enter_context(nc.sbuf_tensor(n, s, d)).ap()
        ps = lambda n, s: st.enter_context(nc.psum_tensor(n, s, F32)).ap()
        Wo_bf = sb("Wo_bf", [128, 16 * 1024], BF16)
        ro_f = sb("ro_f", [128, 8 * 8], F32)
        n32 = sb("n32", [128, 8 * 512], F32)
        wtmp = sb("wtmp", [128, 1024], F32)
        y2_bf = sb("y2_bf", [128, 16 * 512], BF16)
        xn = sb("xn", [128, 8 * 512], F32)
        sq = sb("sq", [128, 512], F32)
        sf = sb("sf", [128, 512], F32)
        nf = sb("nf", [128, 512], F32)
        n_bf = sb("n_bf", [128, 8 * 512], BF16)
        gcol = sb("gcol", [128, 8], F32)
        onec = sb("onec", [128, 1], F32)
        oner = sb("oner", [1, 128], F32)
        I_f = sb("I_f", [128, 128], F32)
        ex2r = sb("ex2r", [1, 512], F32)
        lnvr = sb("lnvr", [1, 512], F32)
        rstdr = sb("rstdr", [1, 512], F32)
        probs = sb("probs", [8, 512], F32)
        tpro = sb("tpro", [128, 8], F32)
        m1 = sb("m1", [128, 1], F32)
        eqm = sb("eqm", [128, 8], F32)
        cnt = sb("cnt", [128, 1], F32)
        big = sb("big", [128, 8], F32)
        wo_ = sb("wo_", [128, 8], F32)
        m2 = sb("m2", [128, 1], F32)
        c2 = sb("c2", [128, 1], F32)
        min2 = sb("min2", [128, 1], F32)
        keep = sb("keep", [128, 8], F32)
        den = sb("den", [128, 1], F32)
        rden = sb("rden", [128, 1], F32)
        combt = sb("combt", [128, 8], F32)
        combf = sb("combf", [8, 512], F32)
        cst = sb("cst", [128, 1], F32)
        psS = ps("psS", [128, 512])
        psQ = ps("psQ", [1, 512])
        psR = ps("psR", [128, 512])
        psP = ps("psP", [8, 512])
        psT = ps("psT", [128, 8])
        psCT = ps("psCT", [8, 128])

        p = Prog(nc)
        p.v('memset', cst, 1e-8)
        p.dma(onec, ones128)
        p.dma(oner, onesrow)
        p.dma(I_f, I128)
        for k in range(16):
            p.dma(wtmp, WoutT[k * 128:(k + 1) * 128, :])
            p.s('copy', Wo_bf[:, k * 1024:(k + 1) * 1024], wtmp)
        for k in range(8):
            p.dma(ro_f[:, k * 8:(k + 1) * 8], routT[k * 128:(k + 1) * 128, :])
            p.dma(gcol[:, k:k + 1], gam[k * 128:(k + 1) * 128, :])
        for ci in range(NCH):
            c0 = ci * 512
            for k in range(16):
                p.dma(y2_bf[:, k * 512:(k + 1) * 512], y2T[k * 128:(k + 1) * 128, c0:c0 + 512])
            for m in range(8):
                for k in range(16):
                    p.mm(psS, Wo_bf[:, k * 1024 + m * 128:k * 1024 + (m + 1) * 128],
                         y2_bf[:, k * 512:(k + 1) * 512], start=(k == 0), stop=(k == 15),
                         wait=(k == 0))
                p.s('copy', sf, psS)
                p.dma(sT_o[m * 128:(m + 1) * 128, c0:c0 + 512], sf)
            for k in range(8):
                p.dma(xn[:, k * 512:(k + 1) * 512], xnT[k * 128:(k + 1) * 128, c0:c0 + 512])
            for k in range(8):
                p.s('activation', sq, xn[:, k * 512:(k + 1) * 512], AF.Square)
                p.mm(psQ[0:1, :], onec, sq, start=(k == 0), stop=(k == 7))
            p.s('activation', ex2r[0:1, :], psQ[0:1, :], AF.Copy, scale=1.0 / 1024.0)
            p.s('activation', lnvr[0:1, :], ex2r[0:1, :], AF.Ln, bias=cst[0:1, 0:1])
            p.s('activation', rstdr[0:1, :], lnvr[0:1, :], AF.Exp, scale=-0.5)
            p.mm(psR, oner, rstdr[0:1, :])
            for k in range(8):
                p.v('tensor_tensor', nf, xn[:, k * 512:(k + 1) * 512], psR, OP.mult)
                p.v('tensor_scalar', n32[:, k * 512:(k + 1) * 512], nf, gcol[:, k:k + 1],
                    None, OP.mult)
                p.s('copy', n_bf[:, k * 512:(k + 1) * 512], n32[:, k * 512:(k + 1) * 512])
                p.dma(normT_o[k * 128:(k + 1) * 128, c0:c0 + 512], n_bf[:, k * 512:(k + 1) * 512])
            for k in range(8):
                p.mm(psP[0:8, :], ro_f[:, k * 8:(k + 1) * 8], n32[:, k * 512:(k + 1) * 512],
                     start=(k == 0), stop=(k == 7), wait=(k == 0))
            p.s('activation', probs, psP[0:8, :], AF.Sigmoid)
            for j in range(4):
                p.tr(psT, probs[0:8, j * 128:(j + 1) * 128], I_f[0:8, 0:8])
                p.s('copy', tpro, psT)
                p.v('reduce_max', m1, tpro, axis=AX.X)
                p.v('tensor_scalar', eqm, tpro, m1, None, OP.is_ge)
                p.v('reduce_sum', cnt, eqm, axis=AX.X)
                p.v('tensor_scalar', big, eqm, 1e30, None, OP.mult)
                p.v('tensor_sub', wo_, tpro, big)
                p.v('reduce_max', m2, wo_, axis=AX.X)
                p.v('tensor_scalar', c2, cnt, 1.5, None, OP.is_ge)
                p.v('tensor_sub', min2, m1, m2)
                p.v('tensor_mul', min2, min2, c2)
                p.v('tensor_add', min2, min2, m2)
                p.v('tensor_scalar', keep, tpro, min2, None, OP.is_ge)
                p.v('tensor_add', den, m1, min2)
                p.v('tensor_scalar', den, den, 1e-6, None, OP.add)
                p.v('reciprocal', rden, den)
                p.v('tensor_mul', combt, tpro, keep)
                p.v('tensor_scalar', combt, combt, rden, None, OP.mult)
                p.tr(psCT, combt, I_f)
                p.s('copy', combf[0:8, j * 128:(j + 1) * 128], psCT[0:8, :])
            p.dma(combT_o[0:8, c0:c0 + 512], combf)
        p.finish()
    return nc


def build_KD1():
    nc = bass.Bass("TRN2", target_bir_lowering=False)
    normT = nc.dram_tensor("normT", [D, TCORE], BF16, kind="ExternalInput").ap()
    sw1 = nc.dram_tensor("sw1", [D, 4096], F32, kind="ExternalInput").ap()
    sw2 = nc.dram_tensor("sw2", [4096, D], F32, kind="ExternalInput").ap()
    sb1 = nc.dram_tensor("sb1", [4096, 1], F32, kind="ExternalInput").ap()
    sb2 = nc.dram_tensor("sb2", [D, 1], F32, kind="ExternalInput").ap()
    sharedT_o = nc.dram_tensor("sharedT", [D, TCORE], F32, kind="ExternalOutput").ap()

    st = ExitStack()
    with st:
        sb = lambda n, s, d: st.enter_context(nc.sbuf_tensor(n, s, d)).ap()
        ps = lambda n, s: st.enter_context(nc.psum_tensor(n, s, F32)).ap()
        w1_bf = sb("w1_bf", [128, 8 * 4096], BF16)
        w2_bf = sb("w2_bf", [128, 32 * 1024], BF16)
        wtmp = sb("wtmp", [128, 1024], F32)
        nb = sb("nb", [128, 8 * 512], BF16)
        he = sb("he", [128, 32 * 512], BF16)
        b1c = sb("b1c", [128, 32], F32)
        b2c = sb("b2c", [128, 8], F32)
        outf = sb("outf", [128, 512], F32)
        outf2 = sb("outf2", [128, 512], F32)
        psH = ps("psH", [128, 512])
        psH2 = ps("psH2", [128, 512])
        psO = ps("psO", [128, 512])
        psO2 = ps("psO2", [128, 512])

        p = Prog(nc)
        for k in range(8):
            for q in range(4):
                p.dma(wtmp, sw1[k * 128:(k + 1) * 128, q * 1024:(q + 1) * 1024])
                p.s('copy', w1_bf[:, k * 4096 + q * 1024:k * 4096 + (q + 1) * 1024], wtmp)
        for k in range(32):
            p.dma(wtmp, sw2[k * 128:(k + 1) * 128, :])
            p.s('copy', w2_bf[:, k * 1024:(k + 1) * 1024], wtmp)
        for m in range(32):
            p.dma(b1c[:, m:m + 1], sb1[m * 128:(m + 1) * 128, :])
        for m in range(8):
            p.dma(b2c[:, m:m + 1], sb2[m * 128:(m + 1) * 128, :])
        for ci in range(NCH):
            c0 = ci * 512
            for k in range(8):
                p.dma(nb[:, k * 512:(k + 1) * 512], normT[k * 128:(k + 1) * 128, c0:c0 + 512])
            for m0 in range(0, 32, 2):
                for (pi, ph) in ((0, psH), (1, psH2)):
                    m = m0 + pi
                    for k in range(8):
                        p.mm(ph, w1_bf[:, k * 4096 + m * 128:k * 4096 + (m + 1) * 128],
                             nb[:, k * 512:(k + 1) * 512], start=(k == 0), stop=(k == 7),
                             wait=(k == 0 and pi == 0))
                p.s('activation', he[:, m0 * 512:(m0 + 1) * 512], psH, AF.Relu,
                    bias=b1c[:, m0:m0 + 1])
                p.s('activation', he[:, (m0 + 1) * 512:(m0 + 2) * 512], psH2, AF.Relu,
                    bias=b1c[:, m0 + 1:m0 + 2], wait=False)
            for m0 in range(0, 8, 2):
                for (pi, ph) in ((0, psO), (1, psO2)):
                    m = m0 + pi
                    for k in range(32):
                        p.mm(ph, w2_bf[:, k * 1024 + m * 128:k * 1024 + (m + 1) * 128],
                             he[:, k * 512:(k + 1) * 512], start=(k == 0), stop=(k == 31),
                             wait=(k == 0 and pi == 0))
                p.v('tensor_scalar', outf, psO, b2c[:, m0:m0 + 1], None, OP.add)
                p.v('tensor_scalar', outf2, psO2, b2c[:, m0 + 1:m0 + 2], None, OP.add,
                    wait=False)
                p.dma(sharedT_o[m0 * 128:(m0 + 1) * 128, c0:c0 + 512], outf)
                p.dma(sharedT_o[(m0 + 1) * 128:(m0 + 2) * 128, c0:c0 + 512], outf2)
        p.finish()
    return nc


def build_KD2():
    TH = 8192
    nc = bass.Bass("TRN2", target_bir_lowering=False)
    normT = nc.dram_tensor("normT", [D, TH], BF16, kind="ExternalInput").ap()
    ew1 = nc.dram_tensor("ew1", [D, 4096], F32, kind="ExternalInput").ap()
    ew2 = nc.dram_tensor("ew2", [4096, D], F32, kind="ExternalInput").ap()
    eb1 = nc.dram_tensor("eb1", [4096, 1], F32, kind="ExternalInput").ap()
    eb2 = nc.dram_tensor("eb2", [D, 1], F32, kind="ExternalInput").ap()
    combr = nc.dram_tensor("combr", [1, TH], F32, kind="ExternalInput").ap()
    onesrow = nc.dram_tensor("onesrow", [1, 128], F32, kind="ExternalInput").ap()
    routT_o = nc.dram_tensor("routT", [D, TH], F32, kind="ExternalOutput").ap()

    st = ExitStack()
    with st:
        sb = lambda n, s, d: st.enter_context(nc.sbuf_tensor(n, s, d)).ap()
        ps = lambda n, s: st.enter_context(nc.psum_tensor(n, s, F32)).ap()
        w1_bf = sb("w1_bf", [128, 8 * 4096], BF16)
        w2_bf = sb("w2_bf", [128, 32 * 1024], BF16)
        wtmp = sb("wtmp", [128, 1024], F32)
        nb = sb("nb", [128, 8 * 512], BF16)
        he = sb("he", [128, 32 * 512], BF16)
        b1c = sb("b1c", [128, 32], F32)
        b2c = sb("b2c", [128, 8], F32)
        crow = sb("crow", [1, 512], F32)
        gbc = sb("gbc", [128, 512], F32)
        outf = sb("outf", [128, 512], F32)
        outf2 = sb("outf2", [128, 512], F32)
        oner = sb("oner", [1, 128], F32)
        psH = ps("psH", [128, 512])
        psH2 = ps("psH2", [128, 512])
        psO = ps("psO", [128, 512])
        psO2 = ps("psO2", [128, 512])
        psG = ps("psG", [128, 512])

        p = Prog(nc)
        p.dma(oner, onesrow)
        for k in range(8):
            for q in range(4):
                p.dma(wtmp, ew1[k * 128:(k + 1) * 128, q * 1024:(q + 1) * 1024])
                p.s('copy', w1_bf[:, k * 4096 + q * 1024:k * 4096 + (q + 1) * 1024], wtmp)
        for k in range(32):
            p.dma(wtmp, ew2[k * 128:(k + 1) * 128, :])
            p.s('copy', w2_bf[:, k * 1024:(k + 1) * 1024], wtmp)
        for m in range(32):
            p.dma(b1c[:, m:m + 1], eb1[m * 128:(m + 1) * 128, :])
        for m in range(8):
            p.dma(b2c[:, m:m + 1], eb2[m * 128:(m + 1) * 128, :])
        for ci in range(TH // 512):
            c0 = ci * 512
            for k in range(8):
                p.dma(nb[:, k * 512:(k + 1) * 512], normT[k * 128:(k + 1) * 128, c0:c0 + 512])
            p.dma(crow, combr[0:1, c0:c0 + 512])
            p.mm(psG, oner, crow)
            p.s('copy', gbc, psG)
            for m0 in range(0, 32, 2):
                for (pi, ph) in ((0, psH), (1, psH2)):
                    m = m0 + pi
                    for k in range(8):
                        p.mm(ph, w1_bf[:, k * 4096 + m * 128:k * 4096 + (m + 1) * 128],
                             nb[:, k * 512:(k + 1) * 512], start=(k == 0), stop=(k == 7),
                             wait=(k == 0 and pi == 0))
                p.s('activation', he[:, m0 * 512:(m0 + 1) * 512], psH, AF.Relu,
                    bias=b1c[:, m0:m0 + 1])
                p.s('activation', he[:, (m0 + 1) * 512:(m0 + 2) * 512], psH2, AF.Relu,
                    bias=b1c[:, m0 + 1:m0 + 2], wait=False)
            for m0 in range(0, 8, 2):
                for (pi, ph) in ((0, psO), (1, psO2)):
                    m = m0 + pi
                    for k in range(32):
                        p.mm(ph, w2_bf[:, k * 1024 + m * 128:k * 1024 + (m + 1) * 128],
                             he[:, k * 512:(k + 1) * 512], start=(k == 0), stop=(k == 31),
                             wait=(k == 0 and pi == 0))
                p.v('scalar_tensor_tensor', outf, psO, b2c[:, m0:m0 + 1], gbc,
                    OP.add, OP.mult)
                p.v('scalar_tensor_tensor', outf2, psO2, b2c[:, m0 + 1:m0 + 2], gbc,
                    OP.add, OP.mult, wait=False)
                p.dma(routT_o[m0 * 128:(m0 + 1) * 128, c0:c0 + 512], outf)
                p.dma(routT_o[(m0 + 1) * 128:(m0 + 2) * 128, c0:c0 + 512], outf2)
        p.finish()
    return nc


def build_KDsum():
    nc = bass.Bass("TRN2", target_bir_lowering=False)
    rins = [nc.dram_tensor(f"r{e}", [D, TCORE], F32, kind="ExternalInput").ap() for e in range(8)]
    sharedT = nc.dram_tensor("sharedT", [D, TCORE], F32, kind="ExternalInput").ap()
    sT = nc.dram_tensor("sT", [D, TCORE], F32, kind="ExternalInput").ap()
    x0T = nc.dram_tensor("x0T", [D, TCORE], F32, kind="ExternalInput").ap()
    x1T_o = nc.dram_tensor("x1T", [D, TCORE], F32, kind="ExternalOutput").ap()

    st = ExitStack()
    with st:
        sb = lambda n, s, d: st.enter_context(nc.sbuf_tensor(n, s, d)).ap()
        acc = sb("acc", [128, 512], F32)
        t2 = sb("t2", [128, 512], F32)
        g = sb("g", [128, 512], F32)
        p = Prog(nc)
        for ci in range(NCH):
            c0 = ci * 512
            for m in range(8):
                rs = (m * 128, (m + 1) * 128)
                p.dma(acc, sharedT[rs[0]:rs[1], c0:c0 + 512])
                for e in range(8):
                    p.dma(t2, rins[e][rs[0]:rs[1], c0:c0 + 512])
                    p.v('tensor_add', acc, acc, t2)
                p.s('activation', g, acc, AF.Silu)
                p.dma(t2, sT[rs[0]:rs[1], c0:c0 + 512])
                p.v('tensor_mul', g, g, t2)
                p.dma(t2, x0T[rs[0]:rs[1], c0:c0 + 512])
                p.v('tensor_add', g, g, t2)
                p.dma(x1T_o[rs[0]:rs[1], c0:c0 + 512], g)
        p.finish()
    return nc


def build_KE():
    nc = bass.Bass("TRN2", target_bir_lowering=False)
    x1T = nc.dram_tensor("x1T", [D, TCORE], F32, kind="ExternalInput").ap()
    Wqkv = nc.dram_tensor("Wqkv", [D, 3072], F32, kind="ExternalInput").ap()
    cosB = nc.dram_tensor("cosB", [128, TCORE], F32, kind="ExternalInput").ap()
    sinB = nc.dram_tensor("sinB", [128, TCORE], F32, kind="ExternalInput").ap()
    I128b = nc.dram_tensor("I128b", [128, 128], BF16, kind="ExternalInput").ap()
    ones128 = nc.dram_tensor("ones128", [128, 1], F32, kind="ExternalInput").ap()
    onesrow = nc.dram_tensor("onesrow", [1, 128], F32, kind="ExternalInput").ap()
    qT_o = nc.dram_tensor("qT", [D, TCORE], BF16, kind="ExternalOutput").ap()
    kT_o = nc.dram_tensor("kT", [D, TCORE], BF16, kind="ExternalOutput").ap()
    vR_o = nc.dram_tensor("vR", [TCORE, D], BF16, kind="ExternalOutput").ap()
    xn1T_o = nc.dram_tensor("xn1T", [D, TCORE], BF16, kind="ExternalOutput").ap()
    pool_o = nc.dram_tensor("pool", [128, 8], F32, kind="ExternalOutput").ap()

    st = ExitStack()
    with st:
        sb = lambda n, s, d: st.enter_context(nc.sbuf_tensor(n, s, d)).ap()
        ps = lambda n, s, d=F32: st.enter_context(nc.psum_tensor(n, s, d)).ap()
        Wq_bf = sb("Wq_bf", [128, 8 * 3072], BF16)
        wtmp = sb("wtmp", [128, 1024], F32)
        xin = sb("xin", [128, 8 * 512], F32)
        sq = sb("sq", [128, 512], F32)
        xnf = sb("xnf", [128, 512], F32)
        xn_bf = sb("xn_bf", [128, 8 * 512], BF16)
        cosb = sb("cosb", [128, 2048], F32)
        sinb = sb("sinb", [128, 2048], F32)
        qf = sb("qf", [128, 512], F32)
        qsw = sb("qsw", [128, 512], F32)
        t1_ = sb("t1_", [128, 512], F32)
        t2_ = sb("t2_", [128, 512], F32)
        qbf = sb("qbf", [128, 512], BF16)
        vbf = sb("vbf", [128, 512], BF16)
        vrow = sb("vrow", [128, 128], BF16)
        poolacc = sb("poolacc", [128, 8], F32)
        pcol = sb("pcol", [128, 1], F32)
        onec = sb("onec", [128, 1], F32)
        oner = sb("oner", [1, 128], F32)
        I_b = sb("I_b", [128, 128], BF16)
        meanr = sb("meanr", [1, 512], F32)
        ex2r = sb("ex2r", [1, 512], F32)
        msqr = sb("msqr", [1, 512], F32)
        varr = sb("varr", [1, 512], F32)
        lnvr = sb("lnvr", [1, 512], F32)
        rstdr = sb("rstdr", [1, 512], F32)
        cst = sb("cst", [128, 1], F32)
        psS = ps("psS", [1, 512])
        psQ = ps("psQ", [1, 512])
        psM = ps("psM", [128, 512])
        psR = ps("psR", [128, 512])
        psX = ps("psX", [128, 512])
        psTb = ps("psTb", [128, 128], BF16)

        p = Prog(nc)
        p.v('memset', cst, 1e-5)
        p.dma(onec, ones128)
        p.dma(oner, onesrow)
        p.dma(I_b, I128b)
        p.dma(cosb, cosB)
        p.dma(sinb, sinB)
        p.v('memset', poolacc, 0.0)
        for k in range(8):
            for q in range(3):
                p.dma(wtmp, Wqkv[k * 128:(k + 1) * 128, q * 1024:(q + 1) * 1024])
                p.s('copy', Wq_bf[:, k * 3072 + q * 1024:k * 3072 + (q + 1) * 1024], wtmp)
        for ci in range(NCH):
            c0 = ci * 512
            for k in range(8):
                p.dma(xin[:, k * 512:(k + 1) * 512], x1T[k * 128:(k + 1) * 128, c0:c0 + 512])
            for k in range(8):
                p.mm(psS[0:1, :], onec, xin[:, k * 512:(k + 1) * 512], start=(k == 0),
                     stop=(k == 7), wait=(k == 0))
            for k in range(8):
                p.s('activation', sq, xin[:, k * 512:(k + 1) * 512], AF.Square)
                p.mm(psQ[0:1, :], onec, sq, start=(k == 0), stop=(k == 7))
            p.s('activation', meanr[0:1, :], psS[0:1, :], AF.Copy, scale=1.0 / 1024.0)
            p.s('activation', ex2r[0:1, :], psQ[0:1, :], AF.Copy, scale=1.0 / 1024.0)
            p.v('tensor_mul', msqr[0:1, :], meanr[0:1, :], meanr[0:1, :])
            p.v('tensor_sub', varr[0:1, :], ex2r[0:1, :], msqr[0:1, :])
            p.s('activation', lnvr[0:1, :], varr[0:1, :], AF.Ln, bias=cst[0:1, 0:1])
            p.s('activation', rstdr[0:1, :], lnvr[0:1, :], AF.Exp, scale=-0.5)
            p.mm(psM, oner, meanr[0:1, :])
            p.mm(psR, oner, rstdr[0:1, :])
            for k in range(8):
                p.v('tensor_sub', xnf, xin[:, k * 512:(k + 1) * 512], psM)
                p.v('tensor_mul', xnf, xnf, psR)
                p.v('reduce_sum', pcol, xnf, axis=AX.X)
                p.v('tensor_add', poolacc[:, k:k + 1], poolacc[:, k:k + 1], pcol)
                p.s('copy', xn_bf[:, k * 512:(k + 1) * 512], xnf)
                p.dma(xn1T_o[k * 128:(k + 1) * 128, c0:c0 + 512], xn_bf[:, k * 512:(k + 1) * 512])
            for m in range(24):
                for k in range(8):
                    p.mm(psX, Wq_bf[:, k * 3072 + m * 128:k * 3072 + (m + 1) * 128],
                         xn_bf[:, k * 512:(k + 1) * 512], start=(k == 0), stop=(k == 7),
                         wait=(k == 0))
                if m < 16:
                    p.s('copy', qf, psX)
                    for g4 in range(4):
                        p.dma(qsw[((g4 ^ 1) * 32):((g4 ^ 1) * 32 + 32), :],
                              qf[g4 * 32:g4 * 32 + 32, :])
                    p.v('tensor_mul', t1_, qf, cosb[:, c0:c0 + 512])
                    p.v('tensor_mul', t2_, qsw, sinb[:, c0:c0 + 512])
                    p.v('tensor_add', t1_, t1_, t2_)
                    p.s('copy', qbf, t1_)
                    dst = qT_o if m < 8 else kT_o
                    mm = m if m < 8 else m - 8
                    p.dma(dst[mm * 128:(mm + 1) * 128, c0:c0 + 512], qbf)
                else:
                    p.s('copy', vbf, psX)
                    for j in range(4):
                        p.tr(psTb, vbf[:, j * 128:(j + 1) * 128], I_b)
                        p.s('copy', vrow, psTb)
                        p.dma(vR_o[c0 + j * 128:c0 + (j + 1) * 128,
                                   (m - 16) * 128:(m - 15) * 128], vrow)
        p.dma(pool_o, poolacc)
        p.finish()
    return nc


def build_KF():
    T = 8192
    nc = bass.Bass("TRN2", target_bir_lowering=False)
    qT = nc.dram_tensor("qT", [256, T], BF16, kind="ExternalInput").ap()
    kT = nc.dram_tensor("kT", [256, T], BF16, kind="ExternalInput").ap()
    vR = nc.dram_tensor("vR", [T, 256], BF16, kind="ExternalInput").ap()
    TRI = nc.dram_tensor("TRI", [128, 4 * 512], F32, kind="ExternalInput").ap()
    ones128 = nc.dram_tensor("ones128", [128, 1], F32, kind="ExternalInput").ap()
    onesrow = nc.dram_tensor("onesrow", [1, 128], F32, kind="ExternalInput").ap()
    aoT_o = nc.dram_tensor("aoT", [256, T], BF16, kind="ExternalOutput").ap()

    st = ExitStack()
    with st:
        sb = lambda n, s, d: st.enter_context(nc.sbuf_tensor(n, s, d)).ap()
        ps = lambda n, s: st.enter_context(nc.psum_tensor(n, s, F32)).ap()
        qfull = sb("qfull", [64, T], BF16)
        kfull = sb("kfull", [64, T], BF16)
        Vt = sb("Vt", [128, 64 * 64], BF16)
        tri = sb("tri", [128, 4 * 512], F32)
        ptef = sb("ptef", [128, 512], F32)
        ptbf = sb("ptbf", [128, 512], BF16)
        rs_ = sb("rs_", [1, 512], F32)
        rrs = sb("rrs", [1, 512], F32)
        of_ = sb("of_", [64, 512], F32)
        aob = sb("aob", [64, 512], BF16)
        onec = sb("onec", [128, 1], F32)
        oner = sb("oner", [1, 128], F32)
        psST = ps("psST", [128, 512])
        psRS = ps("psRS", [1, 512])
        psRR = ps("psRR", [128, 512])
        psO = ps("psO", [64, 512])

        obf = sb("obf", [128, 1], BF16)
        p = Prog(nc)
        p.dma(onec, ones128)
        p.dma(oner, onesrow)
        p.dma(tri, TRI)
        p.s('copy', obf, onec)
        for u in range(4):
            p.dma(qfull, qT[u * 64:(u + 1) * 64, :])
            p.dma(kfull, kT[u * 64:(u + 1) * 64, :])
            for t in range(64):
                p.dma(Vt[:, t * 64:(t + 1) * 64],
                      vR[t * 128:(t + 1) * 128, u * 64:(u + 1) * 64])
            for Q in range(16):
                nk = 4 * (Q + 1)
                qs = qfull[:, Q * 512:(Q + 1) * 512]
                p.mm(psST, kfull[:, 0:128], qs)
                for kt in range(nk):
                    if kt >= 4 * Q:
                        j = kt - 4 * Q
                        p.v('tensor_tensor', psST, psST, tri[:, j * 512:(j + 1) * 512], OP.add)
                    p.s('activation', ptbf, psST, AF.Exp, scale=0.125)
                    p.mm(psRS[0:1, :], obf, ptbf, start=(kt == 0), stop=(kt == nk - 1))
                    p.mm(psO, Vt[:, kt * 64:(kt + 1) * 64], ptbf, start=(kt == 0),
                         stop=(kt == nk - 1), wait=False)
                    if kt + 1 < nk:
                        p.mm(psST, kfull[:, (kt + 1) * 128:(kt + 2) * 128], qs, wait=False)
                p.s('copy', rs_[0:1, :], psRS[0:1, :])
                p.v('reciprocal', rrs[0:1, :], rs_[0:1, :])
                p.mm(psRR, oner, rrs[0:1, :])
                p.s('copy', of_, psO[0:64, :])
                p.v('tensor_tensor', of_, of_, psRR[0:64, :], OP.mult)
                p.s('copy', aob, of_)
                p.dma(aoT_o[u * 64:(u + 1) * 64, Q * 512:(Q + 1) * 512], aob)
        p.finish()
    return nc


def build_KG(b_core):
    nc = bass.Bass("TRN2", target_bir_lowering=False)
    aoT = nc.dram_tensor("aoT", [D, TCORE], BF16, kind="ExternalInput").ap()
    WoA = nc.dram_tensor("WoA", [D, D], F32, kind="ExternalInput").ap()
    poolP = nc.dram_tensor("poolP", [8, D], F32, kind="ExternalInput").ap()
    gwT = nc.dram_tensor("gwT", [D, 8], F32, kind="ExternalInput").ap()
    gb = nc.dram_tensor("gb", [8, 1], F32, kind="ExternalInput").ap()
    expT = nc.dram_tensor("expT", [8 * D, D], F32, kind="ExternalInput").ap()
    ebT = nc.dram_tensor("ebT", [D, 8], F32, kind="ExternalInput").ap()
    xn1T = nc.dram_tensor("xn1T", [D, TCORE], BF16, kind="ExternalInput").ap()
    x1T = nc.dram_tensor("x1T", [D, TCORE], F32, kind="ExternalInput").ap()
    I128 = nc.dram_tensor("I128", [128, 128], F32, kind="ExternalInput").ap()
    onesrow = nc.dram_tensor("onesrow", [1, 128], F32, kind="ExternalInput").ap()
    outT_o = nc.dram_tensor("outT", [D, TCORE], F32, kind="ExternalOutput").ap()

    st = ExitStack()
    with st:
        sb = lambda n, s, d: st.enter_context(nc.sbuf_tensor(n, s, d)).ap()
        ps = lambda n, s: st.enter_context(nc.psum_tensor(n, s, F32)).ap()
        Wo_bf = sb("Wo_bf", [128, 8 * 1024], BF16)
        ew_bf = sb("ew_bf", [128, 8 * 1024], BF16)
        wtmp = sb("wtmp", [128, 1024], F32)
        gwc = sb("gwc", [128, 8 * 8], F32)
        ebc = sb("ebc", [128, 8 * 8], F32)
        gbc = sb("gbc", [8, 1], F32)
        pr = sb("pr", [8, 1024], F32)
        ptj = sb("ptj", [128, 8], F32)
        pool2 = sb("pool2", [128, 16], F32)
        lgf = sb("lgf", [8, 2], F32)
        lg2 = sb("lg2", [2, 8], F32)
        m1 = sb("m1", [2, 1], F32)
        mneg = sb("mneg", [2, 1], F32)
        eqm = sb("eqm", [2, 8], F32)
        cnt = sb("cnt", [2, 1], F32)
        big = sb("big", [2, 8], F32)
        wo_ = sb("wo_", [2, 8], F32)
        m2 = sb("m2", [2, 1], F32)
        c2 = sb("c2", [2, 1], F32)
        min2 = sb("min2", [2, 1], F32)
        keep = sb("keep", [2, 8], F32)
        ex = sb("ex", [2, 8], F32)
        den = sb("den", [2, 1], F32)
        rden = sb("rden", [2, 1], F32)
        gat = sb("gat", [2, 8], F32)
        gT8 = sb("gT8", [8, 2], F32)
        grow = sb("grow", [1, 16], F32)
        gcols = sb("gcols", [128, 8], F32)
        xnb = sb("xnb", [128, 8 * 512], BF16)
        aob = sb("aob", [128, 8 * 512], BF16)
        s1f = sb("s1f", [128, 8 * 512], F32)
        accf = sb("accf", [128, 8 * 512], F32)
        eo = sb("eo", [128, 512], F32)
        eo2 = sb("eo2", [128, 512], F32)
        g1 = sb("g1", [128, 512], F32)
        x1f = sb("x1f", [128, 512], F32)
        I_f = sb("I_f", [128, 128], F32)
        oner = sb("oner", [1, 128], F32)
        psX = ps("psX", [128, 512])
        psX2 = ps("psX2", [128, 512])
        psPT = ps("psPT", [128, 8])
        psL = ps("psL", [8, 2])
        psLT = ps("psLT", [2, 8])
        psGT = ps("psGT", [8, 2])
        psGC = ps("psGC", [128, 1])

        p = Prog(nc)
        p.dma(oner, onesrow)
        p.dma(I_f, I128)
        p.dma(gbc, gb)
        for k in range(8):
            p.dma(wtmp, WoA[k * 128:(k + 1) * 128, :])
            p.s('copy', Wo_bf[:, k * 1024:(k + 1) * 1024], wtmp)
            p.dma(gwc[:, k * 8:(k + 1) * 8], gwT[k * 128:(k + 1) * 128, :])
            p.dma(ebc[:, k * 8:(k + 1) * 8], ebT[k * 128:(k + 1) * 128, :])
        p.dma(pr, poolP)
        for j in range(8):
            p.tr(psPT, pr[0:8, j * 128:(j + 1) * 128], I_f[0:8, 0:8])
            p.s('copy', ptj, psPT)
            p.v('reduce_sum', pool2[:, j * 2:j * 2 + 1], ptj[:, 0:4], axis=AX.X)
            p.v('reduce_sum', pool2[:, j * 2 + 1:j * 2 + 2], ptj[:, 4:8], axis=AX.X)
        p.v('tensor_scalar', pool2, pool2, 1.0 / 8192.0, None, OP.mult)
        for j in range(8):
            p.mm(psL[0:8, 0:2], gwc[:, j * 8:(j + 1) * 8], pool2[:, j * 2:(j + 1) * 2],
                 start=(j == 0), stop=(j == 7), wait=(j == 0))
        p.v('tensor_scalar', lgf, psL[0:8, 0:2], gbc[0:8, 0:1], None, OP.add)
        p.tr(psLT, lgf[0:8, 0:2], I_f[0:8, 0:8])
        p.s('copy', lg2, psLT[0:2, 0:8])
        p.v('reduce_max', m1, lg2, axis=AX.X)
        p.v('tensor_scalar', mneg, m1, -1.0, None, OP.mult)
        p.v('tensor_scalar', eqm, lg2, m1, None, OP.is_ge)
        p.v('reduce_sum', cnt, eqm, axis=AX.X)
        p.v('tensor_scalar', big, eqm, 1e30, None, OP.mult)
        p.v('tensor_sub', wo_, lg2, big)
        p.v('reduce_max', m2, wo_, axis=AX.X)
        p.v('tensor_scalar', c2, cnt, 1.5, None, OP.is_ge)
        p.v('tensor_sub', min2, m1, m2)
        p.v('tensor_mul', min2, min2, c2)
        p.v('tensor_add', min2, min2, m2)
        p.v('tensor_scalar', keep, lg2, min2, None, OP.is_ge)
        p.s('activation', ex, lg2, AF.Exp, bias=mneg)
        p.v('tensor_mul', ex, ex, keep)
        p.v('reduce_sum', den, ex, axis=AX.X)
        p.v('reciprocal', rden, den)
        p.v('tensor_scalar', gat, ex, rden, None, OP.mult)
        p.tr(psGT, gat[0:2, 0:8], I_f[0:2, 0:2])
        p.s('copy', gT8, psGT[0:8, 0:2])
        for e in range(8):
            p.dma(grow[0:1, e * 2:(e + 1) * 2], gT8[e:e + 1, 0:2])
        for e in range(8):
            p.mm(psGC, oner, grow[0:1, e * 2 + b_core:e * 2 + b_core + 1])
            p.s('copy', gcols[:, e:e + 1], psGC)
        for ci in range(NCH):
            c0 = ci * 512
            for k in range(8):
                p.dma(aob[:, k * 512:(k + 1) * 512], aoT[k * 128:(k + 1) * 128, c0:c0 + 512])
                p.dma(xnb[:, k * 512:(k + 1) * 512], xn1T[k * 128:(k + 1) * 128, c0:c0 + 512])
            for m0 in range(0, 8, 2):
                for (pi, ph) in ((0, psX), (1, psX2)):
                    m = m0 + pi
                    for k in range(8):
                        p.mm(ph, Wo_bf[:, k * 1024 + m * 128:k * 1024 + (m + 1) * 128],
                             aob[:, k * 512:(k + 1) * 512], start=(k == 0), stop=(k == 7),
                             wait=(k == 0 and pi == 0))
                p.s('copy', s1f[:, m0 * 512:(m0 + 1) * 512], psX)
                p.s('copy', s1f[:, (m0 + 1) * 512:(m0 + 2) * 512], psX2, wait=False)
            for e in range(8):
                for k in range(8):
                    p.dma(wtmp, expT[e * 1024 + k * 128:e * 1024 + (k + 1) * 128, :])
                    p.s('copy', ew_bf[:, k * 1024:(k + 1) * 1024], wtmp)
                for m0 in range(0, 8, 2):
                    for (pi, ph) in ((0, psX), (1, psX2)):
                        m = m0 + pi
                        for k in range(8):
                            p.mm(ph, ew_bf[:, k * 1024 + m * 128:k * 1024 + (m + 1) * 128],
                                 xnb[:, k * 512:(k + 1) * 512], start=(k == 0),
                                 stop=(k == 7), wait=(k == 0 and pi == 0))
                    p.s('activation', eo, psX, AF.Gelu,
                        bias=ebc[:, m0 * 8 + e:m0 * 8 + e + 1])
                    p.s('activation', eo2, psX2, AF.Gelu,
                        bias=ebc[:, (m0 + 1) * 8 + e:(m0 + 1) * 8 + e + 1], wait=False)
                    for (m, eox, w8) in ((m0, eo, True), (m0 + 1, eo2, False)):
                        if e == 0:
                            p.v('tensor_scalar', accf[:, m * 512:(m + 1) * 512], eox,
                                gcols[:, 0:1], None, OP.mult, wait=w8)
                        else:
                            p.v('scalar_tensor_tensor', accf[:, m * 512:(m + 1) * 512], eox,
                                gcols[:, e:e + 1], accf[:, m * 512:(m + 1) * 512],
                                OP.mult, OP.add, wait=w8)
            for m in range(8):
                p.s('activation', g1, accf[:, m * 512:(m + 1) * 512], AF.Silu)
                p.v('tensor_mul', g1, g1, s1f[:, m * 512:(m + 1) * 512])
                p.dma(x1f, x1T[m * 128:(m + 1) * 128, c0:c0 + 512])
                p.v('tensor_add', g1, g1, x1f)
                p.dma(outT_o[m * 128:(m + 1) * 128, c0:c0 + 512], g1)
        p.finish()
    return nc


def _f(a):
    return np.ascontiguousarray(a, dtype=np.float32)


def kernel(**inp):
    x = _f(inp['L0_attn_x'])
    xT = np.ascontiguousarray(x.reshape(NT, D).T)
    ones128 = np.ones((128, 1), f32)
    onesrow = np.ones((1, 128), f32)
    I128 = np.eye(128, dtype=f32)
    I128b = np.eye(128, dtype=BF)

    W_inT = np.ascontiguousarray(_f(inp['L0_attn_W_in']).T)
    W_xT = np.ascontiguousarray(_f(inp['L0_attn_W_x']).T)
    W_dtT = np.ascontiguousarray(_f(inp['L0_attn_W_dt']).T)
    convw = np.ascontiguousarray(_f(inp['L0_attn_conv_w'])[:, 0, :])
    convb = _f(inp['L0_attn_conv_b']).reshape(DI, 1)
    bdt = _f(inp['L0_attn_b_dt']).reshape(DI, 1)
    nc = build_KA()
    maps = []
    for c in range(NCORE):
        xin = np.zeros((D, TCORE + 3), f32)
        xin[:, 3:] = xT[:, c * TCORE:(c + 1) * TCORE]
        if c not in (0, 4):
            xin[:, :3] = xT[:, c * TCORE - 3:c * TCORE]
        maps.append(dict(xin=xin, WinT=W_inT, WxT=W_xT, WdtT=W_dtT, convw=convw,
                         convb=convb, bdt=bdt, ones128=ones128, onesrow=onesrow))
    rA = _run(nc, maps)
    xnT = np.concatenate([r['xnT'] for r in rA], axis=1)
    xcT = np.concatenate([r['xcT'] for r in rA], axis=1)
    dtT = np.concatenate([r['dtT'] for r in rA], axis=1)
    resT = np.concatenate([r['resT'] for r in rA], axis=1)
    BTf = np.concatenate([r['BT'] for r in rA], axis=1)
    CTf = np.concatenate([r['CT'] for r in rA], axis=1)

    At = np.ascontiguousarray(np.tile(_f(inp['L0_attn_A_log']).reshape(1, 16), (128, 1)))
    Dvf = _f(inp['L0_attn_D']).reshape(DI, 1)
    nc = build_KB()
    maps = []
    for c in range(NCORE):
        r0 = c * 256
        maps.append(dict(dtT=np.ascontiguousarray(dtT[r0:r0 + 256]),
                         xcT=np.ascontiguousarray(xcT[r0:r0 + 256]),
                         resT=np.ascontiguousarray(resT[r0:r0 + 256]),
                         BT=BTf, CT=CTf, At=At,
                         Dv=np.ascontiguousarray(Dvf[r0:r0 + 256]),
                         onesrow=onesrow))
    rB = _run(nc, maps)
    y2T = np.concatenate([r['y2T'] for r in rB], axis=0)

    W_outT = np.ascontiguousarray(_f(inp['L0_attn_W_out']).T)
    gam = _f(inp['L0_ffn_norm_weight']).reshape(D, 1)
    routT = np.ascontiguousarray(_f(inp['L0_ffn_router_w']).T)
    nc = build_KC()
    maps = []
    for c in range(NCORE):
        cs = slice(c * TCORE, (c + 1) * TCORE)
        maps.append(dict(y2T=np.ascontiguousarray(y2T[:, cs]), WoutT=W_outT,
                         xnT=np.ascontiguousarray(xnT[:, cs]), gam=gam, routT=routT,
                         I128=I128, ones128=ones128, onesrow=onesrow))
    rC = _run(nc, maps)
    sT = [r['sT'] for r in rC]
    normT = [r['normT'] for r in rC]
    combT = np.concatenate([r['combT'] for r in rC], axis=1)

    nc = build_KD1()
    maps = [dict(normT=normT[c], sw1=_f(inp['L0_ffn_sw1']), sw2=_f(inp['L0_ffn_sw2']),
                 sb1=_f(inp['L0_ffn_sb1']).reshape(4096, 1),
                 sb2=_f(inp['L0_ffn_sb2']).reshape(D, 1)) for c in range(NCORE)]
    rD1 = _run(nc, maps)
    sharedT = [r['sharedT'] for r in rD1]

    ew1 = _f(inp['L0_ffn_ew1'])
    ew2 = _f(inp['L0_ffn_ew2'])
    eb1 = _f(inp['L0_ffn_eb1'])
    eb2 = _f(inp['L0_ffn_eb2'])
    normhalf = [np.concatenate([normT[c] for c in range(4)], axis=1),
                np.concatenate([normT[c] for c in range(4, 8)], axis=1)]
    routhalf = []
    nc = build_KD2()
    for half in range(2):
        cs = slice(half * 8192, (half + 1) * 8192)
        maps = []
        for e in range(NCORE):
            maps.append(dict(normT=normhalf[half], ew1=np.ascontiguousarray(ew1[e]),
                             ew2=np.ascontiguousarray(ew2[e]),
                             eb1=np.ascontiguousarray(eb1[e]).reshape(4096, 1),
                             eb2=np.ascontiguousarray(eb2[e]).reshape(D, 1),
                             combr=np.ascontiguousarray(combT[e:e + 1, cs]),
                             onesrow=onesrow))
        rD2 = _run(nc, maps)
        routhalf.append([r['routT'] for r in rD2])

    nc = build_KDsum()
    maps = []
    for c in range(NCORE):
        half = c // 4
        lo = (c % 4) * TCORE
        m = {f'r{e}': np.ascontiguousarray(routhalf[half][e][:, lo:lo + TCORE])
             for e in range(8)}
        m['sharedT'] = sharedT[c]
        m['sT'] = sT[c]
        m['x0T'] = np.ascontiguousarray(xT[:, c * TCORE:(c + 1) * TCORE])
        maps.append(m)
    rDs = _run(nc, maps)
    x1T = [r['x1T'] for r in rDs]

    DH = 64
    inv = 1.0 / (10000.0 ** (np.arange(0, DH, 2, dtype=np.float64) / DH))
    fr = np.arange(8192, dtype=np.float64)[:, None] * inv[None, :]
    cosT = np.cos(fr).T.astype(f32)
    sinT = np.sin(fr).T.astype(f32)
    cosB = np.ascontiguousarray(np.tile(cosT, (4, 1)))
    sinB = np.ascontiguousarray(np.concatenate([-sinT, sinT, -sinT, sinT], axis=0))
    nc = build_KE()
    maps = []
    for c in range(NCORE):
        t0 = (c * TCORE) % 8192
        maps.append(dict(x1T=x1T[c], Wqkv=_f(inp['L1_attn_Wqkv']),
                         cosB=np.ascontiguousarray(cosB[:, t0:t0 + TCORE]),
                         sinB=np.ascontiguousarray(sinB[:, t0:t0 + TCORE]),
                         I128b=I128b, ones128=ones128, onesrow=onesrow))
    rE = _run(nc, maps)
    qTf = np.concatenate([r['qT'] for r in rE], axis=1)
    kTf = np.concatenate([r['kT'] for r in rE], axis=1)
    vRf = np.concatenate([r['vR'] for r in rE], axis=0)
    xn1T = [r['xn1T'] for r in rE]
    poolP = np.stack([np.ascontiguousarray(r['pool'].T).reshape(D) for r in rE], axis=0)

    TRI = np.zeros((128, 4 * 512), f32)
    for j in range(4):
        blk = np.where(np.arange(128)[:, None] > (j * 128 + np.arange(512))[None, :] - j * 0,
                       0.0, 0.0)
        r = np.arange(128)[:, None]
        qq = np.arange(512)[None, :]
        mask = (r + j * 128) > qq
        TRI[:, j * 512:(j + 1) * 512] = np.where(mask, -1e30, 0.0)
    nc = build_KF()
    maps = []
    for c in range(NCORE):
        qs, ks, vs = [], [], []
        for u in range(4):
            pair = c * 4 + u
            b, h = pair // 16, pair % 16
            qs.append(qTf[h * 64:(h + 1) * 64, b * 8192:(b + 1) * 8192])
            ks.append(kTf[h * 64:(h + 1) * 64, b * 8192:(b + 1) * 8192])
            vs.append(vRf[b * 8192:(b + 1) * 8192, h * 64:(h + 1) * 64])
        maps.append(dict(qT=np.ascontiguousarray(np.concatenate(qs, axis=0)),
                         kT=np.ascontiguousarray(np.concatenate(ks, axis=0)),
                         vR=np.ascontiguousarray(np.concatenate(vs, axis=1)),
                         TRI=TRI, ones128=ones128, onesrow=onesrow))
    rF = _run(nc, maps)
    aoTf = np.zeros((D, NT), BF)
    for c in range(NCORE):
        ao = rF[c]['aoT']
        for u in range(4):
            pair = c * 4 + u
            b, h = pair // 16, pair % 16
            aoTf[h * 64:(h + 1) * 64, b * 8192:(b + 1) * 8192] = ao[u * 64:(u + 1) * 64]

    gwT = np.ascontiguousarray(_f(inp['L1_ffn_gate_w']).T)
    gb = _f(inp['L1_ffn_gate_b']).reshape(8, 1)
    expT = np.ascontiguousarray(_f(inp['L1_ffn_expert_w']).transpose(0, 2, 1)).reshape(8 * D, D)
    ebT = np.ascontiguousarray(_f(inp['L1_ffn_expert_b']).T)
    nc = build_KG(0)
    maps = []
    for c in range(NCORE):
        cs = slice(c * TCORE, (c + 1) * TCORE)
        b = c // 4
        pp = poolP if b == 0 else np.ascontiguousarray(poolP[[4, 5, 6, 7, 0, 1, 2, 3]])
        maps.append(dict(aoT=np.ascontiguousarray(aoTf[:, cs]), WoA=_f(inp['L1_attn_Wout']),
                         poolP=pp, gwT=gwT, gb=gb, expT=expT, ebT=ebT,
                         xn1T=xn1T[c], x1T=x1T[c], I128=I128, onesrow=onesrow))
    rG = _run(nc, maps)
    outT = np.concatenate([r['outT'] for r in rG], axis=1)
    out = np.ascontiguousarray(outT.T).reshape(2, 8192, D).astype(np.float32)
    return out


# version 06 (final)
# speedup vs baseline: 1.3328x; 1.0199x over previous
import numpy as np
import ml_dtypes
import concourse.bass as bass
import concourse.mybir as mybir
from concourse.bass_utils import run_bass_kernel_spmd
from contextlib import ExitStack

F32 = mybir.dt.float32
BF16 = mybir.dt.bfloat16
AF = mybir.ActivationFunctionType
OP = mybir.AluOpType
AX = mybir.AxisListType
BF = ml_dtypes.bfloat16
f32 = np.float32

NCORE = 8
D = 1024
DI = 2048
DS = 16
NT = 16384
TCORE = 2048
CH = 512
NCH = TCORE // CH


class Prog:

    def __init__(self, nc):
        self.nc = nc
        self.ops = []

    def op(self, eng, fn, inc=1, wait=True):
        self.ops.append((eng, fn, inc, wait))

    def dma(self, out, in_, **kw):
        self.op('sync', lambda e, o=out, i=in_, kw=kw: e.dma_start(out=o, in_=i, **kw), inc=16)

    def mm(self, out, lhsT, rhs, start=True, stop=True, wait=True):
        self.op('tensor',
                lambda e, o=out, l=lhsT, r=rhs, s1=start, s2=stop: e.matmul(o, l, r, start=s1, stop=s2),
                inc=1, wait=wait)

    def tr(self, out, in_, ident):
        self.op('tensor', lambda e, o=out, i=in_, d=ident: e.transpose(o, i, d))

    def v(self, method, *a, wait=True, **kw):
        self.op('vector', lambda e, m=method, a=a, kw=kw: getattr(e, m)(*a, **kw), wait=wait)

    def s(self, method, *a, wait=True, **kw):
        self.op('scalar', lambda e, m=method, a=a, kw=kw: getattr(e, m)(*a, **kw), wait=wait)

    def finish(self):
        nc = self.nc
        ops = self.ops
        with nc.semaphore("tok") as tok, nc.Block() as block:
            def mk(name):
                def body(eng):
                    cnt = 0
                    prev_en = None
                    for (en, fn, inc, wait) in ops:
                        if en == name:
                            if wait and cnt > 0 and not (
                                    prev_en == name and name in ('sync', 'tensor')):
                                eng.wait_ge(tok, cnt)
                            r = fn(eng)
                            if inc:
                                r.then_inc(tok, inc)
                        cnt += inc
                        prev_en = en
                    if name == 'sync':
                        eng.wait_ge(tok, cnt)
                return body
            block.sync(mk('sync'))
            block.tensor(mk('tensor'))
            block.vector(mk('vector'))
            block.scalar(mk('scalar'))
            if any(en == 'gpsimd' for (en, _, _, _) in ops):
                block.gpsimd(mk('gpsimd'))


def _run(nc, in_maps):
    import os
    tr = bool(os.environ.get('KPROF'))
    res = run_bass_kernel_spmd(nc, in_maps, core_ids=list(range(NCORE)), trace=tr)
    if tr:
        print('KPROF launch exec_ns:', res.exec_time_ns, flush=True)
    return res.results


def build_KA():
    nc = bass.Bass("TRN2", target_bir_lowering=False)
    xin_h = nc.dram_tensor("xin", [D, TCORE + 3], F32, kind="ExternalInput").ap()
    WinT = nc.dram_tensor("WinT", [D, 2 * DI], F32, kind="ExternalInput").ap()
    WxT = nc.dram_tensor("WxT", [DI, 33], F32, kind="ExternalInput").ap()
    WdtT = nc.dram_tensor("WdtT", [1, DI], F32, kind="ExternalInput").ap()
    convw = nc.dram_tensor("convw", [DI, 4], F32, kind="ExternalInput").ap()
    convb = nc.dram_tensor("convb", [DI, 1], F32, kind="ExternalInput").ap()
    bdt = nc.dram_tensor("bdt", [DI, 1], F32, kind="ExternalInput").ap()
    ones128 = nc.dram_tensor("ones128", [128, 1], F32, kind="ExternalInput").ap()
    onesrow = nc.dram_tensor("onesrow", [1, 128], F32, kind="ExternalInput").ap()
    xnT_o = nc.dram_tensor("xnT", [D, TCORE], F32, kind="ExternalOutput").ap()
    xcT_o = nc.dram_tensor("xcT", [DI, TCORE], F32, kind="ExternalOutput").ap()
    dtT_o = nc.dram_tensor("dtT", [DI, TCORE], F32, kind="ExternalOutput").ap()
    resT_o = nc.dram_tensor("resT", [DI, TCORE], F32, kind="ExternalOutput").ap()
    BT_o = nc.dram_tensor("BT", [16, TCORE], F32, kind="ExternalOutput").ap()
    CT_o = nc.dram_tensor("CT", [16, TCORE], F32, kind="ExternalOutput").ap()

    st = ExitStack()
    with st:
        sb = lambda n, s, d: st.enter_context(nc.sbuf_tensor(n, s, d)).ap()
        ps = lambda n, s: st.enter_context(nc.psum_tensor(n, s, F32)).ap()
        xin = sb("xinb", [128, 8 * 512], F32)
        sq = sb("sq", [128, 512], F32)
        xnf = sb("xnf", [128, 512], F32)
        xn_bf = sb("xn_bf", [128, 8 * 512], BF16)
        Win_bf = sb("Win_bf", [128, 8 * 4096], BF16)
        Wx_bf = sb("Wx_bf", [128, 16 * 33], BF16)
        Wdt_bf = sb("Wdt_bf", [1, DI], BF16)
        cwc = sb("cwc", [128, 16 * 4], F32)
        cbc = sb("cbc", [128, 16], F32)
        bdtc = sb("bdtc", [128, 16], F32)
        onec = sb("onec", [128, 1], F32)
        oner = sb("oner", [1, 128], F32)
        xpb = sb("xpb", [128, 16 * 515], F32)
        xcf = sb("xcf", [128, 512], F32)
        accv = sb("accv", [128, 512], F32)
        xc_bf = sb("xc_bf", [128, 16 * 512], BF16)
        resb = sb("resb", [128, 512], F32)
        ssmf = sb("ssmf", [33, 512], F32)
        dtrf = sb("dtrf", [1, 512], F32)
        dtr_bf = sb("dtr_bf", [1, 512], BF16)
        zf = sb("zf", [128, 512], F32)
        azf = sb("azf", [128, 512], F32)
        ef = sb("ef", [128, 512], F32)
        lf = sb("lf", [128, 512], F32)
        rf = sb("rf", [128, 512], F32)
        dtf = sb("dtf", [128, 512], F32)
        meanr = sb("meanr", [1, 512], F32)
        ex2r = sb("ex2r", [1, 512], F32)
        msqr = sb("msqr", [1, 512], F32)
        varr = sb("varr", [1, 512], F32)
        lnvr = sb("lnvr", [1, 512], F32)
        rstdr = sb("rstdr", [1, 512], F32)
        cst = sb("cst", [128, 2], F32)

        psS = ps("psS", [1, 512])
        psQ = ps("psQ", [1, 512])
        psM = ps("psM", [128, 512])
        psR = ps("psR", [128, 512])
        psX = ps("psX", [128, 512])
        psW = ps("psW", [33, 512])
        psD = ps("psD", [128, 512])

        p = Prog(nc)
        p.v('memset', cst[:, 0:1], 1e-5)
        p.v('memset', cst[:, 1:2], 1.0)
        p.dma(onec, ones128)
        p.dma(oner, onesrow)
        for k in range(8):
            p.dma(xin[:, 0:4096], WinT[k * 128:(k + 1) * 128, :])
            p.s('copy', Win_bf[:, k * 4096:(k + 1) * 4096], xin[:, 0:4096])
        for k in range(16):
            p.dma(xin[:, 0:33], WxT[k * 128:(k + 1) * 128, :])
            p.s('copy', Wx_bf[:, k * 33:(k + 1) * 33], xin[:, 0:33])
            p.dma(cwc[:, k * 4:(k + 1) * 4], convw[k * 128:(k + 1) * 128, :])
            p.dma(cbc[:, k:k + 1], convb[k * 128:(k + 1) * 128, :])
            p.dma(bdtc[:, k:k + 1], bdt[k * 128:(k + 1) * 128, :])
        for q in range(4):
            p.dma(xin[0:1, 0:512], WdtT[0:1, q * 512:(q + 1) * 512])
            p.s('copy', Wdt_bf[0:1, q * 512:(q + 1) * 512], xin[0:1, 0:512])

        def chunk(co, W, tail_off, main_ci):
            for k in range(8):
                p.dma(xin[:, k * 512:k * 512 + W], xin_h[k * 128:(k + 1) * 128, co:co + W])
            for k in range(8):
                p.mm(psS[0:1, 0:W], onec, xin[:, k * 512:k * 512 + W],
                     start=(k == 0), stop=(k == 7), wait=(k == 0))
            for k in range(8):
                p.s('activation', sq[:, 0:W], xin[:, k * 512:k * 512 + W], AF.Square)
                p.mm(psQ[0:1, 0:W], onec, sq[:, 0:W], start=(k == 0), stop=(k == 7))
            p.s('activation', meanr[0:1, 0:W], psS[0:1, 0:W], AF.Copy, scale=1.0 / 1024.0)
            p.s('activation', ex2r[0:1, 0:W], psQ[0:1, 0:W], AF.Copy, scale=1.0 / 1024.0)
            p.v('tensor_mul', msqr[0:1, 0:W], meanr[0:1, 0:W], meanr[0:1, 0:W])
            p.v('tensor_sub', varr[0:1, 0:W], ex2r[0:1, 0:W], msqr[0:1, 0:W])
            p.s('activation', lnvr[0:1, 0:W], varr[0:1, 0:W], AF.Ln, bias=cst[0:1, 0:1])
            p.s('activation', rstdr[0:1, 0:W], lnvr[0:1, 0:W], AF.Exp, scale=-0.5)
            p.mm(psM[:, 0:W], oner, meanr[0:1, 0:W])
            p.mm(psR[:, 0:W], oner, rstdr[0:1, 0:W])
            for k in range(8):
                p.v('tensor_sub', xnf[:, 0:W], xin[:, k * 512:k * 512 + W], psM[:, 0:W])
                p.v('tensor_mul', xnf[:, 0:W], xnf[:, 0:W], psR[:, 0:W])
                p.s('copy', xn_bf[:, k * 512:k * 512 + W], xnf[:, 0:W])
                if main_ci is not None:
                    p.dma(xnT_o[k * 128:(k + 1) * 128, main_ci * 512:main_ci * 512 + W], xnf[:, 0:W])
            n_m = 32 if main_ci is not None else 16
            for m in range(n_m):
                for k in range(8):
                    p.mm(psX[:, 0:W], Win_bf[:, k * 4096 + m * 128:k * 4096 + (m + 1) * 128],
                         xn_bf[:, k * 512:k * 512 + W], start=(k == 0), stop=(k == 7), wait=(k == 0))
                if m < 16:
                    p.s('copy', xpb[:, m * 515 + tail_off:m * 515 + tail_off + W], psX[:, 0:W])
                else:
                    p.s('copy', resb[:, 0:W], psX[:, 0:W])
                    p.dma(resT_o[(m - 16) * 128:(m - 15) * 128, main_ci * 512:main_ci * 512 + W],
                          resb[:, 0:W])

        chunk(0, 3, 0, None)
        for ci in range(NCH):
            chunk(3 + ci * 512, 512, 3, ci)
            for j in range(16):
                src = xpb[:, j * 515:(j + 1) * 515]
                p.v('tensor_scalar', accv, src[:, 0:512], cwc[:, j * 4:j * 4 + 1], None, OP.mult)
                for jj in range(1, 4):
                    p.v('scalar_tensor_tensor', accv, src[:, jj:jj + 512],
                        cwc[:, j * 4 + jj:j * 4 + jj + 1], accv, OP.mult, OP.add)
                p.s('activation', xcf, accv, AF.Silu, bias=cbc[:, j:j + 1])
                p.dma(xcT_o[j * 128:(j + 1) * 128, ci * 512:(ci + 1) * 512], xcf)
                p.s('copy', xc_bf[:, j * 512:(j + 1) * 512], xcf)
                p.v('tensor_copy', src[:, 0:3], src[:, 512:515])
            for k in range(16):
                p.mm(psW[0:33, :], Wx_bf[:, k * 33:(k + 1) * 33], xc_bf[:, k * 512:(k + 1) * 512],
                     start=(k == 0), stop=(k == 15), wait=(k == 0))
            p.s('copy', ssmf, psW[0:33, :])
            p.dma(BT_o[0:16, ci * 512:(ci + 1) * 512], ssmf[0:16, :])
            p.dma(CT_o[0:16, ci * 512:(ci + 1) * 512], ssmf[16:32, :])
            p.dma(dtrf[0:1, :], ssmf[32:33, :])
            p.s('copy', dtr_bf, dtrf)
            for m in range(16):
                p.mm(psD, Wdt_bf[0:1, m * 128:(m + 1) * 128], dtr_bf)
                p.v('tensor_scalar', zf, psD, bdtc[:, m:m + 1], None, OP.add)
                p.s('activation', azf, zf, AF.Abs)
                p.s('activation', ef, azf, AF.Exp, scale=-1.0)
                p.s('activation', lf, ef, AF.Ln, bias=cst[:, 1:2])
                p.v('tensor_scalar', rf, zf, 0.0, None, OP.max)
                p.v('tensor_add', dtf, rf, lf)
                p.dma(dtT_o[m * 128:(m + 1) * 128, ci * 512:(ci + 1) * 512], dtf)
        p.finish()
    return nc


def build_KB():
    TC = 1024
    nc = bass.Bass("TRN2", target_bir_lowering=False)
    dtT = nc.dram_tensor("dtT", [256, NT], F32, kind="ExternalInput").ap()
    xcT = nc.dram_tensor("xcT", [256, NT], F32, kind="ExternalInput").ap()
    resT = nc.dram_tensor("resT", [256, NT], F32, kind="ExternalInput").ap()
    BT = nc.dram_tensor("BT", [16, NT], F32, kind="ExternalInput").ap()
    CT = nc.dram_tensor("CT", [16, NT], F32, kind="ExternalInput").ap()
    At = nc.dram_tensor("At", [128, 16], F32, kind="ExternalInput").ap()
    Dv = nc.dram_tensor("Dv", [256, 1], F32, kind="ExternalInput").ap()
    onesrow = nc.dram_tensor("onesrow", [1, 128], F32, kind="ExternalInput").ap()
    y2T = nc.dram_tensor("y2T", [256, NT], BF16, kind="ExternalOutput").ap()

    st = ExitStack()
    with st:
        sb = lambda n, s, d: st.enter_context(nc.sbuf_tensor(n, s, d)).ap()
        ps = lambda n, s: st.enter_context(nc.psum_tensor(n, s, F32)).ap()
        dt_t = [sb(f"dt{e}", [128, TC], F32) for e in range(2)]
        xc_t = [sb(f"xc{e}", [128, TC], F32) for e in range(2)]
        u_t = [sb(f"u{e}", [128, TC], F32) for e in range(2)]
        ya = [sb(f"ya{e}", [128, TC], F32) for e in range(2)]
        a1 = [sb(f"a1{e}", [128, TC], F32) for e in range(2)]
        a2 = [sb(f"a2{e}", [128, TC], F32) for e in range(2)]
        bv = [sb(f"bv{e}", [128, TC], F32) for e in range(2)]
        hh = [sb(f"hh{e}", [128, TC], F32) for e in range(2)]
        tmp = [sb(f"tmp{e}", [128, TC], F32) for e in range(2)]
        res_t = sb("res_t", [128, TC], F32)
        y2b = sb("y2b", [128, TC], BF16)
        brow = sb("brow", [1, TC], F32)
        crow = sb("crow", [1, TC], F32)
        Acol = sb("Acol", [128, 16], F32)
        Dcol = sb("Dcol", [128, 2], F32)
        carry = sb("carry", [128, 32], F32)
        oner = sb("oner", [1, 128], F32)
        psB = ps("psB", [128, TC])
        psC = ps("psC", [128, TC])

        p = Prog(nc)
        p.dma(oner, onesrow)
        p.dma(Acol, At)
        p.s('activation', Acol, Acol, AF.Exp)
        p.v('tensor_scalar', Acol, Acol, -1.0, None, OP.mult)
        for e in range(2):
            p.dma(Dcol[:, e:e + 1], Dv[e * 128:(e + 1) * 128, :])
        for b in range(2):
            p.v('memset', carry, 0.0)
            for ch in range(8192 // TC):
                c0 = b * 8192 + ch * TC
                for e in range(2):
                    p.dma(dt_t[e], dtT[e * 128:(e + 1) * 128, c0:c0 + TC])
                    p.dma(xc_t[e], xcT[e * 128:(e + 1) * 128, c0:c0 + TC])
                p.v('tensor_mul', u_t[0], dt_t[0], xc_t[0])
                p.v('tensor_mul', u_t[1], dt_t[1], xc_t[1], wait=False)
                for n in range(16):
                    p.dma(brow, BT[n:n + 1, c0:c0 + TC])
                    p.dma(crow, CT[n:n + 1, c0:c0 + TC])
                    for hf in range(2):
                        sl = slice(hf * 512, (hf + 1) * 512)
                        p.mm(psB[:, sl], oner, brow[0:1, sl], wait=(hf == 0))
                        p.mm(psC[:, sl], oner, crow[0:1, sl], wait=False)
                    p.v('tensor_scalar', a1[0], dt_t[0], Acol[:, n:n + 1], None, OP.mult)
                    p.v('tensor_scalar', a1[1], dt_t[1], Acol[:, n:n + 1], None, OP.mult,
                        wait=False)
                    p.s('activation', a2[0], a1[0], AF.Exp)
                    p.s('activation', a2[1], a1[1], AF.Exp, wait=False)
                    first = True
                    for e in range(2):
                        for hf in range(2):
                            sl = slice(hf * 512, (hf + 1) * 512)
                            p.v('tensor_tensor', bv[e][:, sl], u_t[e][:, sl], psB[:, sl],
                                OP.mult, wait=first)
                            first = False
                    p.v('tensor_tensor_scan', hh[0], a2[0], bv[0],
                        carry[:, n:n + 1], OP.mult, OP.add)
                    p.v('tensor_tensor_scan', hh[1], a2[1], bv[1],
                        carry[:, 16 + n:17 + n], OP.mult, OP.add, wait=False)
                    p.s('copy', carry[:, n:n + 1], hh[0][:, TC - 1:TC])
                    p.s('copy', carry[:, 16 + n:17 + n], hh[1][:, TC - 1:TC], wait=False)
                    first = True
                    for e in range(2):
                        for hf in range(2):
                            sl = slice(hf * 512, (hf + 1) * 512)
                            dst = ya[e] if n == 0 else tmp[e]
                            p.v('tensor_tensor', dst[:, sl], hh[e][:, sl], psC[:, sl],
                                OP.mult, wait=first)
                            first = False
                    if n > 0:
                        p.v('tensor_add', ya[0], ya[0], tmp[0])
                        p.v('tensor_add', ya[1], ya[1], tmp[1], wait=False)
                for e in range(2):
                    p.v('scalar_tensor_tensor', ya[e], xc_t[e], Dcol[:, e:e + 1], ya[e],
                        OP.mult, OP.add)
                    p.dma(res_t, resT[e * 128:(e + 1) * 128, c0:c0 + TC])
                    p.v('tensor_add', ya[e], ya[e], res_t)
                    p.s('activation', y2b, ya[e], AF.Silu)
                    p.dma(y2T[e * 128:(e + 1) * 128, c0:c0 + TC], y2b)
        p.finish()
    return nc


def build_KC():
    nc = bass.Bass("TRN2", target_bir_lowering=False)
    y2T = nc.dram_tensor("y2T", [DI, TCORE], BF16, kind="ExternalInput").ap()
    WoutT = nc.dram_tensor("WoutT", [DI, D], F32, kind="ExternalInput").ap()
    xnT = nc.dram_tensor("xnT", [D, TCORE], F32, kind="ExternalInput").ap()
    gam = nc.dram_tensor("gam", [D, 1], F32, kind="ExternalInput").ap()
    routT = nc.dram_tensor("routT", [D, 8], F32, kind="ExternalInput").ap()
    I128 = nc.dram_tensor("I128", [128, 128], F32, kind="ExternalInput").ap()
    ones128 = nc.dram_tensor("ones128", [128, 1], F32, kind="ExternalInput").ap()
    onesrow = nc.dram_tensor("onesrow", [1, 128], F32, kind="ExternalInput").ap()
    sT_o = nc.dram_tensor("sT", [D, TCORE], F32, kind="ExternalOutput").ap()
    normT_o = nc.dram_tensor("normT", [D, TCORE], BF16, kind="ExternalOutput").ap()
    combT_o = nc.dram_tensor("combT", [8, TCORE], F32, kind="ExternalOutput").ap()

    st = ExitStack()
    with st:
        sb = lambda n, s, d: st.enter_context(nc.sbuf_tensor(n, s, d)).ap()
        ps = lambda n, s: st.enter_context(nc.psum_tensor(n, s, F32)).ap()
        Wo_bf = sb("Wo_bf", [128, 16 * 1024], BF16)
        ro_f = sb("ro_f", [128, 8 * 8], F32)
        n32 = sb("n32", [128, 8 * 512], F32)
        wtmp = sb("wtmp", [128, 1024], F32)
        y2_bf = sb("y2_bf", [128, 16 * 512], BF16)
        xn = sb("xn", [128, 8 * 512], F32)
        sq = sb("sq", [128, 512], F32)
        sf = sb("sf", [128, 512], F32)
        nf = sb("nf", [128, 512], F32)
        n_bf = sb("n_bf", [128, 8 * 512], BF16)
        gcol = sb("gcol", [128, 8], F32)
        onec = sb("onec", [128, 1], F32)
        oner = sb("oner", [1, 128], F32)
        I_f = sb("I_f", [128, 128], F32)
        ex2r = sb("ex2r", [1, 512], F32)
        lnvr = sb("lnvr", [1, 512], F32)
        rstdr = sb("rstdr", [1, 512], F32)
        probs = sb("probs", [8, 512], F32)
        tpro = sb("tpro", [128, 8], F32)
        m1 = sb("m1", [128, 1], F32)
        eqm = sb("eqm", [128, 8], F32)
        cnt = sb("cnt", [128, 1], F32)
        big = sb("big", [128, 8], F32)
        wo_ = sb("wo_", [128, 8], F32)
        m2 = sb("m2", [128, 1], F32)
        c2 = sb("c2", [128, 1], F32)
        min2 = sb("min2", [128, 1], F32)
        keep = sb("keep", [128, 8], F32)
        den = sb("den", [128, 1], F32)
        rden = sb("rden", [128, 1], F32)
        combt = sb("combt", [128, 8], F32)
        combf = sb("combf", [8, 512], F32)
        cst = sb("cst", [128, 1], F32)
        psS = ps("psS", [128, 512])
        psQ = ps("psQ", [1, 512])
        psR = ps("psR", [128, 512])
        psP = ps("psP", [8, 512])
        psT = ps("psT", [128, 8])
        psCT = ps("psCT", [8, 128])

        p = Prog(nc)
        p.v('memset', cst, 1e-8)
        p.dma(onec, ones128)
        p.dma(oner, onesrow)
        p.dma(I_f, I128)
        for k in range(16):
            p.dma(wtmp, WoutT[k * 128:(k + 1) * 128, :])
            p.s('copy', Wo_bf[:, k * 1024:(k + 1) * 1024], wtmp)
        for k in range(8):
            p.dma(ro_f[:, k * 8:(k + 1) * 8], routT[k * 128:(k + 1) * 128, :])
            p.dma(gcol[:, k:k + 1], gam[k * 128:(k + 1) * 128, :])
        for ci in range(NCH):
            c0 = ci * 512
            for k in range(16):
                p.dma(y2_bf[:, k * 512:(k + 1) * 512], y2T[k * 128:(k + 1) * 128, c0:c0 + 512])
            for m in range(8):
                for k in range(16):
                    p.mm(psS, Wo_bf[:, k * 1024 + m * 128:k * 1024 + (m + 1) * 128],
                         y2_bf[:, k * 512:(k + 1) * 512], start=(k == 0), stop=(k == 15),
                         wait=(k == 0))
                p.s('copy', sf, psS)
                p.dma(sT_o[m * 128:(m + 1) * 128, c0:c0 + 512], sf)
            for k in range(8):
                p.dma(xn[:, k * 512:(k + 1) * 512], xnT[k * 128:(k + 1) * 128, c0:c0 + 512])
            for k in range(8):
                p.s('activation', sq, xn[:, k * 512:(k + 1) * 512], AF.Square)
                p.mm(psQ[0:1, :], onec, sq, start=(k == 0), stop=(k == 7))
            p.s('activation', ex2r[0:1, :], psQ[0:1, :], AF.Copy, scale=1.0 / 1024.0)
            p.s('activation', lnvr[0:1, :], ex2r[0:1, :], AF.Ln, bias=cst[0:1, 0:1])
            p.s('activation', rstdr[0:1, :], lnvr[0:1, :], AF.Exp, scale=-0.5)
            p.mm(psR, oner, rstdr[0:1, :])
            for k in range(8):
                p.v('tensor_tensor', nf, xn[:, k * 512:(k + 1) * 512], psR, OP.mult)
                p.v('tensor_scalar', n32[:, k * 512:(k + 1) * 512], nf, gcol[:, k:k + 1],
                    None, OP.mult)
                p.s('copy', n_bf[:, k * 512:(k + 1) * 512], n32[:, k * 512:(k + 1) * 512])
                p.dma(normT_o[k * 128:(k + 1) * 128, c0:c0 + 512], n_bf[:, k * 512:(k + 1) * 512])
            for k in range(8):
                p.mm(psP[0:8, :], ro_f[:, k * 8:(k + 1) * 8], n32[:, k * 512:(k + 1) * 512],
                     start=(k == 0), stop=(k == 7), wait=(k == 0))
            p.s('activation', probs, psP[0:8, :], AF.Sigmoid)
            for j in range(4):
                p.tr(psT, probs[0:8, j * 128:(j + 1) * 128], I_f[0:8, 0:8])
                p.s('copy', tpro, psT)
                p.v('reduce_max', m1, tpro, axis=AX.X)
                p.v('tensor_scalar', eqm, tpro, m1, None, OP.is_ge)
                p.v('reduce_sum', cnt, eqm, axis=AX.X)
                p.v('tensor_scalar', big, eqm, 1e30, None, OP.mult)
                p.v('tensor_sub', wo_, tpro, big)
                p.v('reduce_max', m2, wo_, axis=AX.X)
                p.v('tensor_scalar', c2, cnt, 1.5, None, OP.is_ge)
                p.v('tensor_sub', min2, m1, m2)
                p.v('tensor_mul', min2, min2, c2)
                p.v('tensor_add', min2, min2, m2)
                p.v('tensor_scalar', keep, tpro, min2, None, OP.is_ge)
                p.v('tensor_add', den, m1, min2)
                p.v('tensor_scalar', den, den, 1e-6, None, OP.add)
                p.v('reciprocal', rden, den)
                p.v('tensor_mul', combt, tpro, keep)
                p.v('tensor_scalar', combt, combt, rden, None, OP.mult)
                p.tr(psCT, combt, I_f)
                p.s('copy', combf[0:8, j * 128:(j + 1) * 128], psCT[0:8, :])
            p.dma(combT_o[0:8, c0:c0 + 512], combf)
        p.finish()
    return nc


def build_KD1():
    nc = bass.Bass("TRN2", target_bir_lowering=False)
    normT = nc.dram_tensor("normT", [D, TCORE], BF16, kind="ExternalInput").ap()
    sw1 = nc.dram_tensor("sw1", [D, 4096], F32, kind="ExternalInput").ap()
    sw2 = nc.dram_tensor("sw2", [4096, D], F32, kind="ExternalInput").ap()
    sb1 = nc.dram_tensor("sb1", [4096, 1], F32, kind="ExternalInput").ap()
    sb2 = nc.dram_tensor("sb2", [D, 1], F32, kind="ExternalInput").ap()
    sharedT_o = nc.dram_tensor("sharedT", [D, TCORE], F32, kind="ExternalOutput").ap()

    st = ExitStack()
    with st:
        sb = lambda n, s, d: st.enter_context(nc.sbuf_tensor(n, s, d)).ap()
        ps = lambda n, s: st.enter_context(nc.psum_tensor(n, s, F32)).ap()
        w1_bf = sb("w1_bf", [128, 8 * 4096], BF16)
        w2_bf = sb("w2_bf", [128, 32 * 1024], BF16)
        wtmp = sb("wtmp", [128, 1024], F32)
        nb = sb("nb", [128, 8 * 512], BF16)
        he = sb("he", [128, 32 * 512], BF16)
        b1c = sb("b1c", [128, 32], F32)
        b2c = sb("b2c", [128, 8], F32)
        outf = sb("outf", [128, 512], F32)
        outf2 = sb("outf2", [128, 512], F32)
        psH = ps("psH", [128, 512])
        psH2 = ps("psH2", [128, 512])
        psO = ps("psO", [128, 512])
        psO2 = ps("psO2", [128, 512])

        p = Prog(nc)
        for k in range(8):
            for q in range(4):
                p.dma(wtmp, sw1[k * 128:(k + 1) * 128, q * 1024:(q + 1) * 1024])
                p.s('copy', w1_bf[:, k * 4096 + q * 1024:k * 4096 + (q + 1) * 1024], wtmp)
        for k in range(32):
            p.dma(wtmp, sw2[k * 128:(k + 1) * 128, :])
            p.s('copy', w2_bf[:, k * 1024:(k + 1) * 1024], wtmp)
        for m in range(32):
            p.dma(b1c[:, m:m + 1], sb1[m * 128:(m + 1) * 128, :])
        for m in range(8):
            p.dma(b2c[:, m:m + 1], sb2[m * 128:(m + 1) * 128, :])
        for ci in range(NCH):
            c0 = ci * 512
            for k in range(8):
                p.dma(nb[:, k * 512:(k + 1) * 512], normT[k * 128:(k + 1) * 128, c0:c0 + 512])
            for m0 in range(0, 32, 2):
                for (pi, ph) in ((0, psH), (1, psH2)):
                    m = m0 + pi
                    for k in range(8):
                        p.mm(ph, w1_bf[:, k * 4096 + m * 128:k * 4096 + (m + 1) * 128],
                             nb[:, k * 512:(k + 1) * 512], start=(k == 0), stop=(k == 7),
                             wait=(k == 0 and pi == 0))
                p.s('activation', he[:, m0 * 512:(m0 + 1) * 512], psH, AF.Relu,
                    bias=b1c[:, m0:m0 + 1])
                p.s('activation', he[:, (m0 + 1) * 512:(m0 + 2) * 512], psH2, AF.Relu,
                    bias=b1c[:, m0 + 1:m0 + 2], wait=False)
            for m0 in range(0, 8, 2):
                for (pi, ph) in ((0, psO), (1, psO2)):
                    m = m0 + pi
                    for k in range(32):
                        p.mm(ph, w2_bf[:, k * 1024 + m * 128:k * 1024 + (m + 1) * 128],
                             he[:, k * 512:(k + 1) * 512], start=(k == 0), stop=(k == 31),
                             wait=(k == 0 and pi == 0))
                p.v('tensor_scalar', outf, psO, b2c[:, m0:m0 + 1], None, OP.add)
                p.v('tensor_scalar', outf2, psO2, b2c[:, m0 + 1:m0 + 2], None, OP.add,
                    wait=False)
                p.dma(sharedT_o[m0 * 128:(m0 + 1) * 128, c0:c0 + 512], outf)
                p.dma(sharedT_o[(m0 + 1) * 128:(m0 + 2) * 128, c0:c0 + 512], outf2)
        p.finish()
    return nc


def build_KD2():
    TH = 8192
    nc = bass.Bass("TRN2", target_bir_lowering=False)
    normT = nc.dram_tensor("normT", [D, TH], BF16, kind="ExternalInput").ap()
    ew1 = nc.dram_tensor("ew1", [D, 4096], F32, kind="ExternalInput").ap()
    ew2 = nc.dram_tensor("ew2", [4096, D], F32, kind="ExternalInput").ap()
    eb1 = nc.dram_tensor("eb1", [4096, 1], F32, kind="ExternalInput").ap()
    eb2 = nc.dram_tensor("eb2", [D, 1], F32, kind="ExternalInput").ap()
    combr = nc.dram_tensor("combr", [1, TH], F32, kind="ExternalInput").ap()
    onesrow = nc.dram_tensor("onesrow", [1, 128], F32, kind="ExternalInput").ap()
    routT_o = nc.dram_tensor("routT", [D, TH], F32, kind="ExternalOutput").ap()

    st = ExitStack()
    with st:
        sb = lambda n, s, d: st.enter_context(nc.sbuf_tensor(n, s, d)).ap()
        ps = lambda n, s: st.enter_context(nc.psum_tensor(n, s, F32)).ap()
        w1_bf = sb("w1_bf", [128, 8 * 4096], BF16)
        w2_bf = sb("w2_bf", [128, 32 * 1024], BF16)
        wtmp = sb("wtmp", [128, 1024], F32)
        nb = sb("nb", [128, 8 * 512], BF16)
        he = sb("he", [128, 32 * 512], BF16)
        b1c = sb("b1c", [128, 32], F32)
        b2c = sb("b2c", [128, 8], F32)
        crow = sb("crow", [1, 512], F32)
        gbc = sb("gbc", [128, 512], F32)
        outf = sb("outf", [128, 512], F32)
        outf2 = sb("outf2", [128, 512], F32)
        oner = sb("oner", [1, 128], F32)
        psH = ps("psH", [128, 512])
        psH2 = ps("psH2", [128, 512])
        psO = ps("psO", [128, 512])
        psO2 = ps("psO2", [128, 512])
        psG = ps("psG", [128, 512])

        p = Prog(nc)
        p.dma(oner, onesrow)
        for k in range(8):
            for q in range(4):
                p.dma(wtmp, ew1[k * 128:(k + 1) * 128, q * 1024:(q + 1) * 1024])
                p.s('copy', w1_bf[:, k * 4096 + q * 1024:k * 4096 + (q + 1) * 1024], wtmp)
        for k in range(32):
            p.dma(wtmp, ew2[k * 128:(k + 1) * 128, :])
            p.s('copy', w2_bf[:, k * 1024:(k + 1) * 1024], wtmp)
        for m in range(32):
            p.dma(b1c[:, m:m + 1], eb1[m * 128:(m + 1) * 128, :])
        for m in range(8):
            p.dma(b2c[:, m:m + 1], eb2[m * 128:(m + 1) * 128, :])
        for ci in range(TH // 512):
            c0 = ci * 512
            for k in range(8):
                p.dma(nb[:, k * 512:(k + 1) * 512], normT[k * 128:(k + 1) * 128, c0:c0 + 512])
            p.dma(crow, combr[0:1, c0:c0 + 512])
            p.mm(psG, oner, crow)
            p.s('copy', gbc, psG)
            for m0 in range(0, 32, 2):
                for (pi, ph) in ((0, psH), (1, psH2)):
                    m = m0 + pi
                    for k in range(8):
                        p.mm(ph, w1_bf[:, k * 4096 + m * 128:k * 4096 + (m + 1) * 128],
                             nb[:, k * 512:(k + 1) * 512], start=(k == 0), stop=(k == 7),
                             wait=(k == 0 and pi == 0))
                p.s('activation', he[:, m0 * 512:(m0 + 1) * 512], psH, AF.Relu,
                    bias=b1c[:, m0:m0 + 1])
                p.s('activation', he[:, (m0 + 1) * 512:(m0 + 2) * 512], psH2, AF.Relu,
                    bias=b1c[:, m0 + 1:m0 + 2], wait=False)
            for m0 in range(0, 8, 2):
                for (pi, ph) in ((0, psO), (1, psO2)):
                    m = m0 + pi
                    for k in range(32):
                        p.mm(ph, w2_bf[:, k * 1024 + m * 128:k * 1024 + (m + 1) * 128],
                             he[:, k * 512:(k + 1) * 512], start=(k == 0), stop=(k == 31),
                             wait=(k == 0 and pi == 0))
                p.v('scalar_tensor_tensor', outf, psO, b2c[:, m0:m0 + 1], gbc,
                    OP.add, OP.mult)
                p.v('scalar_tensor_tensor', outf2, psO2, b2c[:, m0 + 1:m0 + 2], gbc,
                    OP.add, OP.mult, wait=False)
                p.dma(routT_o[m0 * 128:(m0 + 1) * 128, c0:c0 + 512], outf)
                p.dma(routT_o[(m0 + 1) * 128:(m0 + 2) * 128, c0:c0 + 512], outf2)
        p.finish()
    return nc


def build_KDsum():
    nc = bass.Bass("TRN2", target_bir_lowering=False)
    rins = [nc.dram_tensor(f"r{e}", [D, TCORE], F32, kind="ExternalInput").ap() for e in range(8)]
    sharedT = nc.dram_tensor("sharedT", [D, TCORE], F32, kind="ExternalInput").ap()
    sT = nc.dram_tensor("sT", [D, TCORE], F32, kind="ExternalInput").ap()
    x0T = nc.dram_tensor("x0T", [D, TCORE], F32, kind="ExternalInput").ap()
    x1T_o = nc.dram_tensor("x1T", [D, TCORE], F32, kind="ExternalOutput").ap()

    st = ExitStack()
    with st:
        sb = lambda n, s, d: st.enter_context(nc.sbuf_tensor(n, s, d)).ap()
        acc = sb("acc", [128, 512], F32)
        t2 = sb("t2", [128, 512], F32)
        g = sb("g", [128, 512], F32)
        p = Prog(nc)
        for ci in range(NCH):
            c0 = ci * 512
            for m in range(8):
                rs = (m * 128, (m + 1) * 128)
                p.dma(acc, sharedT[rs[0]:rs[1], c0:c0 + 512])
                for e in range(8):
                    p.dma(t2, rins[e][rs[0]:rs[1], c0:c0 + 512])
                    p.v('tensor_add', acc, acc, t2)
                p.s('activation', g, acc, AF.Silu)
                p.dma(t2, sT[rs[0]:rs[1], c0:c0 + 512])
                p.v('tensor_mul', g, g, t2)
                p.dma(t2, x0T[rs[0]:rs[1], c0:c0 + 512])
                p.v('tensor_add', g, g, t2)
                p.dma(x1T_o[rs[0]:rs[1], c0:c0 + 512], g)
        p.finish()
    return nc


def build_KE():
    nc = bass.Bass("TRN2", target_bir_lowering=False)
    x1T = nc.dram_tensor("x1T", [D, TCORE], F32, kind="ExternalInput").ap()
    Wqkv = nc.dram_tensor("Wqkv", [D, 3072], F32, kind="ExternalInput").ap()
    cosB = nc.dram_tensor("cosB", [128, TCORE], F32, kind="ExternalInput").ap()
    sinB = nc.dram_tensor("sinB", [128, TCORE], F32, kind="ExternalInput").ap()
    I128b = nc.dram_tensor("I128b", [128, 128], BF16, kind="ExternalInput").ap()
    ones128 = nc.dram_tensor("ones128", [128, 1], F32, kind="ExternalInput").ap()
    onesrow = nc.dram_tensor("onesrow", [1, 128], F32, kind="ExternalInput").ap()
    qT_o = nc.dram_tensor("qT", [D, TCORE], BF16, kind="ExternalOutput").ap()
    kT_o = nc.dram_tensor("kT", [D, TCORE], BF16, kind="ExternalOutput").ap()
    vR_o = nc.dram_tensor("vR", [TCORE, D], BF16, kind="ExternalOutput").ap()
    xn1T_o = nc.dram_tensor("xn1T", [D, TCORE], BF16, kind="ExternalOutput").ap()
    pool_o = nc.dram_tensor("pool", [128, 8], F32, kind="ExternalOutput").ap()

    st = ExitStack()
    with st:
        sb = lambda n, s, d: st.enter_context(nc.sbuf_tensor(n, s, d)).ap()
        ps = lambda n, s, d=F32: st.enter_context(nc.psum_tensor(n, s, d)).ap()
        Wq_bf = sb("Wq_bf", [128, 8 * 3072], BF16)
        wtmp = sb("wtmp", [128, 1024], F32)
        xin = sb("xin", [128, 8 * 512], F32)
        sq = sb("sq", [128, 512], F32)
        xnf = sb("xnf", [128, 512], F32)
        xn_bf = sb("xn_bf", [128, 8 * 512], BF16)
        cosb = sb("cosb", [128, 2048], F32)
        sinb = sb("sinb", [128, 2048], F32)
        qf = sb("qf", [128, 512], F32)
        qsw = sb("qsw", [128, 512], F32)
        t1_ = sb("t1_", [128, 512], F32)
        t2_ = sb("t2_", [128, 512], F32)
        qbf = sb("qbf", [128, 512], BF16)
        vbf = sb("vbf", [128, 512], BF16)
        vrow = sb("vrow", [128, 128], BF16)
        poolacc = sb("poolacc", [128, 8], F32)
        pcol = sb("pcol", [128, 1], F32)
        onec = sb("onec", [128, 1], F32)
        oner = sb("oner", [1, 128], F32)
        I_b = sb("I_b", [128, 128], BF16)
        meanr = sb("meanr", [1, 512], F32)
        ex2r = sb("ex2r", [1, 512], F32)
        msqr = sb("msqr", [1, 512], F32)
        varr = sb("varr", [1, 512], F32)
        lnvr = sb("lnvr", [1, 512], F32)
        rstdr = sb("rstdr", [1, 512], F32)
        cst = sb("cst", [128, 1], F32)
        psS = ps("psS", [1, 512])
        psQ = ps("psQ", [1, 512])
        psM = ps("psM", [128, 512])
        psR = ps("psR", [128, 512])
        psX = ps("psX", [128, 512])
        psTb = ps("psTb", [128, 128], BF16)

        p = Prog(nc)
        p.v('memset', cst, 1e-5)
        p.dma(onec, ones128)
        p.dma(oner, onesrow)
        p.dma(I_b, I128b)
        p.dma(cosb, cosB)
        p.dma(sinb, sinB)
        p.v('memset', poolacc, 0.0)
        for k in range(8):
            for q in range(3):
                p.dma(wtmp, Wqkv[k * 128:(k + 1) * 128, q * 1024:(q + 1) * 1024])
                p.s('copy', Wq_bf[:, k * 3072 + q * 1024:k * 3072 + (q + 1) * 1024], wtmp)
        for ci in range(NCH):
            c0 = ci * 512
            for k in range(8):
                p.dma(xin[:, k * 512:(k + 1) * 512], x1T[k * 128:(k + 1) * 128, c0:c0 + 512])
            for k in range(8):
                p.mm(psS[0:1, :], onec, xin[:, k * 512:(k + 1) * 512], start=(k == 0),
                     stop=(k == 7), wait=(k == 0))
            for k in range(8):
                p.s('activation', sq, xin[:, k * 512:(k + 1) * 512], AF.Square)
                p.mm(psQ[0:1, :], onec, sq, start=(k == 0), stop=(k == 7))
            p.s('activation', meanr[0:1, :], psS[0:1, :], AF.Copy, scale=1.0 / 1024.0)
            p.s('activation', ex2r[0:1, :], psQ[0:1, :], AF.Copy, scale=1.0 / 1024.0)
            p.v('tensor_mul', msqr[0:1, :], meanr[0:1, :], meanr[0:1, :])
            p.v('tensor_sub', varr[0:1, :], ex2r[0:1, :], msqr[0:1, :])
            p.s('activation', lnvr[0:1, :], varr[0:1, :], AF.Ln, bias=cst[0:1, 0:1])
            p.s('activation', rstdr[0:1, :], lnvr[0:1, :], AF.Exp, scale=-0.5)
            p.mm(psM, oner, meanr[0:1, :])
            p.mm(psR, oner, rstdr[0:1, :])
            for k in range(8):
                p.v('tensor_sub', xnf, xin[:, k * 512:(k + 1) * 512], psM)
                p.v('tensor_mul', xnf, xnf, psR)
                p.v('reduce_sum', pcol, xnf, axis=AX.X)
                p.v('tensor_add', poolacc[:, k:k + 1], poolacc[:, k:k + 1], pcol)
                p.s('copy', xn_bf[:, k * 512:(k + 1) * 512], xnf)
                p.dma(xn1T_o[k * 128:(k + 1) * 128, c0:c0 + 512], xn_bf[:, k * 512:(k + 1) * 512])
            for m in range(24):
                for k in range(8):
                    p.mm(psX, Wq_bf[:, k * 3072 + m * 128:k * 3072 + (m + 1) * 128],
                         xn_bf[:, k * 512:(k + 1) * 512], start=(k == 0), stop=(k == 7),
                         wait=(k == 0))
                if m < 16:
                    p.s('copy', qf, psX)
                    for g4 in range(4):
                        p.dma(qsw[((g4 ^ 1) * 32):((g4 ^ 1) * 32 + 32), :],
                              qf[g4 * 32:g4 * 32 + 32, :])
                    p.v('tensor_mul', t1_, qf, cosb[:, c0:c0 + 512])
                    p.v('tensor_mul', t2_, qsw, sinb[:, c0:c0 + 512])
                    p.v('tensor_add', t1_, t1_, t2_)
                    p.s('copy', qbf, t1_)
                    dst = qT_o if m < 8 else kT_o
                    mm = m if m < 8 else m - 8
                    p.dma(dst[mm * 128:(mm + 1) * 128, c0:c0 + 512], qbf)
                else:
                    p.s('copy', vbf, psX)
                    for j in range(4):
                        p.tr(psTb, vbf[:, j * 128:(j + 1) * 128], I_b)
                        p.s('copy', vrow, psTb)
                        p.dma(vR_o[c0 + j * 128:c0 + (j + 1) * 128,
                                   (m - 16) * 128:(m - 15) * 128], vrow)
        p.dma(pool_o, poolacc)
        p.finish()
    return nc


def build_KF():
    T = 8192
    nc = bass.Bass("TRN2", target_bir_lowering=False)
    qT = nc.dram_tensor("qT", [256, T], BF16, kind="ExternalInput").ap()
    kT = nc.dram_tensor("kT", [256, T], BF16, kind="ExternalInput").ap()
    vR = nc.dram_tensor("vR", [T, 256], BF16, kind="ExternalInput").ap()
    TRI = nc.dram_tensor("TRI", [128, 4 * 512], F32, kind="ExternalInput").ap()
    ones128 = nc.dram_tensor("ones128", [128, 1], F32, kind="ExternalInput").ap()
    onesrow = nc.dram_tensor("onesrow", [1, 128], F32, kind="ExternalInput").ap()
    aoT_o = nc.dram_tensor("aoT", [256, T], BF16, kind="ExternalOutput").ap()

    st = ExitStack()
    with st:
        sb = lambda n, s, d: st.enter_context(nc.sbuf_tensor(n, s, d)).ap()
        ps = lambda n, s: st.enter_context(nc.psum_tensor(n, s, F32)).ap()
        qfull = sb("qfull", [64, T], BF16)
        kfull = sb("kfull", [64, T], BF16)
        Vt = sb("Vt", [128, 64 * 64], BF16)
        tri = sb("tri", [128, 4 * 512], F32)
        ptef = sb("ptef", [128, 512], F32)
        ptbf = sb("ptbf", [128, 512], BF16)
        rs_ = sb("rs_", [1, 512], F32)
        rrs = sb("rrs", [1, 512], F32)
        of_ = sb("of_", [64, 512], F32)
        aob = sb("aob", [64, 512], BF16)
        onec = sb("onec", [128, 1], F32)
        oner = sb("oner", [1, 128], F32)
        psST = ps("psST", [128, 512])
        psRS = ps("psRS", [1, 512])
        psRR = ps("psRR", [128, 512])
        psO = ps("psO", [64, 512])

        obf = sb("obf", [128, 1], BF16)
        p = Prog(nc)
        p.dma(onec, ones128)
        p.dma(oner, onesrow)
        p.dma(tri, TRI)
        p.s('copy', obf, onec)
        for u in range(4):
            p.dma(qfull, qT[u * 64:(u + 1) * 64, :])
            p.dma(kfull, kT[u * 64:(u + 1) * 64, :])
            for t in range(64):
                p.dma(Vt[:, t * 64:(t + 1) * 64],
                      vR[t * 128:(t + 1) * 128, u * 64:(u + 1) * 64])
            for Q in range(16):
                nk = 4 * (Q + 1)
                qs = qfull[:, Q * 512:(Q + 1) * 512]
                p.mm(psST, kfull[:, 0:128], qs)
                for kt in range(nk):
                    if kt >= 4 * Q:
                        j = kt - 4 * Q
                        p.v('tensor_tensor', psST, psST, tri[:, j * 512:(j + 1) * 512], OP.add)
                    p.s('activation', ptbf, psST, AF.Exp, scale=0.125)
                    p.mm(psRS[0:1, :], obf, ptbf, start=(kt == 0), stop=(kt == nk - 1))
                    p.mm(psO, Vt[:, kt * 64:(kt + 1) * 64], ptbf, start=(kt == 0),
                         stop=(kt == nk - 1), wait=False)
                    if kt + 1 < nk:
                        p.mm(psST, kfull[:, (kt + 1) * 128:(kt + 2) * 128], qs, wait=False)
                p.s('copy', rs_[0:1, :], psRS[0:1, :])
                p.v('reciprocal', rrs[0:1, :], rs_[0:1, :])
                p.mm(psRR, oner, rrs[0:1, :])
                p.s('copy', of_, psO[0:64, :])
                p.v('tensor_tensor', of_, of_, psRR[0:64, :], OP.mult)
                p.s('copy', aob, of_)
                p.dma(aoT_o[u * 64:(u + 1) * 64, Q * 512:(Q + 1) * 512], aob)
        p.finish()
    return nc


def build_KG(b_core):
    nc = bass.Bass("TRN2", target_bir_lowering=False)
    aoT = nc.dram_tensor("aoT", [D, TCORE], BF16, kind="ExternalInput").ap()
    WoA = nc.dram_tensor("WoA", [D, D], F32, kind="ExternalInput").ap()
    poolP = nc.dram_tensor("poolP", [8, D], F32, kind="ExternalInput").ap()
    gwT = nc.dram_tensor("gwT", [D, 8], F32, kind="ExternalInput").ap()
    gb = nc.dram_tensor("gb", [8, 1], F32, kind="ExternalInput").ap()
    expT = nc.dram_tensor("expT", [8 * D, D], F32, kind="ExternalInput").ap()
    ebT = nc.dram_tensor("ebT", [D, 8], F32, kind="ExternalInput").ap()
    xn1T = nc.dram_tensor("xn1T", [D, TCORE], BF16, kind="ExternalInput").ap()
    x1T = nc.dram_tensor("x1T", [D, TCORE], F32, kind="ExternalInput").ap()
    I128 = nc.dram_tensor("I128", [128, 128], F32, kind="ExternalInput").ap()
    onesrow = nc.dram_tensor("onesrow", [1, 128], F32, kind="ExternalInput").ap()
    outT_o = nc.dram_tensor("outT", [D, TCORE], F32, kind="ExternalOutput").ap()

    st = ExitStack()
    with st:
        sb = lambda n, s, d: st.enter_context(nc.sbuf_tensor(n, s, d)).ap()
        ps = lambda n, s: st.enter_context(nc.psum_tensor(n, s, F32)).ap()
        Wo_bf = sb("Wo_bf", [128, 8 * 1024], BF16)
        ew_bf = sb("ew_bf", [128, 8 * 1024], BF16)
        wtmp = sb("wtmp", [128, 1024], F32)
        gwc = sb("gwc", [128, 8 * 8], F32)
        ebc = sb("ebc", [128, 8 * 8], F32)
        gbc = sb("gbc", [8, 1], F32)
        pr = sb("pr", [8, 1024], F32)
        ptj = sb("ptj", [128, 8], F32)
        pool2 = sb("pool2", [128, 16], F32)
        lgf = sb("lgf", [8, 2], F32)
        lg2 = sb("lg2", [2, 8], F32)
        m1 = sb("m1", [2, 1], F32)
        mneg = sb("mneg", [2, 1], F32)
        eqm = sb("eqm", [2, 8], F32)
        cnt = sb("cnt", [2, 1], F32)
        big = sb("big", [2, 8], F32)
        wo_ = sb("wo_", [2, 8], F32)
        m2 = sb("m2", [2, 1], F32)
        c2 = sb("c2", [2, 1], F32)
        min2 = sb("min2", [2, 1], F32)
        keep = sb("keep", [2, 8], F32)
        ex = sb("ex", [2, 8], F32)
        den = sb("den", [2, 1], F32)
        rden = sb("rden", [2, 1], F32)
        gat = sb("gat", [2, 8], F32)
        gT8 = sb("gT8", [8, 2], F32)
        grow = sb("grow", [1, 16], F32)
        gcols = sb("gcols", [128, 8], F32)
        xnb = sb("xnb", [128, 8 * 512], BF16)
        aob = sb("aob", [128, 8 * 512], BF16)
        s1f = sb("s1f", [128, 8 * 512], F32)
        accf = sb("accf", [128, 8 * 512], F32)
        eo = sb("eo", [128, 512], F32)
        eo2 = sb("eo2", [128, 512], F32)
        g1 = sb("g1", [128, 512], F32)
        x1f = sb("x1f", [128, 512], F32)
        I_f = sb("I_f", [128, 128], F32)
        oner = sb("oner", [1, 128], F32)
        psX = ps("psX", [128, 512])
        psX2 = ps("psX2", [128, 512])
        psPT = ps("psPT", [128, 8])
        psL = ps("psL", [8, 2])
        psLT = ps("psLT", [2, 8])
        psGT = ps("psGT", [8, 2])
        psGC = ps("psGC", [128, 1])

        p = Prog(nc)
        p.dma(oner, onesrow)
        p.dma(I_f, I128)
        p.dma(gbc, gb)
        for k in range(8):
            p.dma(wtmp, WoA[k * 128:(k + 1) * 128, :])
            p.s('copy', Wo_bf[:, k * 1024:(k + 1) * 1024], wtmp)
            p.dma(gwc[:, k * 8:(k + 1) * 8], gwT[k * 128:(k + 1) * 128, :])
            p.dma(ebc[:, k * 8:(k + 1) * 8], ebT[k * 128:(k + 1) * 128, :])
        p.dma(pr, poolP)
        for j in range(8):
            p.tr(psPT, pr[0:8, j * 128:(j + 1) * 128], I_f[0:8, 0:8])
            p.s('copy', ptj, psPT)
            p.v('reduce_sum', pool2[:, j * 2:j * 2 + 1], ptj[:, 0:4], axis=AX.X)
            p.v('reduce_sum', pool2[:, j * 2 + 1:j * 2 + 2], ptj[:, 4:8], axis=AX.X)
        p.v('tensor_scalar', pool2, pool2, 1.0 / 8192.0, None, OP.mult)
        for j in range(8):
            p.mm(psL[0:8, 0:2], gwc[:, j * 8:(j + 1) * 8], pool2[:, j * 2:(j + 1) * 2],
                 start=(j == 0), stop=(j == 7), wait=(j == 0))
        p.v('tensor_scalar', lgf, psL[0:8, 0:2], gbc[0:8, 0:1], None, OP.add)
        p.tr(psLT, lgf[0:8, 0:2], I_f[0:8, 0:8])
        p.s('copy', lg2, psLT[0:2, 0:8])
        p.v('reduce_max', m1, lg2, axis=AX.X)
        p.v('tensor_scalar', mneg, m1, -1.0, None, OP.mult)
        p.v('tensor_scalar', eqm, lg2, m1, None, OP.is_ge)
        p.v('reduce_sum', cnt, eqm, axis=AX.X)
        p.v('tensor_scalar', big, eqm, 1e30, None, OP.mult)
        p.v('tensor_sub', wo_, lg2, big)
        p.v('reduce_max', m2, wo_, axis=AX.X)
        p.v('tensor_scalar', c2, cnt, 1.5, None, OP.is_ge)
        p.v('tensor_sub', min2, m1, m2)
        p.v('tensor_mul', min2, min2, c2)
        p.v('tensor_add', min2, min2, m2)
        p.v('tensor_scalar', keep, lg2, min2, None, OP.is_ge)
        p.s('activation', ex, lg2, AF.Exp, bias=mneg)
        p.v('tensor_mul', ex, ex, keep)
        p.v('reduce_sum', den, ex, axis=AX.X)
        p.v('reciprocal', rden, den)
        p.v('tensor_scalar', gat, ex, rden, None, OP.mult)
        p.tr(psGT, gat[0:2, 0:8], I_f[0:2, 0:2])
        p.s('copy', gT8, psGT[0:8, 0:2])
        for e in range(8):
            p.dma(grow[0:1, e * 2:(e + 1) * 2], gT8[e:e + 1, 0:2])
        for e in range(8):
            p.mm(psGC, oner, grow[0:1, e * 2 + b_core:e * 2 + b_core + 1])
            p.s('copy', gcols[:, e:e + 1], psGC)
        for ci in range(NCH):
            c0 = ci * 512
            for k in range(8):
                p.dma(aob[:, k * 512:(k + 1) * 512], aoT[k * 128:(k + 1) * 128, c0:c0 + 512])
                p.dma(xnb[:, k * 512:(k + 1) * 512], xn1T[k * 128:(k + 1) * 128, c0:c0 + 512])
            for m0 in range(0, 8, 2):
                for (pi, ph) in ((0, psX), (1, psX2)):
                    m = m0 + pi
                    for k in range(8):
                        p.mm(ph, Wo_bf[:, k * 1024 + m * 128:k * 1024 + (m + 1) * 128],
                             aob[:, k * 512:(k + 1) * 512], start=(k == 0), stop=(k == 7),
                             wait=(k == 0 and pi == 0))
                p.s('copy', s1f[:, m0 * 512:(m0 + 1) * 512], psX)
                p.s('copy', s1f[:, (m0 + 1) * 512:(m0 + 2) * 512], psX2, wait=False)
            for e in range(8):
                for k in range(8):
                    p.dma(wtmp, expT[e * 1024 + k * 128:e * 1024 + (k + 1) * 128, :])
                    p.s('copy', ew_bf[:, k * 1024:(k + 1) * 1024], wtmp)
                for m0 in range(0, 8, 2):
                    for (pi, ph) in ((0, psX), (1, psX2)):
                        m = m0 + pi
                        for k in range(8):
                            p.mm(ph, ew_bf[:, k * 1024 + m * 128:k * 1024 + (m + 1) * 128],
                                 xnb[:, k * 512:(k + 1) * 512], start=(k == 0),
                                 stop=(k == 7), wait=(k == 0 and pi == 0))
                    p.s('activation', eo, psX, AF.Gelu,
                        bias=ebc[:, m0 * 8 + e:m0 * 8 + e + 1])
                    p.s('activation', eo2, psX2, AF.Gelu,
                        bias=ebc[:, (m0 + 1) * 8 + e:(m0 + 1) * 8 + e + 1], wait=False)
                    for (m, eox, w8) in ((m0, eo, True), (m0 + 1, eo2, False)):
                        if e == 0:
                            p.v('tensor_scalar', accf[:, m * 512:(m + 1) * 512], eox,
                                gcols[:, 0:1], None, OP.mult, wait=w8)
                        else:
                            p.v('scalar_tensor_tensor', accf[:, m * 512:(m + 1) * 512], eox,
                                gcols[:, e:e + 1], accf[:, m * 512:(m + 1) * 512],
                                OP.mult, OP.add, wait=w8)
            for m in range(8):
                p.s('activation', g1, accf[:, m * 512:(m + 1) * 512], AF.Silu)
                p.v('tensor_mul', g1, g1, s1f[:, m * 512:(m + 1) * 512])
                p.dma(x1f, x1T[m * 128:(m + 1) * 128, c0:c0 + 512])
                p.v('tensor_add', g1, g1, x1f)
                p.dma(outT_o[m * 128:(m + 1) * 128, c0:c0 + 512], g1)
        p.finish()
    return nc


def _f(a):
    return np.ascontiguousarray(a, dtype=np.float32)


def kernel(**inp):
    x = _f(inp['L0_attn_x'])
    xT = np.ascontiguousarray(x.reshape(NT, D).T)
    ones128 = np.ones((128, 1), f32)
    onesrow = np.ones((1, 128), f32)
    I128 = np.eye(128, dtype=f32)
    I128b = np.eye(128, dtype=BF)

    W_inT = np.ascontiguousarray(_f(inp['L0_attn_W_in']).T)
    W_xT = np.ascontiguousarray(_f(inp['L0_attn_W_x']).T)
    W_dtT = np.ascontiguousarray(_f(inp['L0_attn_W_dt']).T)
    convw = np.ascontiguousarray(_f(inp['L0_attn_conv_w'])[:, 0, :])
    convb = _f(inp['L0_attn_conv_b']).reshape(DI, 1)
    bdt = _f(inp['L0_attn_b_dt']).reshape(DI, 1)
    nc = build_KA()
    maps = []
    for c in range(NCORE):
        xin = np.zeros((D, TCORE + 3), f32)
        xin[:, 3:] = xT[:, c * TCORE:(c + 1) * TCORE]
        if c not in (0, 4):
            xin[:, :3] = xT[:, c * TCORE - 3:c * TCORE]
        maps.append(dict(xin=xin, WinT=W_inT, WxT=W_xT, WdtT=W_dtT, convw=convw,
                         convb=convb, bdt=bdt, ones128=ones128, onesrow=onesrow))
    rA = _run(nc, maps)
    xnT = np.concatenate([r['xnT'] for r in rA], axis=1)
    xcT = np.concatenate([r['xcT'] for r in rA], axis=1)
    dtT = np.concatenate([r['dtT'] for r in rA], axis=1)
    resT = np.concatenate([r['resT'] for r in rA], axis=1)
    BTf = np.concatenate([r['BT'] for r in rA], axis=1)
    CTf = np.concatenate([r['CT'] for r in rA], axis=1)

    At = np.ascontiguousarray(np.tile(_f(inp['L0_attn_A_log']).reshape(1, 16), (128, 1)))
    Dvf = _f(inp['L0_attn_D']).reshape(DI, 1)
    nc = build_KB()
    maps = []
    for c in range(NCORE):
        r0 = c * 256
        maps.append(dict(dtT=np.ascontiguousarray(dtT[r0:r0 + 256]),
                         xcT=np.ascontiguousarray(xcT[r0:r0 + 256]),
                         resT=np.ascontiguousarray(resT[r0:r0 + 256]),
                         BT=BTf, CT=CTf, At=At,
                         Dv=np.ascontiguousarray(Dvf[r0:r0 + 256]),
                         onesrow=onesrow))
    rB = _run(nc, maps)
    y2T = np.concatenate([r['y2T'] for r in rB], axis=0)

    W_outT = np.ascontiguousarray(_f(inp['L0_attn_W_out']).T)
    gam = _f(inp['L0_ffn_norm_weight']).reshape(D, 1)
    routT = np.ascontiguousarray(_f(inp['L0_ffn_router_w']).T)
    nc = build_KC()
    maps = []
    for c in range(NCORE):
        cs = slice(c * TCORE, (c + 1) * TCORE)
        maps.append(dict(y2T=np.ascontiguousarray(y2T[:, cs]), WoutT=W_outT,
                         xnT=np.ascontiguousarray(xnT[:, cs]), gam=gam, routT=routT,
                         I128=I128, ones128=ones128, onesrow=onesrow))
    rC = _run(nc, maps)
    sT = [r['sT'] for r in rC]
    normT = [r['normT'] for r in rC]
    combT = np.concatenate([r['combT'] for r in rC], axis=1)

    nc = build_KD1()
    maps = [dict(normT=normT[c], sw1=_f(inp['L0_ffn_sw1']), sw2=_f(inp['L0_ffn_sw2']),
                 sb1=_f(inp['L0_ffn_sb1']).reshape(4096, 1),
                 sb2=_f(inp['L0_ffn_sb2']).reshape(D, 1)) for c in range(NCORE)]
    rD1 = _run(nc, maps)
    sharedT = [r['sharedT'] for r in rD1]

    ew1 = _f(inp['L0_ffn_ew1'])
    ew2 = _f(inp['L0_ffn_ew2'])
    eb1 = _f(inp['L0_ffn_eb1'])
    eb2 = _f(inp['L0_ffn_eb2'])
    normhalf = [np.concatenate([normT[c] for c in range(4)], axis=1),
                np.concatenate([normT[c] for c in range(4, 8)], axis=1)]
    routhalf = []
    nc = build_KD2()
    for half in range(2):
        cs = slice(half * 8192, (half + 1) * 8192)
        maps = []
        for e in range(NCORE):
            maps.append(dict(normT=normhalf[half], ew1=np.ascontiguousarray(ew1[e]),
                             ew2=np.ascontiguousarray(ew2[e]),
                             eb1=np.ascontiguousarray(eb1[e]).reshape(4096, 1),
                             eb2=np.ascontiguousarray(eb2[e]).reshape(D, 1),
                             combr=np.ascontiguousarray(combT[e:e + 1, cs]),
                             onesrow=onesrow))
        rD2 = _run(nc, maps)
        routhalf.append([r['routT'] for r in rD2])

    nc = build_KDsum()
    maps = []
    for c in range(NCORE):
        half = c // 4
        lo = (c % 4) * TCORE
        m = {f'r{e}': np.ascontiguousarray(routhalf[half][e][:, lo:lo + TCORE])
             for e in range(8)}
        m['sharedT'] = sharedT[c]
        m['sT'] = sT[c]
        m['x0T'] = np.ascontiguousarray(xT[:, c * TCORE:(c + 1) * TCORE])
        maps.append(m)
    rDs = _run(nc, maps)
    x1T = [r['x1T'] for r in rDs]

    DH = 64
    inv = 1.0 / (10000.0 ** (np.arange(0, DH, 2, dtype=np.float64) / DH))
    fr = np.arange(8192, dtype=np.float64)[:, None] * inv[None, :]
    cosT = np.cos(fr).T.astype(f32)
    sinT = np.sin(fr).T.astype(f32)
    cosB = np.ascontiguousarray(np.tile(cosT, (4, 1)))
    sinB = np.ascontiguousarray(np.concatenate([-sinT, sinT, -sinT, sinT], axis=0))
    nc = build_KE()
    maps = []
    for c in range(NCORE):
        t0 = (c * TCORE) % 8192
        maps.append(dict(x1T=x1T[c], Wqkv=_f(inp['L1_attn_Wqkv']),
                         cosB=np.ascontiguousarray(cosB[:, t0:t0 + TCORE]),
                         sinB=np.ascontiguousarray(sinB[:, t0:t0 + TCORE]),
                         I128b=I128b, ones128=ones128, onesrow=onesrow))
    rE = _run(nc, maps)
    qTf = np.concatenate([r['qT'] for r in rE], axis=1)
    kTf = np.concatenate([r['kT'] for r in rE], axis=1)
    vRf = np.concatenate([r['vR'] for r in rE], axis=0)
    xn1T = [r['xn1T'] for r in rE]
    poolP = np.stack([np.ascontiguousarray(r['pool'].T).reshape(D) for r in rE], axis=0)

    TRI = np.zeros((128, 4 * 512), f32)
    for j in range(4):
        blk = np.where(np.arange(128)[:, None] > (j * 128 + np.arange(512))[None, :] - j * 0,
                       0.0, 0.0)
        r = np.arange(128)[:, None]
        qq = np.arange(512)[None, :]
        mask = (r + j * 128) > qq
        TRI[:, j * 512:(j + 1) * 512] = np.where(mask, -1e30, 0.0)
    nc = build_KF()
    maps = []
    for c in range(NCORE):
        qs, ks, vs = [], [], []
        for u in range(4):
            pair = c * 4 + u
            b, h = pair // 16, pair % 16
            qs.append(qTf[h * 64:(h + 1) * 64, b * 8192:(b + 1) * 8192])
            ks.append(kTf[h * 64:(h + 1) * 64, b * 8192:(b + 1) * 8192])
            vs.append(vRf[b * 8192:(b + 1) * 8192, h * 64:(h + 1) * 64])
        maps.append(dict(qT=np.ascontiguousarray(np.concatenate(qs, axis=0)),
                         kT=np.ascontiguousarray(np.concatenate(ks, axis=0)),
                         vR=np.ascontiguousarray(np.concatenate(vs, axis=1)),
                         TRI=TRI, ones128=ones128, onesrow=onesrow))
    rF = _run(nc, maps)
    aoTf = np.zeros((D, NT), BF)
    for c in range(NCORE):
        ao = rF[c]['aoT']
        for u in range(4):
            pair = c * 4 + u
            b, h = pair // 16, pair % 16
            aoTf[h * 64:(h + 1) * 64, b * 8192:(b + 1) * 8192] = ao[u * 64:(u + 1) * 64]

    gwT = np.ascontiguousarray(_f(inp['L1_ffn_gate_w']).T)
    gb = _f(inp['L1_ffn_gate_b']).reshape(8, 1)
    expT = np.ascontiguousarray(_f(inp['L1_ffn_expert_w']).transpose(0, 2, 1)).reshape(8 * D, D)
    ebT = np.ascontiguousarray(_f(inp['L1_ffn_expert_b']).T)
    nc = build_KG(0)
    maps = []
    for c in range(NCORE):
        cs = slice(c * TCORE, (c + 1) * TCORE)
        b = c // 4
        pp = poolP if b == 0 else np.ascontiguousarray(poolP[[4, 5, 6, 7, 0, 1, 2, 3]])
        maps.append(dict(aoT=np.ascontiguousarray(aoTf[:, cs]), WoA=_f(inp['L1_attn_Wout']),
                         poolP=pp, gwT=gwT, gb=gb, expT=expT, ebT=ebT,
                         xn1T=xn1T[c], x1T=x1T[c], I128=I128, onesrow=onesrow))
    rG = _run(nc, maps)
    outT = np.concatenate([r['outT'] for r in rG], axis=1)
    out = np.ascontiguousarray(outT.T).reshape(2, 8192, D).astype(np.float32)
    return out
